# version 08
# speedup vs baseline: 1.0841x; 1.0188x over previous
import os
import numpy as np
import ml_dtypes
import concourse.bass as bass
import concourse.mybir as mybir
from concourse.bass import IndirectOffsetOnAxis
from concourse.bass_utils import run_bass_kernel_spmd

F32 = mybir.dt.float32
BF16 = mybir.dt.bfloat16
U32 = mybir.dt.uint32
AF = mybir.ActivationFunctionType
ALU = mybir.AluOpType
AX = mybir.AxisListType

NCORES = 8
TOK = 2048
NCH = 8
D = 1024
NT = TOK // 128
NB = TOK // 512
HEADS = 16
DH = 64
SEQ = 4096

N_LAYERS = int(os.environ.get("MK_LAYERS", "4"))
PHASE_STATS = {} if os.environ.get("MK_PHASES") else None
MK_TAPS = set(x for x in os.environ.get("MK_TAPS", "").split(",") if x)



class Ctx:
    def __init__(self, nc, es):
        self.nc = nc
        self.es = es
        self.sems = {}
        self.cnt = {}
        self.phase = "init"
        self.consts = {}
        self.bank_tok = {}
        self.output_written = False

    def sem(self, name):
        if name not in self.sems:
            self.sems[name] = self.es.enter_context(self.nc.semaphore(name))
            self.cnt[name] = 0
        return self.sems[name]

    def inc(self, instr, name, n=1):
        h = self.sem(name)
        self.cnt[name] += n
        instr.then_inc(h, n)
        if PHASE_STATS is not None:
            PHASE_STATS[(self.phase, name)] = PHASE_STATS.get((self.phase, name), 0) + 1
        return self.cnt[name]

    def wait(self, eng, name, val=None):
        h = self.sem(name)
        if val is None:
            val = self.cnt[name]
        assert val <= self.cnt[name], f"wait {name}>={val} emitted {self.cnt[name]}"
        eng.wait_ge(h, val)

    def wait_future(self, eng, name, val):
        eng.wait_ge(self.sem(name), val)

    def dma(self, eng, out, in_, sem="dio", **kw):
        return self.inc(eng.dma_start(out=out, in_=in_, **kw), sem, 16)

    def uname(self, base):
        n = getattr(self, "_uctr", 0)
        self._uctr = n + 1
        return f"{base}_{n}"

    def const(self, name, arr):
        if name not in self.consts:
            self.consts[name] = self.nc.inline_tensor(np.ascontiguousarray(arr), name=name)
        return self.consts[name]

    def bank_acquire(self, t_eng, ps):
        tok = self.bank_tok.get(ps.name)
        if tok is not None:
            self.wait_future(t_eng, tok[0], tok[1])

    def bank_release(self, ps, sem_name, count):
        self.bank_tok[ps.name] = (sem_name, count)

    def bank_clear(self):
        self.bank_tok.clear()

    def tap(self, name, ap, shape, dtype=None):
        if name not in MK_TAPS:
            return
        if dtype is None:
            dtype = F32
        t = self.nc.dram_tensor("dbg_" + name, list(shape), dtype, kind="ExternalOutput")
        self.taps = getattr(self, "taps", {})
        self.taps[name] = t
        with self.nc.Block() as blk:
            @blk.sync
            def _(sy):
                m = self.dma(sy, t.ap(), ap, sem="dbg")
                sy.wait_ge(self.sems["dbg"], m)


def run_pipeline(ctx, blk, jobs, banks, pre_pe=None, pre_evac=None):
    n = len(jobs)
    if n == 0:
        return
    nb_ = len(banks)
    pe_base = ctx.cnt["pe"] if "pe" in ctx.cnt else 0
    ctx.sem("pe"); ctx.sem("evac")
    pe_base = ctx.cnt["pe"]
    ev_base = ctx.cnt["evac"]

    @blk.tensor
    def _(t):
        if pre_pe:
            pre_pe(t)
        for i, job in enumerate(jobs):
            ps = banks[i % nb_]
            if i >= nb_:
                ctx.wait_future(t, "evac", ev_base + i - nb_ + 1)
            else:
                ctx.bank_acquire(t, ps)
            last = job["mm"](t, ps)
            ctx.inc(last, "pe")
    assert ctx.cnt["pe"] == pe_base + n, "each job's mm must inc pe exactly once"

    engines = {}
    for i, job in enumerate(jobs):
        engines.setdefault(job.get("evac_eng", "act"), []).append(i)
    done = [False] * n

    def emit_for(eng_name):
        ids = engines[eng_name]
        section = {"act": blk.scalar, "dve": blk.vector, "pool": blk.gpsimd}[eng_name]
        @section
        def _(eng):
            if pre_evac:
                pre_evac(eng)
            for i in ids:
                job = jobs[i]
                ps = banks[i % nb_]
                ctx.wait_future(eng, "pe", pe_base + i + 1)
                last = job["evac"](eng, ps)
                ctx.cnt["evac"] += 1
                last.then_inc(ctx.sems["evac"], 1)

    for en in engines:
        emit_for(en)
    for i, job in enumerate(jobs):
        ctx.bank_release(banks[i % nb_], "evac", ev_base + i + 1)
    assert len(engines) == 1, "v1: one evac engine per pipeline"


def mm_group(ctx, psum_ap_fn, pairs, bias=None, banked_start=True):
    def mm(t, ps):
        pa = psum_ap_fn(ps)
        last = None
        for j, (lh, rh) in enumerate(pairs):
            last = t.matmul(pa, lh, rh, start=(j == 0), stop=(bias is None and j == len(pairs) - 1))
        if bias is not None:
            br, orow = bias
            last = t.matmul(pa, br, orow, start=False, stop=True)
        return last
    return mm



def row_chain(ctx, eng, ops):
    for f in ops:
        ins = f(eng)
        m = ctx.inc(ins, "dve")
        ctx.wait(eng, "dve", m)


def emit_ln_rows(ctx, blk, src_fn, nch, ntok, eps, inv_n, sq_bufs, seg, act_extra=None):
    nc = ctx.nc
    rw = ctx.rw
    nbl = ntok // 512
    assert nbl <= 4
    sum_banks = [ctx.PS[0], ctx.PS[1], ctx.PS[2], ctx.PS[3]][:nbl]
    sq_banks = [ctx.PS[4], ctx.PRA, ctx.PRB, ctx.PTR][:nbl]
    ctx.sem("act"); ctx.sem("pe")
    act_base = ctx.cnt["act"]
    pe_base = ctx.cnt["pe"]

    @blk.scalar
    def _(a):
        for c in range(nch):
            if c >= 2:
                ctx.wait_future(a, "pe", pe_base + c - 1)
            ctx.inc(a.activation(sq_bufs[c % 2][:, 0:ntok], src_fn(c), AF.Square), "act")

    @blk.tensor
    def _(t):
        for b in range(nbl):
            ctx.bank_acquire(t, sum_banks[b])
            ctx.bank_acquire(t, sq_banks[b])
        for c in range(nch):
            ctx.wait_future(t, "act", act_base + c + 1)
            for b in range(nbl):
                t.matmul(sum_banks[b][0:1, :], ctx.csb["ones_col"][:, :],
                         src_fn(c)[:, b * 512:(b + 1) * 512], start=(c == 0), stop=(c == nch - 1))
                last = t.matmul(sq_banks[b][0:1, :], ctx.csb["ones_col"][:, :],
                                sq_bufs[c % 2][:, b * 512:(b + 1) * 512], start=(c == 0), stop=(c == nch - 1))
            ctx.inc(last, "pe")
    pe_m = ctx.cnt["pe"]

    @blk.vector
    def _(v):
        ctx.wait(v, "pe", pe_m)
        for b in range(nbl):
            ctx.inc(v.tensor_copy(rw[0:1, seg + b * 512:seg + (b + 1) * 512], sum_banks[b][0:1, :]), "dve")
            ctx.inc(v.tensor_copy(rw[0:1, seg + ntok + b * 512:seg + ntok + (b + 1) * 512],
                                  sq_banks[b][0:1, :]), "dve")
        m = ctx.cnt["dve"]
        ctx.wait(v, "dve", m)
        S0 = rw[0:1, seg:seg + ntok]
        S1 = rw[0:1, seg + ntok:seg + 2 * ntok]
        S2 = rw[0:1, seg + 2 * ntok:seg + 3 * ntok]
        row_chain(ctx, v, [
            lambda e: e.tensor_scalar_mul(S0, S0, -inv_n),
            lambda e: e.tensor_scalar(S1, S1, inv_n, eps, ALU.mult, ALU.add),
            lambda e: e.tensor_mul(S2, S0, S0),
            lambda e: e.tensor_sub(S1, S1, S2),
        ])

    @blk.scalar
    def _(a):
        ctx.wait(a, "dve")
        S1 = rw[0:1, seg + ntok:seg + 2 * ntok]
        ctx.inc(a.activation(S1, S1, AF.Sqrt), "act")

    @blk.vector
    def _(v):
        ctx.wait(v, "act")
        S0 = rw[0:1, seg:seg + ntok]
        S1 = rw[0:1, seg + ntok:seg + 2 * ntok]
        S2 = rw[0:1, seg + 2 * ntok:seg + 3 * ntok]
        row_chain(ctx, v, [
            lambda e: e.reciprocal(S1, S1),
            lambda e: e.tensor_mul(S2, S0, S1),
        ])
    for b in range(nbl):
        ctx.bank_release(sum_banks[b], "dve", ctx.cnt["dve"])
        ctx.bank_release(sq_banks[b], "dve", ctx.cnt["dve"])
    return seg + ntok, seg + 2 * ntok


def emit_bcast(ctx, blk, row_off, ntok, out_sb, tag=""):
    nbl = ntok // 512
    jobs = []
    for b in range(nbl):
        def mk(b):
            def mm(t, ps):
                return t.matmul(ps[:, :], ctx.csb["ones_row128"][:, :],
                                ctx.rw[0:1, row_off + b * 512:row_off + (b + 1) * 512],
                                start=True, stop=True)
            def ev(a, ps):
                return a.copy(out_sb[:, b * 512:(b + 1) * 512], ps[:, :])
            return {"mm": mm, "evac": ev, "evac_eng": "act"}
        jobs.append(mk(b))
    run_pipeline(ctx, blk, jobs, banks=[ctx.PS[0], ctx.PS[1]])



def _rows(v):
    v = np.asarray(v, dtype=np.float32).reshape(-1)
    return np.ascontiguousarray(v.reshape(1, v.size))


def _cols(v):
    v = np.asarray(v, dtype=np.float32).reshape(-1)
    return np.ascontiguousarray(v.reshape(v.size // 128, 128).T)


def prepare_inputs(inputs):
    x = np.asarray(inputs["L0_attn_x"], dtype=np.float32)
    xf = x.reshape(NCORES, TOK, D)
    com = {}

    def add(name, arr, dt=np.float32):
        com[name] = np.ascontiguousarray(np.asarray(arr), dtype=dt)

    add("l0_qkvT", np.asarray(inputs["L0_attn_qkv_w"]).T)
    add("l0_projT", np.asarray(inputs["L0_attn_proj_w"]).T)
    add("l0_projb_r", _rows(inputs["L0_attn_proj_b"]))
    add("l0_qnw", np.tile(np.asarray(inputs["L0_attn_q_norm_w"]), 2).reshape(128, 1))
    add("l0_knw", np.tile(np.asarray(inputs["L0_attn_k_norm_w"]), 2).reshape(128, 1))
    add("l0_wfcT", np.asarray(inputs["L0_ffn_W_fc"]).T)
    add("l0_lnw_c", _cols(inputs["L0_ffn_ln_w"]))
    add("l0_lnb_c", _cols(inputs["L0_ffn_ln_b"]))
    add("l0_wprT", np.asarray(inputs["L0_ffn_W_proj"]).T)

    add("l1_WqT", np.asarray(inputs["L1_attn_Wq"]).T)
    add("l1_bq_r", _rows(inputs["L1_attn_bq"]))
    add("l1_WkT", np.asarray(inputs["L1_attn_Wk"]).T)
    add("l1_bk_r", _rows(inputs["L1_attn_bk"]))
    add("l1_WvT", np.asarray(inputs["L1_attn_Wv"]).T)
    add("l1_bv_r", _rows(inputs["L1_attn_bv"]))
    add("l1_gamma_c", _cols(inputs["L1_attn_gamma"]))
    add("l1_beta_c", _cols(inputs["L1_attn_beta"]))
    add("l1_sw1T", np.asarray(inputs["L1_ffn_sw1"]).T)
    add("l1_sb1_r", _rows(inputs["L1_ffn_sb1"]))
    add("l1_sw2T", np.asarray(inputs["L1_ffn_sw2"]).T)
    add("l1_sb2_r", _rows(inputs["L1_ffn_sb2"]))
    add("l1_sw3T", np.asarray(inputs["L1_ffn_sw3"]).T)
    add("l1_sb3_r", _rows(inputs["L1_ffn_sb3"]))
    add("l1_slng_c", _cols(inputs["L1_ffn_s_ln_g"]))
    add("l1_slnb_c", _cols(inputs["L1_ffn_s_ln_b"]))
    gw = np.concatenate([np.asarray(inputs["L1_ffn_sg_w"]).reshape(1, D),
                         np.asarray(inputs["L1_ffn_eg_w"])], axis=0)
    add("l1_gwT", gw.T)
    gb = np.concatenate([np.asarray(inputs["L1_ffn_sg_b"]).reshape(1),
                         np.asarray(inputs["L1_ffn_eg_b"])], axis=0)
    add("l1_gb_r", np.pad(gb, (0, 7)).reshape(1, 16))
    add("l1_expb_r", np.pad(np.asarray(inputs["L1_ffn_exp_bias"]), (0, 8)).reshape(1, 16))
    l1ew1T = np.asarray(inputs["L1_ffn_ew1"]).transpose(0, 2, 1)
    l1eb1 = np.asarray(inputs["L1_ffn_eb1"]).reshape(8, 1, 512)
    l1ew2T = np.asarray(inputs["L1_ffn_ew2"]).transpose(0, 2, 1)
    l1eb2 = np.asarray(inputs["L1_ffn_eb2"]).reshape(8, 1, 512)
    l1ew3T = np.asarray(inputs["L1_ffn_ew3"]).transpose(0, 2, 1)
    l1eb3 = np.asarray(inputs["L1_ffn_eb3"]).reshape(8, 1, 1024)
    add("l1_selg_c", _cols(inputs["L1_ffn_sel_ln_g"]))
    add("l1_selb_c", _cols(inputs["L1_ffn_sel_ln_b"]))

    add("l2_Wq", np.asarray(inputs["L2_attn_Wq"]).reshape(6 * D, D))
    add("l2_bq_r", np.asarray(inputs["L2_attn_bq"]).reshape(6, D))
    add("l2_Wk", np.asarray(inputs["L2_attn_Wk"]).reshape(6 * D, D))
    add("l2_bk_r", np.asarray(inputs["L2_attn_bk"]).reshape(6, D))
    add("l2_Wv", np.asarray(inputs["L2_attn_Wv"]).reshape(6 * D, D))
    add("l2_bv_r", np.asarray(inputs["L2_attn_bv"]).reshape(6, D))
    add("l2_Wo", np.asarray(inputs["L2_attn_Wo"]).reshape(6 * D, D))
    add("l2_bo_r", np.asarray(inputs["L2_attn_bo"]).reshape(6, D))
    add("l2_ln1w_c", np.concatenate([_cols(w) for w in np.asarray(inputs["L2_attn_ln1_w"])], axis=1))
    add("l2_ln1b_c", np.concatenate([_cols(w) for w in np.asarray(inputs["L2_attn_ln1_b"])], axis=1))
    add("l2_W1", np.asarray(inputs["L2_attn_W1"]).reshape(6 * D, 4096))
    add("l2_b1_c", np.concatenate([_cols(w) for w in np.asarray(inputs["L2_attn_b1"])], axis=1))
    add("l2_W2", np.asarray(inputs["L2_attn_W2"]).reshape(6 * 4096, D))
    add("l2_b2_r", np.asarray(inputs["L2_attn_b2"]).reshape(6, D))
    add("l2_ln2w_c", np.concatenate([_cols(w) for w in np.asarray(inputs["L2_attn_ln2_w"])], axis=1))
    add("l2_ln2b_c", np.concatenate([_cols(w) for w in np.asarray(inputs["L2_attn_ln2_b"])], axis=1))
    add("l2_lnfw_c", _cols(inputs["L2_attn_lnf_w"]))
    add("l2_lnfb_c", _cols(inputs["L2_attn_lnf_b"]))
    add("l2_gw", np.asarray(inputs["L2_ffn_gate_w"]))
    add("l2_w1", np.asarray(inputs["L2_ffn_w1"]).reshape(8 * D, 2048))
    add("l2_fb1_c", np.concatenate([_cols(w) for w in np.asarray(inputs["L2_ffn_b1"])], axis=1))
    add("l2_w2", np.asarray(inputs["L2_ffn_w2"]).reshape(8 * 2048, D))
    add("l2_fb2_r", np.asarray(inputs["L2_ffn_b2"]).reshape(8, D))
    add("l2_lng_c", np.concatenate([_cols(w) for w in np.asarray(inputs["L2_ffn_ln_g"])], axis=1))
    add("l2_lnb_c", np.concatenate([_cols(w) for w in np.asarray(inputs["L2_ffn_ln_b"])], axis=1))

    add("l3_WqT", np.asarray(inputs["L3_attn_Wq"]).T)
    add("l3_WkT", np.asarray(inputs["L3_attn_Wk"]).T)
    add("l3_WvT", np.asarray(inputs["L3_attn_Wv"]).T)
    add("l3_WoT", np.asarray(inputs["L3_attn_Wo"]).T)
    add("l3_bo_r", _rows(inputs["L3_attn_bo"]))
    add("l3_lng_c", _cols(inputs["L3_attn_ln_g"]))
    add("l3_lnb_c", _cols(inputs["L3_attn_ln_b"]))
    add("l3_qg", np.tile(np.asarray(inputs["L3_attn_q_gamma"]), 2).reshape(128, 1))
    add("l3_kg", np.tile(np.asarray(inputs["L3_attn_k_gamma"]), 2).reshape(128, 1))
    add("l3_wsw", np.asarray(inputs["L3_ffn_w_switch"]))
    add("l3_bsw_r", np.pad(np.asarray(inputs["L3_ffn_b_switch"]), (0, 8)).reshape(1, 16))

    l3w1 = np.asarray(inputs["L3_ffn_w1"])
    l3b1 = np.asarray(inputs["L3_ffn_b1"])
    l3w2 = np.asarray(inputs["L3_ffn_w2"])
    l3b2 = np.asarray(inputs["L3_ffn_b2"])

    ib = np.asarray(inputs["L1_ffn_index_b"]).astype(np.int64)
    it = np.asarray(inputs["L1_ffn_index_t"]).astype(np.int64)
    flat = (ib * SEQ + it).astype(np.float32)
    flat_tiles = np.ascontiguousarray(flat.reshape(NT, 128).T)

    in_maps = []
    for c in range(NCORES):
        m = dict(com)
        m["x_in"] = np.ascontiguousarray(xf[c].T)
        m["l3_ew1"] = np.ascontiguousarray(l3w1[c], dtype=np.float32)
        m["l3_eb1_c"] = _cols(l3b1[c])
        m["l3_ew2"] = np.ascontiguousarray(l3w2[c], dtype=np.float32)
        m["l3_eb2_r"] = _rows(l3b2[c])
        m["l1_flat"] = flat_tiles
        m["l1_myw1T"] = np.ascontiguousarray(l1ew1T[c], dtype=np.float32)
        m["l1_myb1_r"] = np.ascontiguousarray(l1eb1[c], dtype=np.float32)
        m["l1_myw2T"] = np.ascontiguousarray(l1ew2T[c], dtype=np.float32)
        m["l1_myb2_r"] = np.ascontiguousarray(l1eb2[c], dtype=np.float32)
        m["l1_myw3T"] = np.ascontiguousarray(l1ew3T[c], dtype=np.float32)
        m["l1_myb3_r"] = np.ascontiguousarray(l1eb3[c], dtype=np.float32)
        ci = np.zeros((128, 8), dtype=np.float32)
        ci[:, 0] = c * TOK
        ci[:, 1] = c
        ci[:, 2] = (c % 2) * TOK
        ci[:, 3] = c * 256
        ci[:, 4] = c * 1024
        m["core_info"] = ci
        in_maps.append(m)
    return in_maps



def build(n_layers=4):
    from contextlib import ExitStack
    es = ExitStack()
    nc = bass.Bass("TRN2", target_bir_lowering=False, debug=False, num_devices=NCORES)
    ctx = Ctx(nc, es)

    DI = {}
    def din(name, shape, dt=F32):
        DI[name] = nc.dram_tensor(name, list(shape), dt, kind="ExternalInput")
        return DI[name]

    din("x_in", [D, TOK]); din("core_info", [128, 8]); din("l1_flat", [128, NT])
    din("l0_qkvT", [D, 3 * D]); din("l0_projT", [D, D]); din("l0_projb_r", [1, D])
    din("l0_qnw", [128, 1]); din("l0_knw", [128, 1])
    din("l0_wfcT", [D, 4096]); din("l0_lnw_c", [128, 32]); din("l0_lnb_c", [128, 32])
    din("l0_wprT", [4096, D])
    din("l1_WqT", [D, D]); din("l1_bq_r", [1, D]); din("l1_WkT", [D, D]); din("l1_bk_r", [1, D])
    din("l1_WvT", [D, D]); din("l1_bv_r", [1, D]); din("l1_gamma_c", [128, 8]); din("l1_beta_c", [128, 8])
    din("l1_sw1T", [D, 512]); din("l1_sb1_r", [1, 512]); din("l1_sw2T", [D, 512]); din("l1_sb2_r", [1, 512])
    din("l1_sw3T", [512, D]); din("l1_sb3_r", [1, D]); din("l1_slng_c", [128, 4]); din("l1_slnb_c", [128, 4])
    din("l1_gwT", [D, 9]); din("l1_gb_r", [1, 16]); din("l1_expb_r", [1, 16])
    din("l1_myw1T", [D, 512]); din("l1_myb1_r", [1, 512])
    din("l1_myw2T", [D, 512]); din("l1_myb2_r", [1, 512])
    din("l1_myw3T", [512, D]); din("l1_myb3_r", [1, D])
    din("l1_selg_c", [128, 4]); din("l1_selb_c", [128, 4])
    din("l2_Wq", [6 * D, D]); din("l2_bq_r", [6, D]); din("l2_Wk", [6 * D, D]); din("l2_bk_r", [6, D])
    din("l2_Wv", [6 * D, D]); din("l2_bv_r", [6, D]); din("l2_Wo", [6 * D, D]); din("l2_bo_r", [6, D])
    din("l2_ln1w_c", [128, 48]); din("l2_ln1b_c", [128, 48])
    din("l2_W1", [6 * D, 4096]); din("l2_b1_c", [128, 192]); din("l2_W2", [6 * 4096, D]); din("l2_b2_r", [6, D])
    din("l2_ln2w_c", [128, 48]); din("l2_ln2b_c", [128, 48]); din("l2_lnfw_c", [128, 8]); din("l2_lnfb_c", [128, 8])
    din("l2_gw", [D, 8]); din("l2_w1", [8 * D, 2048]); din("l2_fb1_c", [128, 128])
    din("l2_w2", [8 * 2048, D]); din("l2_fb2_r", [8, D]); din("l2_lng_c", [128, 64]); din("l2_lnb_c", [128, 64])
    din("l3_WqT", [D, D]); din("l3_WkT", [D, D]); din("l3_WvT", [D, D]); din("l3_WoT", [D, D])
    din("l3_bo_r", [1, D]); din("l3_lng_c", [128, 8]); din("l3_lnb_c", [128, 8])
    din("l3_qg", [128, 1]); din("l3_kg", [128, 1]); din("l3_wsw", [D, 8]); din("l3_bsw_r", [1, 16])
    din("l3_ew1", [D, 4096]); din("l3_eb1_c", [128, 32]); din("l3_ew2", [4096, D]); din("l3_eb2_r", [1, D])
    ctx.DI = DI

    x_out = nc.dram_tensor("x_out", [TOK, D], F32, kind="ExternalOutput")
    ctx.x_out = x_out

    ctx.const("c_ident_bf", np.eye(128, dtype=ml_dtypes.bfloat16))
    ctx.const("c_ident_f32", np.eye(128, dtype=np.float32))
    ctx.const("c_ones_col", np.ones((128, 1), dtype=np.float32))
    ctx.const("c_ones_col_bf", np.ones((128, 1), dtype=ml_dtypes.bfloat16))
    ctx.const("c_ones_row128", np.ones((1, 128), dtype=np.float32))
    ctx.const("c_ones_row512", np.ones((1, 512), dtype=np.float32))
    ctx.const("c_tri_incl", np.tril(np.ones((128, 128))).T.astype(ml_dtypes.bfloat16))
    ctx.const("c_tri_excl", np.triu(np.ones((128, 128)), 1).astype(ml_dtypes.bfloat16))
    hm = np.zeros((NCH, 128, 16), dtype=np.float32)
    sl_ = np.zeros((NCH, 16, 128), dtype=np.float32)
    for c in range(NCH):
        hm[c, 0:64, 2 * c] = 1.0
        hm[c, 64:128, 2 * c + 1] = 1.0
        sl_[c, 2 * c, 0:64] = 1.0
        sl_[c, 2 * c + 1, 64:128] = 1.0
    for c in range(NCH):
        ctx.const(f"c_hm{c}", hm[c])
        ctx.const(f"c_sel{c}", sl_[c])
    ctx.const("c_iota8", np.arange(8, dtype=np.float32).reshape(1, 8))
    sel8 = np.zeros((8, 8 * 128), dtype=np.float32)
    for e in range(8):
        sel8[e, e * 128:(e + 1) * 128] = 1.0
    ctx.const("c_sel8", sel8)
    ctx.const("c_iota128", np.arange(128, dtype=np.float32).reshape(128, 1))
    ctx.const("c_zero16", np.zeros((1, 16), dtype=np.float32))

    ctx.sh_xn = nc.dram_tensor("sh_xn", [2 * NCH * 128, TOK], BF16, kind="Internal", addr_space="Shared")
    ctx.sh_kv = nc.dram_tensor("sh_kv", [2, 17 * 4096], F32, kind="Internal", addr_space="Shared")
    ctx.cc_in = nc.dram_tensor("cc_in", [1, 16], F32, kind="Internal")
    ctx.cc_out = nc.dram_tensor("cc_out", [1, 16], F32, kind="Internal")
    ctx.spill_x = nc.dram_tensor("spill_x", [NCH * 128, TOK], F32, kind="Internal")
    ctx.spill_xn = nc.dram_tensor("spill_xn", [NCH * 128, TOK], BF16, kind="Internal")
    ctx.spill_z = nc.dram_tensor("spill_z", [NCH * 128, TOK], F32, kind="Internal")
    ctx.v_loc = nc.dram_tensor("v_loc", [SEQ, D], BF16, kind="Internal")
    ctx.l2_gsc = nc.dram_tensor("l2_gsc", [TOK, 16], F32, kind="Internal")
    ctx.l1_xn_tm = nc.dram_tensor("l1_xn_tm", [TOK + 1, D], BF16, kind="Internal")
    ctx.l1_glog_my = nc.dram_tensor("l1_glog_my", [TOK + 1, 16], F32, kind="Internal")
    ctx.l1_glog_all = nc.dram_tensor("l1_glog_all", [TOK, 16], F32, kind="Internal")
    ctx.l1_xmoe_my = nc.dram_tensor("l1_xmoe_my", [TOK, D], BF16, kind="Internal")
    ctx.l1_xmoe_all = nc.dram_tensor("l1_xmoe_all", [TOK, D], BF16, kind="Internal", addr_space="Shared")
    ctx.l1_contrib = nc.dram_tensor("l1_contrib", [TOK + 1, D], F32, kind="Internal")
    ctx.l1_contrib_all = nc.dram_tensor("l1_contrib_all", [TOK + 1, D], F32, kind="Internal", addr_space="Shared")
    ctx.l1_pack = nc.dram_tensor("l1_pack", [TOK + 1, 8], F32, kind="Internal")
    ctx.l1_inv = nc.dram_tensor("l1_inv", [TOK + 1, 2], F32, kind="Internal")
    ctx.l3_logit = nc.dram_tensor("l3_logit", [TOK, 16], F32, kind="Internal")
    ctx.l3_logit_all = nc.dram_tensor("l3_logit_all", [NCORES * TOK, 16], F32, kind="Internal", addr_space="Shared")
    ctx.l3_xn_my = nc.dram_tensor("l3_xn_my", [TOK, D], BF16, kind="Internal")
    ctx.l3_rt = nc.dram_tensor("l3_rt", [NCORES * TOK, 2], F32, kind="Internal")
    ctx.l3_xb_my = nc.dram_tensor("l3_xb_my", [NCORES * 1024 + 1, D], BF16, kind="Internal")
    ctx.l3_xb_all = nc.dram_tensor("l3_xb_all", [NCORES * 1024 + 1, D], BF16, kind="Internal", addr_space="Shared")
    ctx.l3_y = nc.dram_tensor("l3_y", [1024, D], BF16, kind="Internal")
    ctx.l3_y_all = nc.dram_tensor("l3_y_all", [NCORES * 1024 + 1, D], BF16, kind="Internal", addr_space="Shared")
    ctx.wq_fc = nc.dram_tensor("wq_fc", [D, 4096], BF16, kind="Internal")
    ctx.wq_pr = nc.dram_tensor("wq_pr", [4096, D], BF16, kind="Internal")

    ctx.x_sb = es.enter_context(nc.sbuf_tensor("x_sb", [128, NCH * TOK], F32))
    ctx.xn_bf = es.enter_context(nc.sbuf_tensor("xn_bf", [128, NCH * TOK], BF16))
    ctx.rw = None
    ctx.cinfo = es.enter_context(nc.sbuf_tensor("cinfo", [128, 8], F32))
    csb = {}
    const_list = ["ident_bf", "ident_f32", "ones_col", "ones_col_bf", "ones_row128", "ones_row512",
                  "tri_incl", "tri_excl", "iota8", "iota128"] + \
                 [f"hm{c}" for c in range(NCH)] + [f"sel{c}" for c in range(NCH)]
    for nm in const_list:
        t = ctx.consts["c_" + nm]
        csb[nm] = es.enter_context(nc.sbuf_tensor("sb_" + nm, list(t.shape), t.dtype))
    ctx.csb = csb

    PS = [es.enter_context(nc.psum_tensor(f"ps{i}", [128, 512], F32)) for i in range(5)]
    ctx.PS = PS
    ctx.PRA = es.enter_context(nc.psum_tensor("pra", [128, 512], F32))
    ctx.PRB = es.enter_context(nc.psum_tensor("prb", [128, 512], F32))
    ctx.PTR = es.enter_context(nc.psum_tensor("ptrn", [128, 512], F32))

    def xc(c): return ctx.x_sb[:, c * TOK:(c + 1) * TOK]
    def xnc(c): return ctx.xn_bf[:, c * TOK:(c + 1) * TOK]
    ctx.xc = xc
    ctx.xnc = xnc

    with nc.Block() as blk:
        @blk.sync
        def _(s):
            ctx.dma(s, ctx.cinfo[:, :], DI["core_info"].ap())
            for c in range(NCH):
                ctx.dma(s, xc(c), DI["x_in"].ap()[c * 128:(c + 1) * 128, :])
            for nm in const_list:
                ctx.dma(s, csb[nm][:, :], ctx.consts["c_" + nm].ap())
            ctx.dma(s, ctx.cc_in.ap(), ctx.consts["c_zero16"].ap())
            s.wait_ge(ctx.sems["dio"], ctx.cnt["dio"])

    emit_layer0(ctx)
    if n_layers > 1:
        emit_layer1(ctx)
    if n_layers > 2:
        emit_layer2(ctx)
    if n_layers > 3:
        emit_layer3(ctx)

    if not ctx.output_written:
        emit_store_x_tokenmajor(ctx)

    nc.finalize()
    return nc, ctx


def emit_store_x_tokenmajor(ctx):
    ctx.phase = "emit_store_x_tokenmajor"
    nc = ctx.nc
    ctx.sem("dout")
    with nc.sbuf_tensor(ctx.uname("out_stage"), [128, 2 * D], F32) as stage, nc.Block() as blk:
        dout_base = ctx.cnt["dout"]
        jobs = []
        for tt in range(NT):
            for c in range(NCH):
                def mk(tt, c):
                    def mm(t, ps):
                        return t.transpose(ps[:, 0:128], ctx.xc(c)[:, tt * 128:(tt + 1) * 128],
                                           ctx.csb["ident_f32"][:, :])
                    def ev(a, ps):
                        if c == 0 and tt >= 2:
                            ctx.wait_future(a, "dout", dout_base + 16 * (tt - 1))
                        return a.copy(stage[:, (tt % 2) * D + c * 128:(tt % 2) * D + (c + 1) * 128],
                                      ps[:, 0:128])
                    return {"mm": mm, "evac": ev, "evac_eng": "act"}
                jobs.append(mk(tt, c))
        ev_base = ctx.cnt["evac"]
        run_pipeline(ctx, blk, jobs, banks=[ctx.PS[0], ctx.PS[1], ctx.PS[2], ctx.PS[3]])

        @blk.sync
        def _(s):
            for tt in range(NT):
                ctx.wait_future(s, "evac", ev_base + (tt + 1) * NCH)
                ctx.dma(s, ctx.x_out.ap()[tt * 128:(tt + 1) * 128, :],
                        stage[:, (tt % 2) * D:(tt % 2) * D + D], sem="dout")
            s.wait_ge(ctx.sems["dout"], ctx.cnt["dout"])
    ctx.bank_clear()
    ctx.output_written = True


LN_EPS_OUTER = 1e-5


def emit_outer_ln(ctx, xn_f32_sink=None, rows_out=None):
    ctx.phase = "emit_outer_ln"
    nc = ctx.nc
    with nc.sbuf_tensor(ctx.uname("ln_sq0"), [128, TOK], F32) as sq0, \
         nc.sbuf_tensor(ctx.uname("ln_sq1"), [128, TOK], F32) as sq1, \
         nc.sbuf_tensor(ctx.uname("ln_bA"), [128, TOK], F32) as bA, \
         nc.sbuf_tensor(ctx.uname("ln_bB"), [128, TOK], F32) as bB, \
         nc.sbuf_tensor(ctx.uname("ln_xf"), [128, TOK], F32) as xf, \
         nc.sbuf_tensor(ctx.uname("ln_rw"), [1, 3 * TOK], F32) as rwt:
        ctx.rw = rwt
        with nc.Block() as blk:
            emit_ln_rows(ctx, blk, ctx.xc, NCH, TOK, LN_EPS_OUTER, 1.0 / D, [sq0, sq1], 0)
        ctx.bank_clear()
        with nc.Block() as blk:
            emit_bcast(ctx, blk, TOK, TOK, bA)
            emit_bcast(ctx, blk, 2 * TOK, TOK, bB)
            if rows_out is not None:
                @blk.vector
                def _(v):
                    m1 = ctx.inc(v.tensor_copy(rows_out[0:1, 0:TOK], ctx.rw[0:1, TOK:2 * TOK]), "dve")
                    ctx.inc(v.tensor_copy(rows_out[0:1, TOK:2 * TOK], ctx.rw[0:1, 2 * TOK:3 * TOK]), "dve")
                    ctx.wait(v, "dve")
        ctx.bank_clear()
        with nc.Block() as blk:
            @blk.vector
            def _(v):
                for c in range(NCH):
                    m1 = ctx.inc(v.tensor_mul(xf[:, :], ctx.xc(c), bA[:, :]), "dve")
                    ctx.wait(v, "dve", m1)
                    m2 = ctx.inc(v.tensor_add(xf[:, :], xf[:, :], bB[:, :]), "dve")
                    ctx.wait(v, "dve", m2)
                    ctx.inc(v.tensor_copy(ctx.xnc(c), xf[:, :]), "dve")
                    if xn_f32_sink is not None:
                        xn_f32_sink(v, c, xf)
                    ctx.wait(v, "dve")
        ctx.bank_clear()


def emit_spill_x(ctx):
    nc = ctx.nc
    with nc.Block() as blk:
        @blk.sync
        def _(s):
            for c in range(NCH):
                ctx.dma(s, ctx.spill_x.ap()[c * 128:(c + 1) * 128, :], ctx.xc(c), sem="dio")
            s.wait_ge(ctx.sems["dio"], ctx.cnt["dio"])


def emit_reload_x(ctx):
    nc = ctx.nc
    with nc.Block() as blk:
        @blk.sync
        def _(s):
            for c in range(NCH):
                ctx.dma(s, ctx.xc(c), ctx.spill_x.ap()[c * 128:(c + 1) * 128, :], sem="dio")
            s.wait_ge(ctx.sems["dio"], ctx.cnt["dio"])


def emit_pair_barrier(ctx, g):
    m = ctx.inc(g.collective_compute(
        "AllReduce", ALU.add, replica_groups=[[0, 1], [2, 3], [4, 5], [6, 7]],
        ins=[ctx.cc_in.ap()], outs=[ctx.cc_out.ap()]), "cc", 1)
    g.wait_ge(ctx.sems["cc"], m)
    return m


def emit_share_acts(ctx, src_fn):
    ctx.phase = "emit_share_acts"
    nc = ctx.nc
    with nc.sbuf_tensor(ctx.uname("shidx"), [128, 1], U32) as shidx, \
         nc.sbuf_tensor(ctx.uname("shidx_f"), [128, 1], F32) as shidx_f, nc.Block() as blk:
        @blk.vector
        def _(v):
            m1 = ctx.inc(v.tensor_scalar_mul(shidx_f[:, :], ctx.cinfo[:, 2:3], 0.5), "dve")
            ctx.wait(v, "dve", m1)
            m2 = ctx.inc(v.tensor_add(shidx_f[:, :], shidx_f[:, :], ctx.csb["iota128"][:, :]), "dve")
            ctx.wait(v, "dve", m2)

        @blk.gpsimd
        def _(g):
            ctx.wait(g, "dve")
            ctx.inc(g.tensor_copy(shidx[:, :], shidx_f[:, :]), "pool")
            ctx.wait(g, "pool")
            for c in range(NCH):
                ctx.inc(g.indirect_dma_start(
                    ctx.sh_xn.ap(), IndirectOffsetOnAxis(shidx[:, 0:1], axis=0),
                    src_fn(c), None, element_offset=c * 128 * TOK), "dsh", 16)
            g.wait_ge(ctx.sems["dsh"], ctx.cnt["dsh"])
            emit_pair_barrier(ctx, g)
    ctx.bank_clear()


def emit_l0_quantize(ctx):
    ctx.phase = "emit_l0_quantize"
    nc = ctx.nc
    for tag, (src, dst, nch_, ncols) in [("fc", (ctx.DI["l0_wfcT"], ctx.wq_fc, 8, 4096)),
                                         ("pr", (ctx.DI["l0_wprT"], ctx.wq_pr, 32, 1024))]:
        nelem = float(nch_ * 128 * ncols)
        with nc.sbuf_tensor(f"qz_w0_{tag}", [128, ncols], F32) as w0, \
             nc.sbuf_tensor(f"qz_w1_{tag}", [128, ncols], F32) as w1, \
             nc.sbuf_tensor(f"qz_acc_{tag}", [128, 1], F32) as acc, \
             nc.sbuf_tensor(f"qz_col_{tag}", [128, 1], F32) as col, \
             nc.sbuf_tensor(f"qz_thr_{tag}", [128, 2], F32) as thr, \
             nc.sbuf_tensor(f"qz_m_{tag}", [128, ncols], BF16) as m1b, \
             nc.sbuf_tensor(f"qz_q_{tag}", [128, ncols], BF16) as qb, \
             nc.sbuf_tensor(f"qz_rw_{tag}", [1, 4], F32) as rwt:
            ctx.rw = rwt
            wb = [w0, w1]
            with nc.Block() as blk:
                dwt_base = ctx.cnt.get("dwt", 0)
                ctx.sem("dwt"); dwt_base = ctx.cnt["dwt"]
                dve_base = ctx.cnt.get("dve", 0)
                ctx.sem("dve"); dve_base = ctx.cnt["dve"]
                @blk.sync
                def _(s):
                    for kc in range(nch_):
                        if kc >= 2:
                            ctx.wait_future(s, "dve", dve_base + 1 + 2 * (kc - 1))
                        ctx.dma(s, wb[kc % 2][:, :], src.ap()[kc * 128:(kc + 1) * 128, :], sem="dwt")

                @blk.vector
                def _(v):
                    ctx.inc(v.memset(acc[:, :], 0.0), "dve")
                    for kc in range(nch_):
                        ctx.wait_future(v, "dwt", dwt_base + 16 * (kc + 1))
                        ctx.wait(v, "dve")
                        mm = ctx.inc(v.reduce_sum(col[:, :], wb[kc % 2][:, :], axis=AX.X,
                                                  apply_absolute_value=True), "dve")
                        ctx.wait(v, "dve", mm)
                        ctx.inc(v.tensor_add(acc[:, :], acc[:, :], col[:, :]), "dve")
                    ctx.wait(v, "dve")

                @blk.tensor
                def _(t):
                    ctx.wait(t, "dve")
                    ctx.bank_acquire(t, ctx.PRA)
                    m = ctx.inc(t.matmul(ctx.PRA[0:1, 0:1], acc[:, :], ctx.csb["ones_col"][:, :],
                                         start=True, stop=True), "pe")

                @blk.vector
                def _(v):
                    ctx.wait(v, "pe")
                    row_chain(ctx, v, [
                        lambda e: e.tensor_scalar(ctx.rw[0:1, 0:1], ctx.PRA[0:1, 0:1],
                                                  1.0 / nelem, 1e-6, ALU.mult, ALU.add),
                        lambda e: e.tensor_scalar_mul(ctx.rw[0:1, 0:1], ctx.rw[0:1, 0:1], 0.5),
                    ])

                @blk.tensor
                def _(t):
                    ctx.wait(t, "dve")
                    ctx.bank_acquire(t, ctx.PRB)
                    ctx.inc(t.matmul(ctx.PRB[0:128, 0:1], ctx.csb["ones_row128"][:, :],
                                     ctx.rw[0:1, 0:1], start=True, stop=True), "pe")

                @blk.vector
                def _(v):
                    ctx.wait(v, "pe")
                    m1 = ctx.inc(v.tensor_copy(thr[:, 0:1], ctx.PRB[0:128, 0:1]), "dve")
                    ctx.wait(v, "dve", m1)
                    ctx.inc(v.tensor_scalar_mul(thr[:, 1:2], thr[:, 0:1], -1.0), "dve")
                    ctx.wait(v, "dve")
                ctx.bank_release(ctx.PRA, "dve", ctx.cnt["dve"])
                ctx.bank_release(ctx.PRB, "dve", ctx.cnt["dve"])
            ctx.bank_clear()
            with nc.Block() as blk:
                ctx.sem("dwt"); dwt_base = ctx.cnt["dwt"]
                ctx.sem("dve"); dve_base = ctx.cnt["dve"]
                ctx.sem("dq"); dq_base = ctx.cnt["dq"]
                @blk.sync
                def _(s):
                    for kc in range(nch_):
                        if kc >= 2:
                            ctx.wait_future(s, "dve", dve_base + 3 * (kc - 1))
                        ctx.dma(s, wb[kc % 2][:, :], src.ap()[kc * 128:(kc + 1) * 128, :], sem="dwt")

                @blk.vector
                def _(v):
                    for kc in range(nch_):
                        ctx.wait_future(v, "dwt", dwt_base + 16 * (kc + 1))
                        if kc >= 1:
                            ctx.wait_future(v, "dq", dq_base + 16 * kc)
                        mm = ctx.inc(v.tensor_scalar(m1b[:, :], wb[kc % 2][:, :], thr[:, 0:1],
                                                     None, ALU.is_gt), "dve")
                        mm = ctx.inc(v.tensor_scalar(qb[:, :], wb[kc % 2][:, :], thr[:, 1:2],
                                                     None, ALU.is_lt), "dve")
                        ctx.wait(v, "dve", mm)
                        ctx.inc(v.tensor_sub(qb[:, :], m1b[:, :], qb[:, :]), "dve")
                    ctx.wait(v, "dve")

                @blk.gpsimd
                def _(g):
                    for kc in range(nch_):
                        ctx.wait_future(g, "dve", dve_base + 3 * (kc + 1))
                        ctx.dma(g, dst.ap()[kc * 128:(kc + 1) * 128, :], qb[:, :], sem="dq")
                    g.wait_ge(ctx.sems["dq"], ctx.cnt["dq"])
            ctx.bank_clear()


def load_w_tiles(ctx, s, wbuf, dram_ap, row0, col0, nrows_ch, ncols, sem="dwt"):
    m = None
    for kc in range(nrows_ch):
        m = ctx.dma(s, wbuf[:, kc * ncols:(kc + 1) * ncols],
                    dram_ap[row0 + kc * 128:row0 + (kc + 1) * 128, col0:col0 + ncols], sem=sem)
    return m


def emit_qk_norm_phase(ctx, src_fn, wdram, col0, gain_col, rms_style, out_sb, out_stride, out_col0,
                       wbuf, w_loaded_m, extra_scale=None):
    ctx.phase = "emit_qk_norm_phase"
    nc = ctx.nc
    HT = 1024
    with nc.sbuf_tensor(ctx.uname("qk_qf"), [128, HT], F32) as qf, \
         nc.sbuf_tensor(ctx.uname("qk_qsq"), [128, HT], F32) as qsq, \
         nc.sbuf_tensor(ctx.uname("qk_rms"), [16, HT], F32) as rms_scr:
      bR = qsq
      for m in range(NCH):
       for th in range(2):
        tof = th * HT
        with nc.Block() as blk:
            jobs = []
            for b in range(2):
                def mk(b):
                    def mm(t, ps):
                        if b == 0:
                            ctx.wait_future(t, "dwt", w_loaded_m)
                        last = None
                        for kc in range(NCH):
                            last = t.matmul(ps[:, :],
                                            wbuf[:, kc * 1024 + m * 128:kc * 1024 + (m + 1) * 128],
                                            src_fn(kc)[:, tof + b * 512:tof + (b + 1) * 512],
                                            start=(kc == 0), stop=(kc == NCH - 1))
                        return last
                    def ev(a, ps):
                        return a.copy(qf[:, b * 512:(b + 1) * 512], ps[:, :])
                    return {"mm": mm, "evac": ev, "evac_eng": "act"}
                jobs.append(mk(b))
            run_pipeline(ctx, blk, jobs, banks=[ctx.PS[0], ctx.PS[1]])

            @blk.scalar
            def _(a):
                ctx.wait(a, "evac")
                ctx.inc(a.activation(qsq[:, :], qf[:, :], AF.Square), "act")

            @blk.tensor
            def _(t):
                ctx.wait(t, "act")
                banks = [ctx.PS[4], ctx.PRA]
                for b in range(2):
                    ctx.bank_acquire(t, banks[b])
                    ctx.inc(t.matmul(banks[b][0:16, :], ctx.csb[f"hm{m}"][:, :],
                                     qsq[:, b * 512:(b + 1) * 512], start=True, stop=True), "pe")

            @blk.scalar
            def _(a):
                banks = [ctx.PS[4], ctx.PRA]
                ctx.wait(a, "pe")
                for b in range(2):
                    ctx.inc(a.copy(rms_scr[:, b * 512:(b + 1) * 512], banks[b][0:16, :]), "act")

            @blk.vector
            def _(v):
                ctx.wait(v, "act")
                if rms_style == "mean":
                    row_chain(ctx, v, [
                        lambda e: e.tensor_scalar(rms_scr[:, :], rms_scr[:, :], 1.0 / 64.0, 1e-6,
                                                  ALU.mult, ALU.add),
                    ])

            @blk.scalar
            def _(a):
                ctx.wait(a, "dve")
                ctx.inc(a.activation(rms_scr[:, :], rms_scr[:, :], AF.Sqrt), "act")

            @blk.vector
            def _(v):
                ctx.wait(v, "act")
                if rms_style == "mean":
                    row_chain(ctx, v, [lambda e: e.reciprocal(rms_scr[:, :], rms_scr[:, :])])
                else:
                    row_chain(ctx, v, [
                        lambda e: e.tensor_scalar_mul(rms_scr[:, :], rms_scr[:, :], 0.125),
                        lambda e: e.tensor_scalar_max(rms_scr[:, :], rms_scr[:, :], 1e-8),
                        lambda e: e.reciprocal(rms_scr[:, :], rms_scr[:, :]),
                    ])

            @blk.tensor
            def _(t):
                ctx.wait(t, "dve")
                banks = [ctx.PS[4], ctx.PRA]
                for b in range(2):
                    ctx.bank_acquire(t, banks[b])
                    ctx.inc(t.matmul(banks[b][:, :], ctx.csb[f"sel{m}"][:, :],
                                     rms_scr[:, b * 512:(b + 1) * 512], start=True, stop=True), "pe")

            @blk.scalar
            def _(a):
                banks = [ctx.PS[4], ctx.PRA]
                ctx.wait(a, "pe")
                for b in range(2):
                    ctx.inc(a.copy(bR[:, b * 512:(b + 1) * 512], banks[b][:, :]), "act")
                for b in range(2):
                    ctx.bank_release(banks[b], "act", ctx.cnt["act"])

            @blk.vector
            def _(v):
                ctx.wait(v, "act")
                mm = ctx.inc(v.tensor_mul(qf[:, :], qf[:, :], bR[:, :]), "dve")
                ctx.wait(v, "dve", mm)
                dst = out_sb[:, m * out_stride + out_col0 + tof:m * out_stride + out_col0 + tof + HT]
                if gain_col is not None:
                    ctx.inc(v.tensor_scalar_mul(dst, qf[:, :], gain_col), "dve")
                else:
                    ctx.inc(v.tensor_copy(dst, qf[:, :]), "dve")
                ctx.wait(v, "dve")
        ctx.bank_clear()


def emit_v_phase(ctx, src_fn, wdram, col0, wbuf, w_loaded_m, slot, bias_row=None,
                 act_fn=None, v_dst=None):
    ctx.phase = "emit_v_phase"
    nc = ctx.nc
    if v_dst is None:
        v_dst = ctx.v_loc
    with nc.sbuf_tensor(ctx.uname("vtm"), [128, 8 * 1024], BF16) as vtm, nc.Block() as blk:
        ctx.sem("dio")
        dio_marks = {}
        ev_base = ctx.cnt["evac"]
        jobs = []
        for tt in range(NT):
            for vb in range(2):
                def mk(tt, vb):
                    def mm(t, ps):
                        if tt == 0 and vb == 0:
                            ctx.wait_future(t, "dwt", w_loaded_m)
                        last = None
                        for kc in range(NCH):
                            last = t.matmul(ps[:, :], src_fn(kc)[:, tt * 128:(tt + 1) * 128],
                                            wbuf[:, kc * 1024 + vb * 512:kc * 1024 + (vb + 1) * 512],
                                            start=(kc == 0), stop=(bias_row is None and kc == NCH - 1))
                        if bias_row is not None:
                            last = t.matmul(ps[:, :], ctx.csb["ones_row128"][:, :],
                                            bias_row[0:1, vb * 512:(vb + 1) * 512],
                                            start=False, stop=True)
                        return last
                    def ev(a, ps):
                        if tt >= 8 and vb == 0 and tt % 8 == 0:
                            pass
                        if tt == 8 and vb == 0:
                            ctx.wait_future(a, "dio", dio_marks["h0"])
                        dst = vtm[:, (tt % 8) * 1024 + vb * 512:(tt % 8) * 1024 + (vb + 1) * 512]
                        if act_fn is not None:
                            return a.activation(dst, ps[:, :], act_fn)
                        return a.copy(dst, ps[:, :])
                    return {"mm": mm, "evac": ev, "evac_eng": "act"}
                jobs.append(mk(tt, vb))
        @blk.sync
        def _(s):
            ctx.wait_future(s, "evac", ev_base + 16)
            dio_marks["h0"] = ctx.dma(
                s, v_dst.ap()[slot * TOK:slot * TOK + 1024, :].rearrange("(tt p) d -> p tt d", p=128),
                vtm[:, :].rearrange("p (tt d) -> p tt d", d=D), sem="dio")
            ctx.wait_future(s, "evac", ev_base + 32)
            ctx.dma(s, v_dst.ap()[slot * TOK + 1024:(slot + 1) * TOK, :].rearrange("(tt p) d -> p tt d", p=128),
                    vtm[:, :].rearrange("p (tt d) -> p tt d", d=D), sem="dio")
            s.wait_ge(ctx.sems["dio"], ctx.cnt["dio"])
        run_pipeline(ctx, blk, jobs, banks=[ctx.PS[0], ctx.PS[1], ctx.PS[2], ctx.PS[3]])
    ctx.bank_clear()

def emit_attn_heads(ctx, q_sb, k_sb, out_sb, heads=range(HEADS)):
    nc = ctx.nc
    with nc.sbuf_tensor(ctx.uname("at_vshm"), [128, 32 * 65], BF16) as vshm, \
         nc.sbuf_tensor(ctx.uname("at_kbuf"), [65, SEQ], BF16) as kbuf, \
         nc.sbuf_tensor(ctx.uname("at_q65"), [65, TOK], BF16) as q65, \
         nc.sbuf_tensor(ctx.uname("at_slab"), [128, 8 * 512], BF16) as slab, \
         nc.sbuf_tensor(ctx.uname("at_binv"), [128, 512], F32) as binv, \
         nc.sbuf_tensor(ctx.uname("at_rw"), [1, 2560], F32) as arw:
        for h in heads:
            hc, rb = h // 2, (h % 2) * 64
            with nc.Block() as blk:
                @blk.sync
                def _(s):
                    src = ctx.v_loc.ap()[:, h * 64:(h + 1) * 64].rearrange(
                        "(kt p) d -> p kt d", p=128)
                    dst = vshm[:, :].rearrange("p (kt d) -> p kt d", d=65)[:, :, 0:64]
                    ctx.dma(s, dst, src, sem="dio", allow_slow_non_contiguous=True)
                v_m = ctx.cnt["dio"]

                @blk.gpsimd
                def _(g):
                    ctx.inc(g.memset(vshm[:, :].rearrange("p (kt d) -> p kt d", d=65)[:, :, 64:65], 1.0), "pool")
                    ctx.inc(g.memset(kbuf[64:65, :], -1.0), "pool")
                pool_m = ctx.cnt["pool"]

                @blk.scalar
                def _(a):
                    ctx.inc(a.copy(kbuf[0:64, :], k_sb[rb:rb + 64, hc * SEQ:(hc + 1) * SEQ]), "act")
                    ctx.inc(a.copy(q65[0:64, :], q_sb[rb:rb + 64, hc * TOK:(hc + 1) * TOK]), "act")
                kq_m = ctx.cnt["act"]

                @blk.gpsimd
                def _(g):
                    ctx.inc(g.memset(q65[64:65, :], 20.0), "pool")
                q65_m = ctx.cnt["act"]
                pool_m = ctx.cnt["pool"]

                prev_av = None
                for nb_i in range(NB):
                    for hf in range(4):
                        jobs = []
                        for kt in range(hf * 8, (hf + 1) * 8):
                            def mk(kt, nb_i, hf, prev_av):
                                def mm(t, ps):
                                    if kt == 0:
                                        ctx.wait_future(t, "act", q65_m)
                                        ctx.wait_future(t, "pool", pool_m)
                                    return t.matmul(ps[:, :], kbuf[:, kt * 128:(kt + 1) * 128],
                                                    q65[:, nb_i * 512:(nb_i + 1) * 512],
                                                    start=True, stop=True)
                                def ev(a, ps):
                                    if kt % 8 == 0 and prev_av is not None:
                                        ctx.wait_future(a, "pe", prev_av)
                                    return a.activation(slab[:, (kt % 8) * 512:(kt % 8 + 1) * 512],
                                                        ps[:, :], AF.Exp)
                                return {"mm": mm, "evac": ev, "evac_eng": "act"}
                            jobs.append(mk(kt, nb_i, hf, prev_av))
                        ev_slab = ctx.cnt["evac"] + 8
                        run_pipeline(ctx, blk, jobs, banks=[ctx.PS[0], ctx.PS[1], ctx.PS[2], ctx.PS[3]])

                        @blk.tensor
                        def _(t):
                            ctx.wait_future(t, "evac", ev_slab)
                            if hf == 0:
                                ctx.bank_acquire(t, ctx.PS[4])
                            if nb_i == 0 and hf == 0:
                                ctx.wait_future(t, "dio", v_m)
                                ctx.wait_future(t, "pool", pool_m)
                            for kt in range(hf * 8, (hf + 1) * 8):
                                last = t.matmul(ctx.PS[4][0:65, :], vshm[:, kt * 65:(kt + 1) * 65],
                                                slab[:, (kt % 8) * 512:(kt % 8 + 1) * 512],
                                                start=(kt == 0), stop=(kt == 31))
                            ctx.inc(last, "pe")
                        prev_av = ctx.cnt["pe"]

                    @blk.vector
                    def _(v):
                        ctx.wait(v, "pe")
                        m = ctx.inc(v.reciprocal(arw[0:1, 2048:2560], ctx.PS[4][64:65, :]), "dve")
                        ctx.wait(v, "dve", m)
                    rec_m = ctx.cnt["dve"]

                    @blk.tensor
                    def _(t):
                        ctx.wait(t, "dve", rec_m)
                        ctx.bank_acquire(t, ctx.PTR)
                        ctx.inc(t.matmul(ctx.PTR[:, :], ctx.csb["ones_row128"][:, :],
                                         arw[0:1, 2048:2560], start=True, stop=True), "pe")

                    @blk.vector
                    def _(v):
                        ctx.wait(v, "pe")
                        m = ctx.inc(v.tensor_copy(binv[:, :], ctx.PTR[:, :]), "dve")
                        ctx.wait(v, "dve", m)
                        dst = out_sb[rb:rb + 64, hc * TOK + nb_i * 512:hc * TOK + (nb_i + 1) * 512]
                        ctx.inc(v.tensor_mul(dst, ctx.PS[4][0:64, :], binv[0:64, :]), "dve")
                    ctx.bank_release(ctx.PTR, "dve", ctx.cnt["dve"])
                    ctx.bank_release(ctx.PS[4], "dve", ctx.cnt["dve"])
            ctx.bank_clear()


def emit_fm_matmul(ctx, blk, lhs_tile, rhs_tile, nk, nm, nbl, evac, bias_row=None,
                   first_wait=None, banks=None):
    if banks is None:
        banks = [ctx.PS[0], ctx.PS[1], ctx.PS[2], ctx.PS[3]]
    jobs = []
    for m in range(nm):
        for b in range(nbl):
            def mk(m, b):
                def mm(t, ps):
                    if m == 0 and b == 0 and first_wait is not None:
                        first_wait(t)
                    last = None
                    for k in range(nk):
                        last = t.matmul(ps[:, :], lhs_tile(k, m), rhs_tile(k, b),
                                        start=(k == 0), stop=(bias_row is None and k == nk - 1))
                    if bias_row is not None:
                        br, orow = bias_row(m, b)
                        last = t.matmul(ps[:, :], br, orow, start=False, stop=True)
                    return last
                def ev(eng, ps):
                    return evac(m, b, eng, ps)
                return {"mm": mm, "evac": ev, "evac_eng": "dve"}
            jobs.append(mk(m, b))
    run_pipeline(ctx, blk, jobs, banks=banks)


L0_ATTN_CFG = dict(qw=("l0_qkvT", 0), kw=("l0_qkvT", 1024), vw=("l0_qkvT", 2048),
                   qg="l0_qnw", kg="l0_knw", rms="mean", proj="l0_projT", pb="l0_projb_r")
L3_ATTN_CFG = dict(qw=("l3_WqT", 0), kw=("l3_WkT", 0), vw=("l3_WvT", 0),
                   qg="l3_qg", kg="l3_kg", rms="l3", proj="l3_WoT", pb="l3_bo_r")


def emit_l0_attn(ctx, cfg=L0_ATTN_CFG):
    ctx.phase = "emit_l0_attn"
    nc = ctx.nc
    kview = ctx.x_sb[:, :].bitcast(BF16)
    with nc.sbuf_tensor(ctx.uname("l0_qsb"), [128, NCH * TOK], BF16) as q_sb, \
         nc.sbuf_tensor(ctx.uname("l0_gain"), [128, 2], F32) as gain:
      with nc.sbuf_tensor(ctx.uname("l0_wbuf"), [128, NCH * 1024], BF16) as wbuf:
          with nc.Block() as blk:
              @blk.sync
              def _(s):
                  ctx.dma(s, gain[:, 1:2], ctx.DI[cfg["kg"]].ap(), sem="dio")
                  ctx.dma(s, gain[:, 0:1], ctx.DI[cfg["qg"]].ap(), sem="dio")
              @blk.vector
              def _(v):
                  ctx.wait(v, "dio")
                  ctx.inc(v.tensor_scalar_mul(gain[:, 0:1], gain[:, 0:1], 0.125), "dve")
                  ctx.wait(v, "dve")
              @blk.gpsimd
              def _(g):
                  m = load_w_tiles(ctx, g, wbuf, ctx.DI[cfg["qw"][0]].ap(), 0, cfg["qw"][1], NCH, 1024)
                  g.wait_ge(ctx.sems["dwt"], m)
          ctx.bank_clear()
          emit_qk_norm_phase(ctx, ctx.xnc, None, 0, gain[:, 0:1], cfg["rms"],
                             q_sb, TOK, 0, wbuf, ctx.cnt["dwt"])
          ctx.tap("q", q_sb[:, :], [128, NCH * TOK], BF16)
          with nc.sbuf_tensor(ctx.uname("l0_sxn"), [128, NCH * TOK], BF16) as sxn:
              for slot in range(2):
                  with nc.Block() as blk:
                      @blk.gpsimd
                      def _(g):
                          m = load_w_tiles(ctx, g, wbuf, ctx.DI[cfg["kw"][0]].ap(), 0, cfg["kw"][1], NCH, 1024)
                          g.wait_ge(ctx.sems["dwt"], m)
                      @blk.sync
                      def _(s):
                          src = ctx.sh_xn.ap()[slot * 1024:(slot + 1) * 1024, :].rearrange(
                              "(c p) t -> p c t", p=128)
                          ctx.dma(s, sxn[:, :].rearrange("p (c t) -> p c t", t=TOK), src, sem="dio")
                          s.wait_ge(ctx.sems["dio"], ctx.cnt["dio"])
                  ctx.bank_clear()
                  def sfn(c):
                      return sxn[:, c * TOK:(c + 1) * TOK]
                  emit_qk_norm_phase(ctx, sfn, None, 0, gain[:, 1:2], cfg["rms"],
                                     kview, SEQ, slot * TOK, wbuf, ctx.cnt["dwt"])
                  with nc.Block() as blk:
                      @blk.gpsimd
                      def _(g):
                          m = load_w_tiles(ctx, g, wbuf, ctx.DI[cfg["vw"][0]].ap(), 0, cfg["vw"][1], NCH, 1024)
                          g.wait_ge(ctx.sems["dwt"], m)
                  ctx.bank_clear()
                  emit_v_phase(ctx, sfn, None, 0, wbuf, ctx.cnt["dwt"], slot)
          ctx.tap("k", kview[:, :], [128, NCH * SEQ], BF16)
          ctx.tap("v", ctx.v_loc.ap(), [SEQ, D], BF16)
      with nc.sbuf_tensor(ctx.uname("l0_aout"), [128, NCH * TOK], BF16) as aout:
          emit_attn_heads(ctx, q_sb, kview, aout)
          ctx.tap("aout", aout[:, :], [128, NCH * TOK], BF16)
          emit_reload_x(ctx)
          with nc.sbuf_tensor(ctx.uname("l0_pb"), [1, D], F32) as brow_sb, \
               nc.sbuf_tensor(ctx.uname("l0_wproj"), [128, NCH * 1024], BF16) as wbuf:
              with nc.Block() as blk:
                  @blk.gpsimd
                  def _(g):
                      m = load_w_tiles(ctx, g, wbuf, ctx.DI[cfg["proj"]].ap(), 0, 0, NCH, 1024)
                      g.wait_ge(ctx.sems["dwt"], m)
                  @blk.sync
                  def _(s):
                      ctx.dma(s, brow_sb[0:1, :], ctx.DI[cfg["pb"]].ap(), sem="dio")
                      s.wait_ge(ctx.sems["dio"], ctx.cnt["dio"])
              ctx.bank_clear()
              with nc.Block() as blk:
                  def lhs(k, m):
                      return wbuf[:, k * 1024 + m * 128:k * 1024 + (m + 1) * 128]
                  def rhs(k, b):
                      return aout[:, k * TOK + b * 512:k * TOK + (b + 1) * 512]
                  def ev(m, b, eng, ps):
                      dst = ctx.xc(m)[:, b * 512:(b + 1) * 512]
                      return eng.tensor_add(dst, ps[:, :], dst)
                  def brow(m, b):
                      return (brow_sb[0:1, m * 128:(m + 1) * 128], ctx.csb["ones_row512"][:, :])
                  emit_fm_matmul(ctx, blk, lhs, rhs, NCH, NCH, NB, ev, bias_row=brow)
              ctx.bank_clear()

def emit_l0_ffn(ctx):
    ctx.phase = "emit_l0_ffn"
    nc = ctx.nc
    hq = ctx.x_sb
    with nc.sbuf_tensor(ctx.uname("f0_w"), [128, NCH * 1024], BF16) as wstr, \
         nc.sbuf_tensor(ctx.uname("f0_sh"), [128, 32 * 1024], BF16) as sh, \
         nc.sbuf_tensor(ctx.uname("f0_s0"), [128, 512], F32) as s0, \
         nc.sbuf_tensor(ctx.uname("f0_s1"), [128, 512], F32) as s1, \
         nc.sbuf_tensor(ctx.uname("f0_bA"), [128, 512], F32) as bA, \
         nc.sbuf_tensor(ctx.uname("f0_bB"), [128, 512], F32) as bB, \
         nc.sbuf_tensor(ctx.uname("f0_lnw"), [128, 64], F32) as lnwb, \
         nc.sbuf_tensor(ctx.uname("f0_rw"), [1, 3 * 512], F32) as rwt:
        ctx.rw = rwt
        with nc.Block() as blk:
            @blk.sync
            def _(s):
                ctx.dma(s, lnwb[:, 0:32], ctx.DI["l0_lnw_c"].ap(), sem="dio")
                ctx.dma(s, lnwb[:, 32:64], ctx.DI["l0_lnb_c"].ap(), sem="dio")
                s.wait_ge(ctx.sems["dio"], ctx.cnt["dio"])
        for half in range(2):
            for q4 in range(2):
                t0 = half * 1024 + q4 * 512
                for wh in range(4):
                    with nc.Block() as blk:
                        @blk.gpsimd
                        def _(g):
                            load_w_tiles(ctx, g, wstr, ctx.wq_fc.ap(), 0, wh * 1024, NCH, 1024)
                        wm = ctx.cnt["dwt"]
                        jobs = []
                        for mi in range(8):
                            mch = wh * 8 + mi
                            def mk(mi, mch):
                                def mm(t, ps):
                                    if mi == 0:
                                        ctx.wait_future(t, "dwt", wm)
                                    last = None
                                    for k in range(NCH):
                                        last = t.matmul(ps[:, :],
                                                        wstr[:, k * 1024 + mi * 128:k * 1024 + (mi + 1) * 128],
                                                        ctx.xnc(k)[:, t0:t0 + 512],
                                                        start=(k == 0), stop=(k == NCH - 1))
                                    return last
                                def ev(a, ps):
                                    return a.activation(hq[:, mch * 512:(mch + 1) * 512], ps[:, :], AF.Gelu)
                                return {"mm": mm, "evac": ev, "evac_eng": "act"}
                            jobs.append(mk(mi, mch))
                        run_pipeline(ctx, blk, jobs, banks=[ctx.PS[0], ctx.PS[1], ctx.PS[2], ctx.PS[3]])
                    ctx.bank_clear()
                with nc.Block() as blk:
                    def srcf(c):
                        return hq[:, c * 512:(c + 1) * 512]
                    emit_ln_rows(ctx, blk, srcf, 32, 512, 1e-5, 1.0 / 4096.0, [s0, s1], 0)
                ctx.bank_clear()
                with nc.Block() as blk:
                    emit_bcast(ctx, blk, 512, 512, bA)
                    emit_bcast(ctx, blk, 1024, 512, bB)
                ctx.bank_clear()
                with nc.Block() as blk:
                    @blk.vector
                    def _(v):
                        for c in range(32):
                            m1 = ctx.inc(v.tensor_mul(s0[:, :], hq[:, c * 512:(c + 1) * 512], bA[:, :]), "dve")
                            ctx.wait(v, "dve", m1)
                            m2 = ctx.inc(v.tensor_add(s1[:, :], s0[:, :], bB[:, :]), "dve")
                            ctx.wait(v, "dve", m2)
                            ctx.inc(v.tensor_scalar(sh[:, c * 1024 + q4 * 512:c * 1024 + (q4 + 1) * 512],
                                                    s1[:, :], lnwb[:, c:c + 1], lnwb[:, 32 + c:33 + c],
                                                    ALU.mult, ALU.add), "dve")
                            ctx.wait(v, "dve")
                ctx.bank_clear()
            with nc.Block() as blk:
                @blk.sync
                def _(s):
                    for c in range(NCH):
                        ctx.dma(s, ctx.xc(c)[:, half * 1024:half * 1024 + 1024],
                                ctx.spill_x.ap()[c * 128:(c + 1) * 128, half * 1024:half * 1024 + 1024],
                                sem="dio")
                    s.wait_ge(ctx.sems["dio"], ctx.cnt["dio"])
            ctx.bank_clear()
            for wh in range(4):
                with nc.Block() as blk:
                    @blk.gpsimd
                    def _(g):
                        load_w_tiles(ctx, g, wstr, ctx.wq_pr.ap(), 0, wh * 256, 32, 256)
                    wm = ctx.cnt["dwt"]
                    jobs = []
                    for mi in range(2):
                        m_out = wh * 2 + mi
                        for b in range(2):
                            def mk(mi, m_out, b):
                                def mm(t, ps):
                                    if mi == 0 and b == 0:
                                        ctx.wait_future(t, "dwt", wm)
                                    last = None
                                    for k in range(32):
                                        last = t.matmul(ps[:, :],
                                                        wstr[:, k * 256 + mi * 128:k * 256 + (mi + 1) * 128],
                                                        sh[:, k * 1024 + b * 512:k * 1024 + (b + 1) * 512],
                                                        start=(k == 0), stop=(k == 31))
                                    return last
                                def ev(eng, ps):
                                    dst = ctx.xc(m_out)[:, half * 1024 + b * 512:half * 1024 + (b + 1) * 512]
                                    return eng.tensor_add(dst, ps[:, :], dst)
                                return {"mm": mm, "evac": ev, "evac_eng": "dve"}
                            jobs.append(mk(mi, m_out, b))
                    run_pipeline(ctx, blk, jobs, banks=[ctx.PS[0], ctx.PS[1], ctx.PS[2], ctx.PS[3]])
                ctx.bank_clear()
            if half == 0:
                with nc.Block() as blk:
                    @blk.sync
                    def _(s):
                        for cch in range(NCH):
                            ctx.dma(s, ctx.spill_x.ap()[cch * 128:(cch + 1) * 128, 0:1024],
                                    ctx.xc(cch)[:, 0:1024], sem="dio")
                        s.wait_ge(ctx.sems["dio"], ctx.cnt["dio"])
            else:
                with nc.Block() as blk:
                    @blk.sync
                    def _(s):
                        for cch in range(NCH):
                            ctx.dma(s, ctx.xc(cch)[:, 0:1024],
                                    ctx.spill_x.ap()[cch * 128:(cch + 1) * 128, 0:1024], sem="dio")
                        s.wait_ge(ctx.sems["dio"], ctx.cnt["dio"])


def emit_layer0(ctx):
    emit_outer_ln(ctx)
    ctx.tap("xn", ctx.xn_bf[:, :], [128, NCH * TOK], BF16)
    emit_spill_x(ctx)
    emit_share_acts(ctx, ctx.xnc)
    emit_l0_quantize(ctx)
    ctx.tap("wqfc", ctx.wq_fc.ap(), [D, 4096], BF16)
    emit_l0_attn(ctx)
    ctx.tap("xa", ctx.x_sb[:, :], [128, NCH * TOK], F32)
    emit_spill_x(ctx)
    emit_l0_ffn(ctx)


def emit_rows_bcast_pair(ctx, rows_src, segs, outs):
    nc = ctx.nc
    with nc.Block() as blk:
        jobs = []
        for seg, out_sb in zip(segs, outs):
            for b in range(NB):
                def mk(seg, out_sb, b):
                    def mm(t, ps):
                        return t.matmul(ps[:, :], ctx.csb["ones_row128"][:, :],
                                        rows_src[0:1, seg + b * 512:seg + (b + 1) * 512],
                                        start=True, stop=True)
                    def ev(a, ps):
                        return a.copy(out_sb[:, b * 512:(b + 1) * 512], ps[:, :])
                    return {"mm": mm, "evac": ev, "evac_eng": "act"}
                jobs.append(mk(seg, out_sb, b))
        run_pipeline(ctx, blk, jobs, banks=[ctx.PS[0], ctx.PS[1]])
    ctx.bank_clear()


def emit_ln_tail(ctx, blk, sum_banks, sq_banks, rwt, inv_n, eps, pe_done):
    @blk.vector
    def _(v):
        ctx.wait_future(v, "pe", pe_done)
        for b in range(4):
            ctx.inc(v.tensor_copy(rwt[0:1, b * 512:(b + 1) * 512], sum_banks[b][0:1, :]), "dve")
            ctx.inc(v.tensor_copy(rwt[0:1, TOK + b * 512:TOK + (b + 1) * 512],
                                  sq_banks[b][0:1, :]), "dve")
        ctx.wait(v, "dve")
        S0 = rwt[0:1, 0:TOK]; S1 = rwt[0:1, TOK:2 * TOK]; S2 = rwt[0:1, 2 * TOK:3 * TOK]
        row_chain(ctx, v, [
            lambda e: e.tensor_scalar_mul(S0, S0, -inv_n),
            lambda e: e.tensor_scalar(S1, S1, inv_n, eps, ALU.mult, ALU.add),
            lambda e: e.tensor_mul(S2, S0, S0),
            lambda e: e.tensor_sub(S1, S1, S2),
        ])
    @blk.scalar
    def _(a):
        ctx.wait(a, "dve")
        ctx.inc(a.activation(rwt[0:1, TOK:2 * TOK], rwt[0:1, TOK:2 * TOK], AF.Sqrt), "act")
    @blk.vector
    def _(v):
        ctx.wait(v, "act")
        S0 = rwt[0:1, 0:TOK]; S1 = rwt[0:1, TOK:2 * TOK]; S2 = rwt[0:1, 2 * TOK:3 * TOK]
        row_chain(ctx, v, [
            lambda e: e.reciprocal(S1, S1),
            lambda e: e.tensor_mul(S2, S0, S1),
        ])
    for b in range(4):
        ctx.bank_release(sum_banks[b], "dve", ctx.cnt["dve"])
        ctx.bank_release(sq_banks[b], "dve", ctx.cnt["dve"])


def emit_ln_rows_regen(ctx, blk, src_gen, ndve, ybuf, nch, eps, inv_n, sqb, rwt):
    ctx.sem("pe"); ctx.sem("act"); ctx.sem("dve")
    sum_banks = [ctx.PS[0], ctx.PS[1], ctx.PS[2], ctx.PS[3]]
    sq_banks = [ctx.PS[4], ctx.PRA, ctx.PRB, ctx.PTR]
    pe_base = ctx.cnt["pe"]
    act_base = ctx.cnt["act"]
    dve_base = ctx.cnt["dve"]

    @blk.vector
    def _(v):
        for c in range(nch):
            if c >= 1:
                ctx.wait_future(v, "pe", pe_base + c)
            src_gen(v, c)
    assert ctx.cnt["dve"] == dve_base + ndve * nch, "src_gen must emit ndve dve-incs per chunk"

    @blk.scalar
    def _(a):
        for c in range(nch):
            ctx.wait_future(a, "dve", dve_base + ndve * (c + 1))
            ctx.inc(a.activation(sqb[:, :], ybuf[:, :], AF.Square), "act")

    @blk.tensor
    def _(t):
        for b in range(4):
            ctx.bank_acquire(t, sum_banks[b])
            ctx.bank_acquire(t, sq_banks[b])
        for c in range(nch):
            ctx.wait_future(t, "act", act_base + c + 1)
            for b in range(4):
                t.matmul(sum_banks[b][0:1, :], ctx.csb["ones_col"][:, :],
                         ybuf[:, b * 512:(b + 1) * 512], start=(c == 0), stop=(c == nch - 1))
                last = t.matmul(sq_banks[b][0:1, :], ctx.csb["ones_col"][:, :],
                                sqb[:, b * 512:(b + 1) * 512], start=(c == 0), stop=(c == nch - 1))
            ctx.inc(last, "pe")
    emit_ln_tail(ctx, blk, sum_banks, sq_banks, rwt, inv_n, eps, ctx.cnt["pe"])


def emit_gating_fm(ctx, w_dram, nout, bias_row_sb, logt_sb, dram_rows=None):
    ctx.phase = "emit_gating_fm"
    nc = ctx.nc
    rows16 = ctx.l_rows
    with nc.sbuf_tensor(ctx.uname("gt_w"), [128, NCH * 16], F32) as wsb, \
         nc.sbuf_tensor(ctx.uname("gt_xf"), [128, TOK], F32) as xf, \
         nc.sbuf_tensor(ctx.uname("gt_bA"), [128, TOK], F32) as bA, \
         nc.sbuf_tensor(ctx.uname("gt_bB"), [128, TOK], F32) as bB, \
         nc.sbuf_tensor(ctx.uname("gt_st"), [128, 16 * 16], F32) as stage:
        emit_rows_bcast_pair(ctx, rows16, [0, TOK], [bA, bB])
        with nc.Block() as blk:
            @blk.sync
            def _(s):
                for kc in range(NCH):
                    ctx.dma(s, wsb[:, kc * 16:kc * 16 + nout],
                            w_dram.ap()[kc * 128:(kc + 1) * 128, 0:nout], sem="dio")
                s.wait_ge(ctx.sems["dio"], ctx.cnt["dio"])
        ctx.bank_clear()
        with nc.Block() as blk:
            banks = [ctx.PS[0], ctx.PS[1], ctx.PS[2], ctx.PS[3]]
            ctx.sem("pe"); ctx.sem("dve")
            pe_base = ctx.cnt["pe"]
            @blk.vector
            def _(v):
                for c in range(NCH):
                    if c >= 1:
                        ctx.wait_future(v, "pe", pe_base + c)
                    m = ctx.inc(v.tensor_mul(xf[:, :], ctx.xc(c), bA[:, :]), "dve")
                    ctx.wait(v, "dve", m)
                    m = ctx.inc(v.tensor_add(xf[:, :], xf[:, :], bB[:, :]), "dve")
                    ctx.wait(v, "dve", m)
            dve_b = ctx.cnt["dve"] - 2 * NCH
            @blk.tensor
            def _(t):
                for b in range(NB):
                    ctx.bank_acquire(t, banks[b])
                for c in range(NCH):
                    ctx.wait_future(t, "dve", dve_b + 2 * (c + 1))
                    for b in range(NB):
                        last = t.matmul(banks[b][0:nout, :], wsb[:, c * 16:c * 16 + nout],
                                        xf[:, b * 512:(b + 1) * 512], start=(c == 0), stop=False)
                    ctx.inc(last, "pe")
                for b in range(NB):
                    last = t.matmul(banks[b][0:nout, :], bias_row_sb[0:1, 0:nout],
                                    ctx.csb["ones_row512"][:, :], start=False, stop=True)
                ctx.inc(last, "pe")
            pe_done = ctx.cnt["pe"]
            @blk.scalar
            def _(a):
                ctx.wait_future(a, "pe", pe_done)
                for b in range(NB):
                    ctx.inc(a.copy(logt_sb[0:nout, b * 512:(b + 1) * 512], banks[b][0:nout, :]), "act")
            for b in range(NB):
                ctx.bank_release(banks[b], "act", ctx.cnt["act"])
        ctx.bank_clear()
        if dram_rows is not None:
            with nc.Block() as blk:
                @blk.gpsimd
                def _(g):
                    ctx.inc(g.memset(stage[:, :], 0.0), "pool")
                pool_m = ctx.cnt["pool"]
                ctx.sem("act")
                act_b = ctx.cnt["act"]
                pe_b = ctx.cnt["pe"]
                @blk.tensor
                def _(t):
                    ctx.wait_future(t, "pool", pool_m)
                    ctx.bank_acquire(t, ctx.PTR)
                    for tt in range(NT):
                        if tt >= 1:
                            ctx.wait_future(t, "act", act_b + tt)
                        ctx.inc(t.transpose(ctx.PTR[:, 0:nout],
                                            logt_sb[0:nout, tt * 128:(tt + 1) * 128],
                                            ctx.csb["ident_f32"][0:nout, 0:nout]), "pe")
                @blk.scalar
                def _(a):
                    for tt in range(NT):
                        ctx.wait_future(a, "pe", pe_b + tt + 1)
                        ctx.inc(a.copy(stage[:, tt * 16:tt * 16 + nout], ctx.PTR[:, 0:nout]), "act")
                ctx.bank_release(ctx.PTR, "act", ctx.cnt["act"])
                @blk.sync
                def _(s):
                    ctx.wait(s, "act")
                    ctx.dma(s, dram_rows.ap()[0:TOK, :].rearrange("(tt p) g -> p tt g", p=128),
                            stage[:, :].rearrange("p (tt g) -> p tt g", g=16), sem="dio")
                    s.wait_ge(ctx.sems["dio"], ctx.cnt["dio"])
            ctx.bank_clear()


def emit_xn_tm(ctx, dram_dst, src_bf=None):
    ctx.phase = "emit_xn_tm"
    nc = ctx.nc
    if src_bf is None:
        src_bf = ctx.xn_bf
    with nc.sbuf_tensor(ctx.uname("xtm_st"), [128, 2 * D], BF16) as stage, nc.Block() as blk:
        ctx.sem("dtp"); ctx.sem("dst2")
        dtp_base = ctx.cnt["dtp"]
        dst_base = ctx.cnt["dst2"]
        @blk.sync
        def _(s):
            for tt in range(NT):
                if tt >= 2:
                    ctx.wait_future(s, "dst2", dst_base + 16 * (tt - 1))
                for c in range(NCH):
                    ctx.dma(s, stage[:, (tt % 2) * D + c * 128:(tt % 2) * D + (c + 1) * 128],
                            src_bf[:, c * TOK + tt * 128:c * TOK + (tt + 1) * 128],
                            sem="dtp", transpose=True)
        @blk.gpsimd
        def _(g):
            for tt in range(NT):
                ctx.wait_future(g, "dtp", dtp_base + 16 * NCH * (tt + 1))
                ctx.dma(g, dram_dst.ap()[tt * 128:(tt + 1) * 128, :],
                        stage[:, (tt % 2) * D:(tt % 2) * D + D], sem="dst2")
            g.wait_ge(ctx.sems["dst2"], ctx.cnt["dst2"])
    ctx.bank_clear()


def emit_tm_proj(ctx, src_fn, wbuf, w_m, dst_fn, bias_row_sb=None, act_fn=None):
    nc = ctx.nc
    with nc.Block() as blk:
        jobs = []
        for tt in range(NT):
            for vb in range(2):
                def mk(tt, vb):
                    def mm(t, ps):
                        if tt == 0 and vb == 0 and w_m is not None:
                            ctx.wait_future(t, "dwt", w_m)
                        last = None
                        for kc in range(NCH):
                            last = t.matmul(ps[:, :], src_fn(kc)[:, tt * 128:(tt + 1) * 128],
                                            wbuf[:, kc * 1024 + vb * 512:kc * 1024 + (vb + 1) * 512],
                                            start=(kc == 0), stop=(bias_row_sb is None and kc == NCH - 1))
                        if bias_row_sb is not None:
                            last = t.matmul(ps[:, :], ctx.csb["ones_row128"][:, :],
                                            bias_row_sb[0:1, vb * 512:(vb + 1) * 512],
                                            start=False, stop=True)
                        return last
                    def ev(a, ps):
                        dst = dst_fn(tt, vb)
                        if act_fn is not None:
                            return a.activation(dst, ps[:, :], act_fn)
                        return a.copy(dst, ps[:, :])
                    return {"mm": mm, "evac": ev, "evac_eng": "act"}
                jobs.append(mk(tt, vb))
        run_pipeline(ctx, blk, jobs, banks=[ctx.PS[0], ctx.PS[1], ctx.PS[2], ctx.PS[3]])
    ctx.bank_clear()


def emit_col16_to_row(ctx, col_sb, rw_ap, tmp16):
    nc = ctx.nc
    with nc.Block() as blk:
        @blk.tensor
        def _(t):
            ctx.bank_acquire(t, ctx.PTR)
            ctx.inc(t.transpose(ctx.PTR[0:16, 0:128], col_sb[:, 0:16], ctx.csb["ident_f32"][:, :]), "pe")
        @blk.vector
        def _(v):
            ctx.wait(v, "pe")
            ctx.inc(v.tensor_copy(tmp16[0:16, :], ctx.PTR[0:16, 0:128]), "dve")
        ctx.bank_release(ctx.PTR, "dve", ctx.cnt["dve"])
        @blk.sync
        def _(s):
            ctx.wait(s, "dve")
            ctx.dma(s, rw_ap.rearrange("o (tt p) -> o tt p", p=128),
                    tmp16[0:16, :], sem="dio")
            s.wait_ge(ctx.sems["dio"], ctx.cnt["dio"])
    ctx.bank_clear()


def u32_index(ctx, g, dst_u32, src_f32):
    m = ctx.inc(g.tensor_copy(dst_u32, src_f32), "pool")
    ctx.wait(g, "pool", m)


def emit_clamp_idx(ctx, v, idxf, scr_a, scr_b, n_valid):
    n = float(n_valid)
    row_chain(ctx, v, [
        lambda e: e.tensor_scalar(scr_a, idxf, 0.0, None, ALU.is_ge),
        lambda e: e.tensor_scalar(scr_b, idxf, n - 1.0, None, ALU.is_le),
        lambda e: e.tensor_mul(scr_a, scr_a, scr_b),
        lambda e: e.tensor_scalar_add(idxf, idxf, -n),
        lambda e: e.tensor_mul(idxf, idxf, scr_a),
        lambda e: e.tensor_scalar_add(idxf, idxf, n),
    ])


def emit_layer1(ctx):
    ctx.phase = "emit_layer1"
    nc = ctx.nc
    with nc.sbuf_tensor(ctx.uname("l1_brow"), [1, D], F32) as brow:
        with nc.sbuf_tensor(ctx.uname("l1_rows"), [1, 2 * TOK], F32) as rows16, \
             nc.sbuf_tensor(ctx.uname("l1_logt"), [16, TOK], F32) as logt:
            ctx.l_rows = rows16
            emit_outer_ln(ctx, rows_out=rows16)
            with nc.Block() as blk:
                @blk.sync
                def _(s):
                    ctx.dma(s, brow[0:1, 0:16], ctx.DI["l1_gb_r"].ap(), sem="dio")
                    s.wait_ge(ctx.sems["dio"], ctx.cnt["dio"])
            emit_gating_fm(ctx, ctx.DI["l1_gwT"], 9, brow, logt, dram_rows=ctx.l1_glog_my)
            emit_xn_tm(ctx, ctx.l1_xn_tm)
        emit_spill_x(ctx)

        ktm = ctx.x_sb[:, :].bitcast(BF16)
        def ktile(tt):
            return ktm[:, tt * 1024:(tt + 1) * 1024]
        def vtile(tt):
            return ktm[:, 16384 + tt * 1024:16384 + (tt + 1) * 1024]
        apre_holder = []
        with nc.sbuf_tensor(ctx.uname("l1_apre"), [128, NCH * TOK], BF16) as apre:
            apre_holder.append(apre)
            with nc.sbuf_tensor(ctx.uname("l1_q"), [128, NCH * TOK], BF16) as q_sb:
                with nc.sbuf_tensor(ctx.uname("l1_wb"), [128, NCH * 1024], BF16) as wbuf:
                    with nc.Block() as blk:
                        @blk.gpsimd
                        def _(g):
                            load_w_tiles(ctx, g, wbuf, ctx.DI["l1_WqT"].ap(), 0, 0, NCH, 1024)
                        wm = ctx.cnt["dwt"]
                        @blk.sync
                        def _(s):
                            ctx.dma(s, brow[0:1, :], ctx.DI["l1_bq_r"].ap(), sem="dio")
                            s.wait_ge(ctx.sems["dio"], ctx.cnt["dio"])
                        jobs = []
                        for m in range(NCH):
                            for b in range(NB):
                                def mk(m, b):
                                    def mm(t, ps):
                                        if m == 0 and b == 0:
                                            ctx.wait_future(t, "dwt", wm)
                                        for k in range(NCH):
                                            t.matmul(ps[:, :],
                                                     wbuf[:, k * 1024 + m * 128:k * 1024 + (m + 1) * 128],
                                                     ctx.xnc(k)[:, b * 512:(b + 1) * 512],
                                                     start=(k == 0), stop=False)
                                        return t.matmul(ps[:, :], brow[0:1, m * 128:(m + 1) * 128],
                                                        ctx.csb["ones_row512"][:, :], start=False, stop=True)
                                    def evr(a, ps):
                                        return a.activation(q_sb[:, m * TOK + b * 512:m * TOK + (b + 1) * 512],
                                                            ps[:, :], AF.Relu)
                                    return {"mm": mm, "evac": evr, "evac_eng": "act"}
                                jobs.append(mk(m, b))
                        run_pipeline(ctx, blk, jobs, banks=[ctx.PS[0], ctx.PS[1], ctx.PS[2], ctx.PS[3]])
                    ctx.bank_clear()
                    with nc.Block() as blk:
                        @blk.gpsimd
                        def _(g):
                            load_w_tiles(ctx, g, wbuf, ctx.DI["l1_WkT"].ap(), 0, 0, NCH, 1024)
                        wmk = ctx.cnt["dwt"]
                        @blk.sync
                        def _(s):
                            ctx.dma(s, brow[0:1, :], ctx.DI["l1_bk_r"].ap(), sem="dio")
                            s.wait_ge(ctx.sems["dio"], ctx.cnt["dio"])
                    ctx.bank_clear()
                    emit_tm_proj(ctx, ctx.xnc, wbuf, wmk,
                                 lambda tt, vb: ktile(tt)[:, vb * 512:(vb + 1) * 512],
                                 bias_row_sb=brow, act_fn=AF.Relu)
                    with nc.Block() as blk:
                        @blk.gpsimd
                        def _(g):
                            load_w_tiles(ctx, g, wbuf, ctx.DI["l1_WvT"].ap(), 0, 0, NCH, 1024)
                        wmv = ctx.cnt["dwt"]
                        @blk.sync
                        def _(s):
                            ctx.dma(s, brow[0:1, :], ctx.DI["l1_bv_r"].ap(), sem="dio")
                            s.wait_ge(ctx.sems["dio"], ctx.cnt["dio"])
                    ctx.bank_clear()
                    emit_tm_proj(ctx, ctx.xnc, wbuf, wmv,
                                 lambda tt, vb: vtile(tt)[:, vb * 512:(vb + 1) * 512],
                                 bias_row_sb=brow, act_fn=None)
                with nc.sbuf_tensor(ctx.uname("l1_kv"), [64, 16 * 65], F32) as kvsb, \
                     nc.sbuf_tensor(ctx.uname("l1_kvp"), [64, 16 * 65], F32) as kvp, \
                     nc.sbuf_tensor(ctx.uname("l1_kvb"), [128, 8 * 65], BF16) as kvbf, \
                     nc.sbuf_tensor(ctx.uname("l1_binv"), [128, 512], F32) as binv, \
                     nc.sbuf_tensor(ctx.uname("l1_arw"), [1, 512], F32) as arw, \
                     nc.sbuf_tensor(ctx.uname("l1_shi"), [128, 2], U32) as shidx, \
                     nc.sbuf_tensor(ctx.uname("l1_shf"), [128, 2], F32) as shidx_f:
                    with nc.Block() as blk:
                        banks = [ctx.PS[4], ctx.PRA, ctx.PRB, ctx.PTR]
                        @blk.tensor
                        def _(t):
                            for bi in range(4):
                                ctx.bank_acquire(t, banks[bi])
                            for h in range(HEADS):
                                bank = banks[h // 4]
                                cb = (h % 4) * 65
                                for tt in range(NT):
                                    t.matmul(bank[0:64, cb:cb + 64],
                                             ktile(tt)[:, h * 64:(h + 1) * 64],
                                             vtile(tt)[:, h * 64:(h + 1) * 64],
                                             start=(h % 4 == 0 and tt == 0), stop=False)
                                for tt in range(NT):
                                    last = t.matmul(bank[0:64, cb + 64:cb + 65],
                                                    ktile(tt)[:, h * 64:(h + 1) * 64],
                                                    ctx.csb["ones_col_bf"][:, :],
                                                    start=False, stop=(h % 4 == 3 and tt == NT - 1))
                                if h % 4 == 3:
                                    ctx.inc(last, "pe")
                        pe_done = ctx.cnt["pe"]
                        @blk.scalar
                        def _(a):
                            ctx.wait_future(a, "pe", pe_done)
                            for bi in range(4):
                                ctx.inc(a.copy(kvsb[:, bi * 260:(bi + 1) * 260], banks[bi][0:64, 0:260]), "act")
                        for bi in range(4):
                            ctx.bank_release(banks[bi], "act", ctx.cnt["act"])
                    ctx.bank_clear()
                    shkv = ctx.sh_kv.ap().rearrange("s (p q) -> (s p) q", p=64)[:, 0:1040]
                    with nc.Block() as blk:
                        @blk.vector
                        def _(v):
                            m1 = ctx.inc(v.tensor_scalar(shidx_f[:, 0:1], ctx.cinfo[:, 2:3],
                                                         1.0 / 32.0, None, ALU.mult), "dve")
                            m1 = ctx.inc(v.tensor_scalar(shidx_f[:, 1:2], ctx.cinfo[:, 2:3],
                                                         -1.0 / 32.0, 64.0, ALU.mult, ALU.add), "dve")
                            ctx.wait(v, "dve", m1)
                            m2 = ctx.inc(v.tensor_add(shidx_f[:, 0:1], shidx_f[:, 0:1],
                                                      ctx.csb["iota128"][:, :]), "dve")
                            m2 = ctx.inc(v.tensor_add(shidx_f[:, 1:2], shidx_f[:, 1:2],
                                                      ctx.csb["iota128"][:, :]), "dve")
                            ctx.wait(v, "dve", m2)
                        @blk.gpsimd
                        def _(g):
                            ctx.wait(g, "dve")
                            ctx.wait(g, "act")
                            u32_index(ctx, g, shidx[:, :], shidx_f[:, :])
                            ctx.inc(g.indirect_dma_start(
                                shkv, IndirectOffsetOnAxis(shidx[0:64, 0:1], axis=0),
                                kvsb[:, :], None), "dsh", 16)
                            g.wait_ge(ctx.sems["dsh"], ctx.cnt["dsh"])
                            emit_pair_barrier(ctx, g)
                            ctx.inc(g.indirect_dma_start(
                                kvp[:, :], None, shkv,
                                IndirectOffsetOnAxis(shidx[0:64, 1:2], axis=0)), "dsh", 16)
                            g.wait_ge(ctx.sems["dsh"], ctx.cnt["dsh"])
                        @blk.vector
                        def _(v):
                            ctx.wait(v, "dsh")
                            m1 = ctx.inc(v.tensor_add(kvsb[:, :], kvsb[:, :], kvp[:, :]), "dve")
                            ctx.wait(v, "dve", m1)
                            src = kvsb[:, :].rearrange("p (h q) -> p h q", q=65)
                            dste = kvbf[0:64, :].rearrange("p (h q) -> p h q", q=65)
                            dsto = kvbf[64:128, :].rearrange("p (h q) -> p h q", q=65)
                            m2 = ctx.inc(v.tensor_copy(dste, src[:, 0:16:2, :]), "dve")
                            m2 = ctx.inc(v.tensor_copy(dsto, src[:, 1:16:2, :]), "dve")
                            ctx.wait(v, "dve", m2)
                    ctx.bank_clear()
                    for h in range(HEADS):
                        hc, rb = h // 2, (h % 2) * 64
                        with nc.Block() as blk:
                            for nb_i in range(NB):
                                @blk.tensor
                                def _(t, h=h, nb_i=nb_i, hc=hc, rb=rb):
                                    ctx.bank_acquire(t, ctx.PS[4])
                                    ctx.inc(t.matmul(
                                        ctx.PS[4][0:65, :],
                                        kvbf[rb:rb + 64, hc * 65:(hc + 1) * 65],
                                        q_sb[rb:rb + 64, hc * TOK + nb_i * 512:hc * TOK + (nb_i + 1) * 512],
                                        start=True, stop=True), "pe")
                                @blk.vector
                                def _(v):
                                    ctx.wait(v, "pe")
                                    m1 = ctx.inc(v.tensor_scalar_max(arw[0:1, :], ctx.PS[4][64:65, :], 1e-6), "dve")
                                    ctx.wait(v, "dve", m1)
                                    m2 = ctx.inc(v.reciprocal(arw[0:1, :], arw[0:1, :]), "dve")
                                    ctx.wait(v, "dve", m2)
                                @blk.tensor
                                def _(t):
                                    ctx.wait(t, "dve")
                                    ctx.bank_acquire(t, ctx.PRA)
                                    ctx.inc(t.matmul(ctx.PRA[:, :], ctx.csb["ones_row128"][:, :],
                                                     arw[0:1, :], start=True, stop=True), "pe")
                                @blk.scalar
                                def _(a):
                                    ctx.wait(a, "pe")
                                    ctx.inc(a.copy(binv[:, :], ctx.PRA[:, :]), "act")
                                ctx.bank_release(ctx.PRA, "act", ctx.cnt["act"])
                                @blk.vector
                                def _(v, h=h, nb_i=nb_i, hc=hc, rb=rb):
                                    ctx.wait(v, "act")
                                    dst = apre[rb:rb + 64, hc * TOK + nb_i * 512:hc * TOK + (nb_i + 1) * 512]
                                    ctx.inc(v.tensor_mul(dst, ctx.PS[4][0:64, :], binv[0:64, :]), "dve")
                                ctx.bank_release(ctx.PS[4], "dve", ctx.cnt["dve"])
                        ctx.bank_clear()
            with nc.sbuf_tensor(ctx.uname("l1_bA"), [128, TOK], F32) as bA, \
                 nc.sbuf_tensor(ctx.uname("l1_bB"), [128, TOK], F32) as bB, \
                 nc.sbuf_tensor(ctx.uname("l1_y"), [128, TOK], F32) as ybuf, \
                 nc.sbuf_tensor(ctx.uname("l1_sq"), [128, TOK], F32) as sqb, \
                 nc.sbuf_tensor(ctx.uname("l1_rw2"), [1, 3 * TOK], F32) as rwt, \
                 nc.sbuf_tensor(ctx.uname("l1_gbc"), [128, 16], F32) as gbc:
                ctx.rw = rwt
                with nc.Block() as blk:
                    @blk.sync
                    def _(s):
                        ctx.dma(s, gbc[:, 0:8], ctx.DI["l1_gamma_c"].ap(), sem="dio")
                        ctx.dma(s, gbc[:, 8:16], ctx.DI["l1_beta_c"].ap(), sem="dio")
                        s.wait_ge(ctx.sems["dio"], ctx.cnt["dio"])
                with nc.Block() as blk:
                    def ygen(v, c):
                        m = ctx.inc(v.tensor_add(ybuf[:, :], apre[:, c * TOK:(c + 1) * TOK],
                                                 ctx.xnc(c)), "dve")
                        ctx.wait(v, "dve", m)
                    emit_ln_rows_regen(ctx, blk, ygen, 1, ybuf, NCH, 1e-5, 1.0 / D, sqb, rwt)
                ctx.bank_clear()
                emit_rows_bcast_pair(ctx, rwt, [TOK, 2 * TOK], [bA, bB])
                with nc.Block() as blk:
                    @blk.vector
                    def _(v):
                        for c in range(NCH):
                            m = ctx.inc(v.tensor_add(ybuf[:, :], apre[:, c * TOK:(c + 1) * TOK],
                                                     ctx.xnc(c)), "dve")
                            ctx.wait(v, "dve", m)
                            m = ctx.inc(v.tensor_mul(ybuf[:, :], ybuf[:, :], bA[:, :]), "dve")
                            ctx.wait(v, "dve", m)
                            m = ctx.inc(v.tensor_add(ybuf[:, :], ybuf[:, :], bB[:, :]), "dve")
                            ctx.wait(v, "dve", m)
                            ctx.inc(v.tensor_scalar(apre[:, c * TOK:(c + 1) * TOK], ybuf[:, :],
                                                    gbc[:, c:c + 1], gbc[:, 8 + c:9 + c],
                                                    ALU.mult, ALU.add), "dve")
                            ctx.wait(v, "dve")
                ctx.bank_clear()
            emit_l1_moe(ctx, brow)
            with nc.Block() as blk:
                @blk.vector
                def _(v):
                    for c in range(NCH):
                        dst = ctx.xc(c)
                        m = ctx.inc(v.tensor_add(dst, dst, apre[:, c * TOK:(c + 1) * TOK]), "dve")
                        ctx.wait(v, "dve", m)
            ctx.bank_clear()


def emit_glu512_ln(ctx, src_fn, w1d, b1d, w2d, b2d, gc_sb, gcol, tok0, ntok,
                   zf, zbf_slice, sqb, t1scr, bAB, rwt, wbuf, brow):
    nc = ctx.nc
    nbl = ntok // 512
    for (wd, bd, second) in [(w1d, b1d, False), (w2d, b2d, True)]:
        with nc.Block() as blk:
            @blk.gpsimd
            def _(g):
                load_w_tiles(ctx, g, wbuf, wd.ap(), 0, 0, NCH, 512)
            wm = ctx.cnt["dwt"]
            @blk.sync
            def _(s):
                ctx.dma(s, brow[0:1, 0:512], bd.ap(), sem="dio")
                s.wait_ge(ctx.sems["dio"], ctx.cnt["dio"])
            jobs = []
            for m in range(4):
                for b in range(nbl):
                    def mk(m, b, second):
                        def mm(t, ps):
                            if m == 0 and b == 0:
                                ctx.wait_future(t, "dwt", wm)
                            for k in range(NCH):
                                t.matmul(ps[:, :], wbuf[:, k * 512 + m * 128:k * 512 + (m + 1) * 128],
                                         src_fn(k)[:, tok0 + b * 512:tok0 + (b + 1) * 512],
                                         start=(k == 0), stop=False)
                            return t.matmul(ps[:, :], brow[0:1, m * 128:(m + 1) * 128],
                                            ctx.csb["ones_row512"][:, :], start=False, stop=True)
                        if not second:
                            def ev(a, ps):
                                return a.activation(zf[:, m * ntok + b * 512:m * ntok + (b + 1) * 512],
                                                    ps[:, :], AF.Silu)
                            return {"mm": mm, "evac": ev, "evac_eng": "act"}
                        def ev2(e, ps):
                            dst = zf[:, m * ntok + b * 512:m * ntok + (b + 1) * 512]
                            return e.tensor_mul(dst, ps[:, :], dst)
                        return {"mm": mm, "evac": ev2, "evac_eng": "dve"}
                    jobs.append(mk(m, b, second))
            run_pipeline(ctx, blk, jobs, banks=[ctx.PS[0], ctx.PS[1], ctx.PS[2], ctx.PS[3]])
        ctx.bank_clear()
    with nc.Block() as blk:
        def zsrc(c):
            return zf[:, c * ntok:(c + 1) * ntok]
        emit_ln_rows(ctx, blk, zsrc, 4, ntok, 1e-5, 1.0 / 512.0, [sqb, t1scr], 0)
    ctx.bank_clear()
    emit_rows_bcast_pair_n(ctx, rwt, [ntok, 2 * ntok], [bAB[:, 0:ntok], bAB[:, ntok:2 * ntok]], ntok)
    with nc.Block() as blk:
        @blk.vector
        def _(v):
            for c in range(4):
                m = ctx.inc(v.tensor_mul(sqb[:, 0:ntok], zf[:, c * ntok:(c + 1) * ntok],
                                         bAB[:, 0:ntok]), "dve")
                ctx.wait(v, "dve", m)
                m = ctx.inc(v.tensor_add(sqb[:, 0:ntok], sqb[:, 0:ntok], bAB[:, ntok:2 * ntok]), "dve")
                ctx.wait(v, "dve", m)
                ctx.inc(v.tensor_scalar(zbf_slice(c), sqb[:, 0:ntok],
                                        gc_sb[:, gcol + c:gcol + c + 1],
                                        gc_sb[:, gcol + 4 + c:gcol + 5 + c],
                                        ALU.mult, ALU.add), "dve")
                ctx.wait(v, "dve")
    ctx.bank_clear()


def emit_rows_bcast_pair_n(ctx, rows_src, segs, outs, ntok):
    nc = ctx.nc
    with nc.Block() as blk:
        jobs = []
        for seg, out_sb in zip(segs, outs):
            for b in range(ntok // 512):
                def mk(seg, out_sb, b):
                    def mm(t, ps):
                        return t.matmul(ps[:, :], ctx.csb["ones_row128"][:, :],
                                        rows_src[0:1, seg + b * 512:seg + (b + 1) * 512],
                                        start=True, stop=True)
                    def ev(a, ps):
                        return a.copy(out_sb[:, b * 512:(b + 1) * 512], ps[:, :])
                    return {"mm": mm, "evac": ev, "evac_eng": "act"}
                jobs.append(mk(seg, out_sb, b))
        run_pipeline(ctx, blk, jobs, banks=[ctx.PS[0], ctx.PS[1]])
    ctx.bank_clear()


def fm_jobs(ctx, lhs_fn, rhs_fn, nk, nm, nbl, bias_fn, evac_fn, wm, evac_eng="dve"):
    jobs = []
    for m in range(nm):
        for b in range(nbl):
            def mk(m, b):
                def mm(t, ps):
                    if m == 0 and b == 0 and wm is not None:
                        ctx.wait_future(t, "dwt", wm)
                    last = None
                    for k in range(nk):
                        last = t.matmul(ps[:, :], lhs_fn(k, m), rhs_fn(k, b),
                                        start=(k == 0), stop=(bias_fn is None and k == nk - 1))
                    if bias_fn is None:
                        return last
                    br, orow = bias_fn(m, b)
                    return t.matmul(ps[:, :], br, orow, start=False, stop=True)
                def mm2(t, ps):
                    last = mm(t, ps)
                    return last
                def ev(e, ps):
                    return evac_fn(m, b, e, ps)
                return {"mm": mm2, "evac": ev, "evac_eng": evac_eng}
            jobs.append(mk(m, b))
    return jobs


def emit_l1_moe(ctx, brow):
    ctx.phase = "emit_l1_moe"
    nc = ctx.nc
    print("[sbuf] l1_moe entry bytes_remaining:", nc.sbuf_bytes_remaining)
    with nc.sbuf_tensor(ctx.uname("m_idx"), [128, NT], U32) as idxs, \
         nc.sbuf_tensor(ctx.uname("m_idf"), [128, NT], F32) as idxf, \
         nc.sbuf_tensor(ctx.uname("m_st"), [128, D], BF16) as stage_bf, \
         nc.sbuf_tensor(ctx.uname("m_stf"), [128, 32], F32) as stage_g:
        with nc.Block() as blk:
            @blk.sync
            def _(s):
                ctx.dma(s, idxf[:, :], ctx.DI["l1_flat"].ap(), sem="dio")
                s.wait_ge(ctx.sems["dio"], ctx.cnt["dio"])
            @blk.vector
            def _(v):
                ctx.wait(v, "dio")
                m = ctx.inc(v.tensor_scalar(idxf[:, :], idxf[:, :], ctx.cinfo[:, 0:1], None,
                                            ALU.subtract), "dve")
                ctx.wait(v, "dve", m)
                emit_clamp_idx(ctx, v, idxf[:, :], stage_g[:, 0:NT], stage_g[:, NT:2 * NT], TOK)
            @blk.gpsimd
            def _(g):
                ctx.inc(g.memset(stage_bf[:, :], 0.0), "pool")
                g.wait_ge(ctx.sems["pool"], ctx.cnt["pool"])
                ctx.inc(g.dma_start(out=ctx.l1_xn_tm.ap()[TOK:TOK + 1, :],
                                    in_=stage_bf[0:1, :]), "dst2", 16)
                ctx.inc(g.dma_start(out=ctx.l1_glog_my.ap()[TOK:TOK + 1, :],
                                    in_=stage_bf[0:1, 0:32].bitcast(F32)), "dst2", 16)
                ctx.inc(g.dma_start(out=ctx.l1_pack.ap()[TOK:TOK + 1, :],
                                    in_=stage_bf[0:1, 0:16].bitcast(F32)), "dst2", 16)
                ctx.inc(g.dma_start(out=ctx.l1_contrib.ap()[TOK:TOK + 1, 0:512],
                                    in_=stage_bf[0:1, 0:1024].bitcast(F32)), "dst2", 16)
                ctx.inc(g.dma_start(out=ctx.l1_contrib.ap()[TOK:TOK + 1, 512:1024],
                                    in_=stage_bf[0:1, 0:1024].bitcast(F32)), "dst2", 16)
                g.wait_ge(ctx.sems["dst2"], ctx.cnt["dst2"])
        ctx.bank_clear()
        with nc.Block() as blk:
            @blk.gpsimd
            def _(g):
                ctx.wait(g, "dve")
                u32_index(ctx, g, idxs[:, :], idxf[:, :])
                for tt in range(NT):
                    ctx.inc(g.indirect_dma_start(
                        stage_bf[:, :], None, ctx.l1_xn_tm.ap(),
                        IndirectOffsetOnAxis(idxs[:, tt:tt + 1], axis=0)), "dsh", 16)
                    ctx.inc(g.indirect_dma_start(
                        stage_g[:, 0:16], None, ctx.l1_glog_my.ap(),
                        IndirectOffsetOnAxis(idxs[:, tt:tt + 1], axis=0)), "dsh", 16)
                    g.wait_ge(ctx.sems["dsh"], ctx.cnt["dsh"])
                    ctx.inc(g.dma_start(out=ctx.l1_xmoe_my.ap()[tt * 128:(tt + 1) * 128, :],
                                        in_=stage_bf[:, :]), "dst2", 16)
                    ctx.inc(g.dma_start(out=ctx.l1_glog_all.ap()[tt * 128:(tt + 1) * 128, :],
                                        in_=stage_g[:, 0:16]), "dst2", 16)
                    g.wait_ge(ctx.sems["dst2"], ctx.cnt["dst2"])
                ctx.inc(g.collective_compute(
                    "AllReduce", ALU.add, replica_groups=[[0, 1, 2, 3, 4, 5, 6, 7]],
                    ins=[ctx.l1_xmoe_my.ap()], outs=[ctx.l1_xmoe_all.ap()]), "cc", 1)
                ctx.inc(g.collective_compute(
                    "AllReduce", ALU.add, replica_groups=[[0, 1, 2, 3, 4, 5, 6, 7]],
                    ins=[ctx.l1_glog_all.ap()], outs=[ctx.l1_glog_all.ap()]), "cc", 1)
                g.wait_ge(ctx.sems["cc"], ctx.cnt["cc"])
        ctx.bank_clear()
    bmy_cm = nc.sbuf_tensor(ctx.uname("m_bmy"), [128, TOK], F32)
    if True:
        bmyw = bmy_cm.__enter__()
        with nc.sbuf_tensor(ctx.uname("m_gl"), [128, NT * 16], F32) as gl, \
             nc.sbuf_tensor(ctx.uname("m_b8"), [128, 24], F32) as b8, \
             nc.sbuf_tensor(ctx.uname("m_sc"), [128, 8], F32) as scw, \
             nc.sbuf_tensor(ctx.uname("m_myw"), [128, NT], F32) as myw, \
             nc.sbuf_tensor(ctx.uname("m_shw"), [128, NT], F32) as shw, \
             nc.sbuf_tensor(ctx.uname("m_t16"), [16, 128], F32) as tmp16, \
             nc.sbuf_tensor(ctx.uname("m_rw"), [1, 3 * TOK], F32) as rwt:
            ctx.rw = rwt
            with nc.Block() as blk:
                @blk.sync
                def _(s):
                    ctx.dma(s, gl[:, :].rearrange("p (tt g) -> p tt g", g=16),
                            ctx.l1_glog_all.ap().rearrange("(tt p) g -> p tt g", p=128), sem="dio")
                    ctx.dma(s, b8[0:1, 0:16], ctx.DI["l1_expb_r"].ap(), sem="dio")
                    s.wait_ge(ctx.sems["dio"], ctx.cnt["dio"])
                @blk.tensor
                def _(t):
                    ctx.wait(t, "dio")
                    ctx.bank_acquire(t, ctx.PRA)
                    t.matmul(ctx.PRA[:, 0:8], ctx.csb["ones_row128"][:, :], b8[0:1, 0:8],
                             start=True, stop=False)
                    ctx.inc(t.matmul(ctx.PRA[:, 8:16], ctx.csb["ones_row128"][:, :],
                                     ctx.csb["iota8"][0:1, :], start=False, stop=True), "pe")
                @blk.scalar
                def _(a):
                    ctx.wait(a, "pe")
                    ctx.inc(a.copy(b8[:, 0:16], ctx.PRA[:, 0:16]), "act")
                ctx.bank_release(ctx.PRA, "act", ctx.cnt["act"])
                @blk.vector
                def _(v):
                    ctx.wait(v, "act")
                    m = ctx.inc(v.tensor_scalar_add(b8[:, 16:24], b8[:, 8:16], -8.0), "dve")
                    ctx.wait(v, "dve", m)
            ctx.bank_clear()
            with nc.Block() as blk:
                @blk.scalar
                def _(a):
                    for tt in range(NT):
                        ctx.inc(a.activation(gl[:, tt * 16:tt * 16 + 9], gl[:, tt * 16:tt * 16 + 9],
                                             AF.Sigmoid), "act")
                sig_m = ctx.cnt["act"]
                @blk.vector
                def _(v):
                    ctx.wait(v, "act", sig_m)
                    for tt in range(NT):
                        g0 = gl[:, tt * 16:tt * 16 + 16]
                        eg = gl[:, tt * 16 + 1:tt * 16 + 9]
                        row_chain(ctx, v, [
                            lambda e, eg=eg: e.tensor_add(scw[:, 0:8], eg, b8[:, 0:8]),
                            lambda e, tt=tt: e.reduce_max(myw[:, tt:tt + 1], scw[:, 0:8], axis=AX.X),
                            lambda e, tt=tt: e.tensor_scalar(scw[:, 0:8], scw[:, 0:8],
                                                             myw[:, tt:tt + 1], None, ALU.is_ge),
                            lambda e: e.tensor_mul(scw[:, 0:8], scw[:, 0:8], b8[:, 16:24]),
                            lambda e: e.tensor_scalar_add(scw[:, 0:8], scw[:, 0:8], 8.0),
                            lambda e, tt=tt: e.tensor_reduce(myw[:, tt:tt + 1], scw[:, 0:8],
                                                             axis=AX.X, op=ALU.min),
                            lambda e, tt=tt: e.tensor_scalar(scw[:, 0:8], b8[:, 8:16],
                                                             myw[:, tt:tt + 1], None, ALU.is_equal),
                            lambda e, eg=eg: e.tensor_mul(scw[:, 0:8], scw[:, 0:8], eg),
                            lambda e, tt=tt: e.reduce_sum(shw[:, tt:tt + 1], scw[:, 0:8], axis=AX.X),
                            lambda e, tt=tt, g0=g0: e.tensor_sub(shw[:, tt:tt + 1], g0[:, 0:1],
                                                                 shw[:, tt:tt + 1]),
                        ])
                sub_m = ctx.cnt["dve"]
                @blk.scalar
                def _(a):
                    ctx.wait_future(a, "dve", sub_m)
                    for tt in range(NT):
                        ctx.inc(a.activation(shw[:, tt:tt + 1], shw[:, tt:tt + 1], AF.Sigmoid), "act")
                @blk.vector
                def _(v):
                    ctx.wait(v, "act")
                    row_chain(ctx, v, [
                        lambda e: e.tensor_scalar(myw[:, :], myw[:, :], ctx.cinfo[:, 1:2], None,
                                                  ALU.is_equal),
                        lambda e: e.tensor_scalar(gl[:, 0:NT], shw[:, :], -1.0, 1.0,
                                                  ALU.mult, ALU.add),
                        lambda e: e.tensor_mul(myw[:, :], myw[:, :], gl[:, 0:NT]),
                    ])
            ctx.bank_clear()
            emit_col16_to_row(ctx, myw, rwt[0:1, 0:TOK], tmp16)
            with nc.Block() as blk:
                @blk.gpsimd
                def _(g):
                    ctx.inc(g.memset(scw[:, 0:8], 0.0), "pool")
                    g.wait_ge(ctx.sems["pool"], ctx.cnt["pool"])
                    ctx.inc(g.tensor_scalar_add(scw[:, 0:1], scw[:, 0:1], 1.0), "pool")
                    g.wait_ge(ctx.sems["pool"], ctx.cnt["pool"])
                    for tt in range(NT):
                        ctx.inc(g.tensor_copy(scw[:, 1:2], shw[:, tt:tt + 1]), "pool")
                        g.wait_ge(ctx.sems["pool"], ctx.cnt["pool"])
                        ctx.inc(g.dma_start(out=ctx.l1_pack.ap()[tt * 128:(tt + 1) * 128, :],
                                            in_=scw[:, 0:8]), "dst2", 16)
                        g.wait_ge(ctx.sems["dst2"], ctx.cnt["dst2"])
            ctx.bank_clear()
            emit_rows_bcast_pair_n(ctx, rwt, [0], [bmyw], TOK)
            print("[sbuf] before expert:", nc.sbuf_bytes_remaining)
            xal = ctx.x_sb[:, :].bitcast(BF16)
            xmT = xal[:, 0:16384]
            zbf = xal[:, 16384:24576]
            with nc.sbuf_tensor(ctx.uname("m_wb"), [128, NCH * 512], BF16) as wbuf, \
                 nc.sbuf_tensor(ctx.uname("m_z"), [128, 4 * 512], F32) as zf, \
                 nc.sbuf_tensor(ctx.uname("m_sq"), [128, 512], F32) as sqb, \
                 nc.sbuf_tensor(ctx.uname("m_t1"), [128, 512], F32) as t1scr, \
                 nc.sbuf_tensor(ctx.uname("m_bab"), [128, 2 * 512], F32) as bAB, \
                 nc.sbuf_tensor(ctx.uname("m_selc"), [128, 8], F32) as selc, \
                 nc.sbuf_tensor(ctx.uname("m_tst"), [128, 256], F32) as tmst:
                with nc.Block() as blk:
                    ctx.sem("dtp")
                    @blk.sync
                    def _(s):
                        ctx.dma(s, selc[:, 0:4], ctx.DI["l1_selg_c"].ap(), sem="dio")
                        ctx.dma(s, selc[:, 4:8], ctx.DI["l1_selb_c"].ap(), sem="dio")
                        for tt in range(NT):
                            for c in range(NCH):
                                ctx.dma(s, xmT[:, c * TOK + tt * 128:c * TOK + (tt + 1) * 128],
                                        ctx.l1_xmoe_all.ap()[tt * 128:(tt + 1) * 128,
                                                             c * 128:(c + 1) * 128],
                                        sem="dtp", transpose=True)
                        s.wait_ge(ctx.sems["dtp"], ctx.cnt["dtp"])
                        s.wait_ge(ctx.sems["dio"], ctx.cnt["dio"])
                ctx.bank_clear()
                def xmc(c):
                    return xmT[:, c * TOK:(c + 1) * TOK]
                for mq in range(4):
                    def zsl(c, mq=mq):
                        return zbf[:, c * TOK + mq * 512:c * TOK + (mq + 1) * 512]
                    emit_glu512_ln(ctx, xmc, ctx.DI["l1_myw1T"], ctx.DI["l1_myb1_r"],
                                   ctx.DI["l1_myw2T"], ctx.DI["l1_myb2_r"], selc, 0,
                                   mq * 512, 512, zf, zsl, sqb, t1scr, bAB, rwt, wbuf, brow)
                with nc.Block() as blk:
                    @blk.gpsimd
                    def _(g):
                        load_w_tiles(ctx, g, wbuf, ctx.DI["l1_myw3T"].ap(), 0, 0, 4, 1024)
                    wm3 = ctx.cnt["dwt"]
                    @blk.sync
                    def _(s):
                        ctx.dma(s, brow[0:1, 0:1024], ctx.DI["l1_myb3_r"].ap(), sem="dio")
                        s.wait_ge(ctx.sems["dio"], ctx.cnt["dio"])
                ctx.bank_clear()
                ctx.sem("dst2")
                fmscr2 = ctx.es.enter_context(nc.sbuf_tensor(ctx.uname("m_fm"), [128, TOK], F32)) if False else None
                fm_cm = nc.sbuf_tensor(ctx.uname("m_fm"), [128, TOK], F32)
                fmscr2 = fm_cm.__enter__()
                for m in range(NCH):
                    with nc.Block() as blk:
                        def lhsE(k, mm_):
                            return wbuf[:, k * 1024 + m * 128:k * 1024 + (m + 1) * 128]
                        def rhsE(k, b):
                            return zbf[:, k * TOK + b * 512:k * TOK + (b + 1) * 512]
                        def biasE(mm_, b):
                            return (brow[0:1, m * 128:(m + 1) * 128], ctx.csb["ones_row512"][:, :])
                        def evacE(mm_, b, e, ps):
                            return e.tensor_mul(fmscr2[:, b * 512:(b + 1) * 512], ps[:, :],
                                                bmyw[:, b * 512:(b + 1) * 512])
                        jobs = fm_jobs(ctx, lhsE, rhsE, 4, 1, NB, biasE, evacE,
                                       wm3 if m == 0 else None)
                        run_pipeline(ctx, blk, jobs, banks=[ctx.PS[0], ctx.PS[1], ctx.PS[2], ctx.PS[3]])
                        ctx.sem("act")
                        pe_b = ctx.cnt["pe"]
                        act_b = ctx.cnt["act"]
                        dst_b = ctx.cnt["dst2"]
                        ev_done = ctx.cnt["evac"]
                        @blk.tensor
                        def _(t, m=m):
                            ctx.wait_future(t, "evac", ev_done)
                            for tt in range(NT):
                                ctx.bank_acquire(t, ctx.PTR)
                                ctx.inc(t.transpose(ctx.PTR[:, 0:128],
                                                    fmscr2[:, tt * 128:(tt + 1) * 128],
                                                    ctx.csb["ident_f32"][:, :]), "pe")
                                ctx.bank_release(ctx.PTR, "act", act_b + tt + 1)
                        @blk.scalar
                        def _(a, m=m):
                            for tt in range(NT):
                                ctx.wait_future(a, "pe", pe_b + tt + 1)
                                if tt >= 2:
                                    ctx.wait_future(a, "dst2", dst_b + 16 * (tt - 1))
                                ctx.inc(a.copy(tmst[:, (tt % 2) * 128:(tt % 2) * 128 + 128],
                                               ctx.PTR[:, 0:128]), "act")
                        @blk.gpsimd
                        def _(g, m=m):
                            for tt in range(NT):
                                ctx.wait_future(g, "act", act_b + tt + 1)
                                ctx.inc(g.dma_start(
                                    out=ctx.l1_contrib.ap()[tt * 128:(tt + 1) * 128,
                                                            m * 128:(m + 1) * 128],
                                    in_=tmst[:, (tt % 2) * 128:(tt % 2) * 128 + 128]), "dst2", 16)
                            g.wait_ge(ctx.sems["dst2"], ctx.cnt["dst2"])
                    ctx.bank_clear()
                fm_cm.__exit__(None, None, None)
                with nc.Block() as blk:
                    @blk.gpsimd
                    def _(g):
                        ctx.inc(g.collective_compute(
                            "AllReduce", ALU.add, replica_groups=[[0, 1, 2, 3, 4, 5, 6, 7]],
                            ins=[ctx.l1_contrib.ap()], outs=[ctx.l1_contrib_all.ap()]), "cc", 1)
                        g.wait_ge(ctx.sems["cc"], ctx.cnt["cc"])
                ctx.bank_clear()
        bmy_cm.__exit__(None, None, None)
        emit_reload_x(ctx)
        with nc.sbuf_tensor(ctx.uname("o_idx"), [128, NT], U32) as idxs, \
             nc.sbuf_tensor(ctx.uname("o_idf"), [128, NT], F32) as idxf, \
             nc.sbuf_tensor(ctx.uname("o_iv"), [128, 2 * NT + 4], F32) as invf, \
             nc.sbuf_tensor(ctx.uname("o_ivu"), [128, NT], U32) as invu, \
             nc.sbuf_tensor(ctx.uname("o_st"), [128, D], F32) as stage, \
             nc.sbuf_tensor(ctx.uname("o_pk"), [128, 8], F32) as pk, \
             nc.sbuf_tensor(ctx.uname("o_fac"), [128, NT], F32) as fac, \
             nc.sbuf_tensor(ctx.uname("o_t16"), [16, 128], F32) as tmp16b, \
             nc.sbuf_tensor(ctx.uname("o_rw"), [1, 3 * TOK], F32) as rwt2:
            ctx.rw = rwt2
            with nc.Block() as blk:
                @blk.sync
                def _(s):
                    ctx.dma(s, idxf[:, :], ctx.DI["l1_flat"].ap(), sem="dio")
                    s.wait_ge(ctx.sems["dio"], ctx.cnt["dio"])
                @blk.vector
                def _(v):
                    ctx.wait(v, "dio")
                    m = ctx.inc(v.tensor_scalar(idxf[:, :], idxf[:, :], ctx.cinfo[:, 0:1], None,
                                                ALU.subtract), "dve")
                    ctx.wait(v, "dve", m)
                    emit_clamp_idx(ctx, v, idxf[:, :], pk[:, 0:NT] if False else fac[:, :],
                                   invf[:, 0:NT], TOK)
                    m = ctx.inc(v.memset(invf[:, 2 * NT:2 * NT + 2], 99999.0), "dve")
                    ctx.wait(v, "dve", m)
                    for tt in range(NT):
                        m = ctx.inc(v.tensor_scalar_add(invf[:, tt:tt + 1], ctx.csb["iota128"][:, :],
                                                        float(tt * 128)), "dve")
                    ctx.wait(v, "dve")
                @blk.gpsimd
                def _(g):
                    ctx.wait(g, "dve")
                    u32_index(ctx, g, idxs[:, :], idxf[:, :])
                    for tt in range(NT):
                        ctx.inc(g.dma_start(out=ctx.l1_inv.ap()[tt * 128:(tt + 1) * 128, :],
                                            in_=invf[:, 2 * NT:2 * NT + 2]), "dst2", 16)
                    g.wait_ge(ctx.sems["dst2"], ctx.cnt["dst2"])
                    for tt in range(NT):
                        ctx.inc(g.tensor_copy(invf[:, 2 * NT + 2:2 * NT + 3], invf[:, tt:tt + 1]), "pool")
                        ctx.inc(g.tensor_copy(invf[:, 2 * NT + 3:2 * NT + 4], invf[:, tt:tt + 1]), "pool")
                        g.wait_ge(ctx.sems["pool"], ctx.cnt["pool"])
                        ctx.inc(g.indirect_dma_start(
                            ctx.l1_inv.ap(), IndirectOffsetOnAxis(idxs[:, tt:tt + 1], axis=0),
                            invf[:, 2 * NT + 2:2 * NT + 4], None), "dsh", 16)
                    g.wait_ge(ctx.sems["dsh"], ctx.cnt["dsh"])
            ctx.bank_clear()
            with nc.Block() as blk:
                @blk.sync
                def _(s):
                    ctx.dma(s, invf[:, 0:2 * NT].rearrange("p (tt g) -> p tt g", g=2),
                            ctx.l1_inv.ap()[0:TOK, :].rearrange("(tt p) g -> p tt g", p=128), sem="dio")
                    s.wait_ge(ctx.sems["dio"], ctx.cnt["dio"])
                @blk.vector
                def _(v):
                    ctx.wait(v, "dio")
                    m = ctx.inc(v.tensor_copy(
                        fac[:, :], invf[:, 0:2 * NT].rearrange("p (tt g) -> p tt g", g=2)[:, :, 0:1]
                        .rearrange("p tt g -> p (tt g)")), "dve")
                    ctx.wait(v, "dve", m)
                    emit_clamp_idx(ctx, v, fac[:, :], pk[:, 0:8].bitcast(F32)[:, 0:8] if False else
                                   invf[:, 0:NT], invf[:, NT:2 * NT], TOK)
                @blk.gpsimd
                def _(g):
                    ctx.wait(g, "dve")
                    u32_index(ctx, g, invu[:, :], fac[:, :])
            ctx.bank_clear()
            with nc.Block() as blk:
                @blk.gpsimd
                def _(g):
                    for tt in range(NT):
                        ctx.inc(g.indirect_dma_start(
                            pk[:, :], None, ctx.l1_pack.ap(),
                            IndirectOffsetOnAxis(invu[:, tt:tt + 1], axis=0)), "dsh", 16)
                        g.wait_ge(ctx.sems["dsh"], ctx.cnt["dsh"])
                        ctx.inc(g.tensor_mul(pk[:, 2:3], pk[:, 0:1], pk[:, 1:2]), "pool")
                        g.wait_ge(ctx.sems["pool"], ctx.cnt["pool"])
                        ctx.inc(g.tensor_sub(pk[:, 2:3], pk[:, 2:3], pk[:, 0:1]), "pool")
                        g.wait_ge(ctx.sems["pool"], ctx.cnt["pool"])
                        ctx.inc(g.tensor_scalar_add(fac[:, tt:tt + 1], pk[:, 2:3], 1.0), "pool")
                        g.wait_ge(ctx.sems["pool"], ctx.cnt["pool"])
            ctx.bank_clear()
            emit_col16_to_row(ctx, fac, rwt2[0:1, 0:TOK], tmp16b)
            with nc.sbuf_tensor(ctx.uname("o_bf"), [128, TOK], F32) as bfac:
                emit_rows_bcast_pair_n(ctx, rwt2, [0], [bfac], TOK)
                with nc.Block() as blk:
                    ctx.sem("dve")
                    for tt in range(NT):
                        @blk.gpsimd
                        def _(g, tt=tt):
                            if tt >= 1:
                                ctx.wait(g, "dve")
                            ctx.inc(g.indirect_dma_start(
                                stage[:, :], None, ctx.l1_contrib_all.ap(),
                                IndirectOffsetOnAxis(invu[:, tt:tt + 1], axis=0)), "dsh", 16)
                            g.wait_ge(ctx.sems["dsh"], ctx.cnt["dsh"])
                        pe_b = ctx.cnt["pe"]
                        for c in range(NCH):
                            @blk.tensor
                            def _(t, tt=tt, c=c):
                                ctx.wait(t, "dsh")
                                ctx.bank_acquire(t, ctx.PTR)
                                ctx.inc(t.transpose(ctx.PTR[:, 0:128],
                                                    stage[:, c * 128:(c + 1) * 128],
                                                    ctx.csb["ident_f32"][:, :]), "pe")
                            @blk.vector
                            def _(v, tt=tt, c=c):
                                ctx.wait(v, "pe")
                                dst = ctx.xc(c)[:, tt * 128:(tt + 1) * 128]
                                ctx.inc(v.tensor_add(dst, dst, ctx.PTR[:, 0:128]), "dve")
                            ctx.bank_release(ctx.PTR, "dve", ctx.cnt["dve"])
                ctx.bank_clear()
                print("[sbuf] before shared:", nc.sbuf_bytes_remaining)
                with nc.sbuf_tensor(ctx.uname("s_wb"), [128, NCH * 512], BF16) as wbuf, \
                     nc.sbuf_tensor(ctx.uname("s_z"), [128, 4 * 512], F32) as zf, \
                     nc.sbuf_tensor(ctx.uname("s_zb"), [128, 4 * 512], BF16) as zbf, \
                     nc.sbuf_tensor(ctx.uname("s_sq"), [128, 512], F32) as sqb, \
                     nc.sbuf_tensor(ctx.uname("s_t1"), [128, 512], F32) as t1scr, \
                     nc.sbuf_tensor(ctx.uname("s_bab"), [128, 2 * 512], F32) as bAB, \
                     nc.sbuf_tensor(ctx.uname("s_gc"), [128, 8], F32) as slc, \
                     nc.sbuf_tensor(ctx.uname("s_t2"), [128, 512], F32) as t2scr:
                    with nc.Block() as blk:
                        @blk.sync
                        def _(s):
                            ctx.dma(s, slc[:, 0:4], ctx.DI["l1_slng_c"].ap(), sem="dio")
                            ctx.dma(s, slc[:, 4:8], ctx.DI["l1_slnb_c"].ap(), sem="dio")
                            s.wait_ge(ctx.sems["dio"], ctx.cnt["dio"])
                    for q4 in range(4):
                        def zsl(c):
                            return zbf[:, c * 512:(c + 1) * 512]
                        emit_glu512_ln(ctx, ctx.xnc, ctx.DI["l1_sw1T"], ctx.DI["l1_sb1_r"],
                                       ctx.DI["l1_sw2T"], ctx.DI["l1_sb2_r"], slc, 0,
                                       q4 * 512, 512, zf, zsl, sqb, t1scr, bAB, rwt2, wbuf, brow)
                        with nc.Block() as blk:
                            @blk.gpsimd
                            def _(g):
                                load_w_tiles(ctx, g, wbuf, ctx.DI["l1_sw3T"].ap(), 0, 0, 4, 1024)
                            wm = ctx.cnt["dwt"]
                            @blk.sync
                            def _(s):
                                ctx.dma(s, brow[0:1, 0:1024], ctx.DI["l1_sb3_r"].ap(), sem="dio")
                                s.wait_ge(ctx.sems["dio"], ctx.cnt["dio"])
                            def lhs3(k, m):
                                return wbuf[:, k * 1024 + m * 128:k * 1024 + (m + 1) * 128]
                            def rhs3(k, b):
                                return zbf[:, k * 512:(k + 1) * 512]
                            def bias3(m, b):
                                return (brow[0:1, m * 128:(m + 1) * 128], ctx.csb["ones_row512"][:, :])
                            def evac3(m, b, e, ps):
                                mm1 = ctx.inc(e.tensor_mul(t2scr[:, :], ps[:, :],
                                                           bfac[:, q4 * 512:(q4 + 1) * 512]), "dve")
                                ctx.wait(e, "dve", mm1)
                                dst = ctx.xc(m)[:, q4 * 512:(q4 + 1) * 512]
                                return e.tensor_add(dst, dst, t2scr[:, :])
                            jobs = fm_jobs(ctx, lhs3, rhs3, 4, NCH, 1, bias3, evac3, wm)
                            run_pipeline(ctx, blk, jobs,
                                         banks=[ctx.PS[0], ctx.PS[1], ctx.PS[2], ctx.PS[3]])
                        ctx.bank_clear()


def emit_linattn_kv(ctx, ktile, vtile, kvsb, kvp, kvbf, shidx, shidx_f):
    nc = ctx.nc
    with nc.Block() as blk:
        banks = [ctx.PS[4], ctx.PRA, ctx.PRB, ctx.PTR]
        @blk.tensor
        def _(t):
            for bi in range(4):
                ctx.bank_acquire(t, banks[bi])
            for h in range(HEADS):
                bank = banks[h // 4]
                cb = (h % 4) * 65
                for tt in range(NT):
                    t.matmul(bank[0:64, cb:cb + 64], ktile(tt)[:, h * 64:(h + 1) * 64],
                             vtile(tt)[:, h * 64:(h + 1) * 64],
                             start=(h % 4 == 0 and tt == 0), stop=False)
                for tt in range(NT):
                    last = t.matmul(bank[0:64, cb + 64:cb + 65],
                                    ktile(tt)[:, h * 64:(h + 1) * 64],
                                    ctx.csb["ones_col_bf"][:, :],
                                    start=False, stop=(h % 4 == 3 and tt == NT - 1))
                if h % 4 == 3:
                    ctx.inc(last, "pe")
        pe_done = ctx.cnt["pe"]
        @blk.scalar
        def _(a):
            ctx.wait_future(a, "pe", pe_done)
            for bi in range(4):
                ctx.inc(a.copy(kvsb[:, bi * 260:(bi + 1) * 260], banks[bi][0:64, 0:260]), "act")
        for bi in range(4):
            ctx.bank_release(banks[bi], "act", ctx.cnt["act"])
    ctx.bank_clear()
    shkv = ctx.sh_kv.ap().rearrange("s (p q) -> (s p) q", p=64)[:, 0:1040]
    with nc.Block() as blk:
        @blk.vector
        def _(v):
            m1 = ctx.inc(v.tensor_scalar(shidx_f[:, 0:1], ctx.cinfo[:, 2:3],
                                         1.0 / 32.0, None, ALU.mult), "dve")
            m1 = ctx.inc(v.tensor_scalar(shidx_f[:, 1:2], ctx.cinfo[:, 2:3],
                                         -1.0 / 32.0, 64.0, ALU.mult, ALU.add), "dve")
            ctx.wait(v, "dve", m1)
            m2 = ctx.inc(v.tensor_add(shidx_f[:, 0:1], shidx_f[:, 0:1],
                                      ctx.csb["iota128"][:, :]), "dve")
            m2 = ctx.inc(v.tensor_add(shidx_f[:, 1:2], shidx_f[:, 1:2],
                                      ctx.csb["iota128"][:, :]), "dve")
            ctx.wait(v, "dve", m2)
        @blk.gpsimd
        def _(g):
            ctx.wait(g, "dve")
            ctx.wait(g, "act")
            u32_index(ctx, g, shidx[:, :], shidx_f[:, :])
            ctx.inc(g.indirect_dma_start(
                shkv, IndirectOffsetOnAxis(shidx[0:64, 0:1], axis=0),
                kvsb[:, :], None), "dsh", 16)
            g.wait_ge(ctx.sems["dsh"], ctx.cnt["dsh"])
            emit_pair_barrier(ctx, g)
            ctx.inc(g.indirect_dma_start(
                kvp[:, :], None, shkv,
                IndirectOffsetOnAxis(shidx[0:64, 1:2], axis=0)), "dsh", 16)
            g.wait_ge(ctx.sems["dsh"], ctx.cnt["dsh"])
        @blk.vector
        def _(v):
            ctx.wait(v, "dsh")
            m1 = ctx.inc(v.tensor_add(kvsb[:, :], kvsb[:, :], kvp[:, :]), "dve")
            ctx.wait(v, "dve", m1)
            src = kvsb[:, :].rearrange("p (h q) -> p h q", q=65)
            dste = kvbf[0:64, :].rearrange("p (h q) -> p h q", q=65)
            dsto = kvbf[64:128, :].rearrange("p (h q) -> p h q", q=65)
            m2 = ctx.inc(v.tensor_copy(dste, src[:, 0:16:2, :]), "dve")
            m2 = ctx.inc(v.tensor_copy(dsto, src[:, 1:16:2, :]), "dve")
            ctx.wait(v, "dve", m2)
    ctx.bank_clear()


def emit_linattn_ctx(ctx, q_sb, kvbf, apre, binv, arw, denom_mode):
    nc = ctx.nc
    for h in range(HEADS):
        hc, rb = h // 2, (h % 2) * 64
        with nc.Block() as blk:
            for nb_i in range(NB):
                @blk.tensor
                def _(t, h=h, nb_i=nb_i, hc=hc, rb=rb):
                    ctx.bank_acquire(t, ctx.PS[4])
                    ctx.inc(t.matmul(
                        ctx.PS[4][0:65, :], kvbf[rb:rb + 64, hc * 65:(hc + 1) * 65],
                        q_sb[rb:rb + 64, hc * TOK + nb_i * 512:hc * TOK + (nb_i + 1) * 512],
                        start=True, stop=True), "pe")
                @blk.vector
                def _(v):
                    ctx.wait(v, "pe")
                    if denom_mode == "max":
                        m1 = ctx.inc(v.tensor_scalar_max(arw[0:1, :], ctx.PS[4][64:65, :], 1e-6), "dve")
                    else:
                        m1 = ctx.inc(v.tensor_scalar_add(arw[0:1, :], ctx.PS[4][64:65, :], 1e-6), "dve")
                    ctx.wait(v, "dve", m1)
                    m2 = ctx.inc(v.reciprocal(arw[0:1, :], arw[0:1, :]), "dve")
                    ctx.wait(v, "dve", m2)
                @blk.tensor
                def _(t):
                    ctx.wait(t, "dve")
                    ctx.bank_acquire(t, ctx.PRA)
                    ctx.inc(t.matmul(ctx.PRA[:, :], ctx.csb["ones_row128"][:, :],
                                     arw[0:1, :], start=True, stop=True), "pe")
                @blk.scalar
                def _(a):
                    ctx.wait(a, "pe")
                    ctx.inc(a.copy(binv[:, :], ctx.PRA[:, :]), "act")
                ctx.bank_release(ctx.PRA, "act", ctx.cnt["act"])
                @blk.vector
                def _(v, h=h, nb_i=nb_i, hc=hc, rb=rb):
                    ctx.wait(v, "act")
                    dst = apre[rb:rb + 64, hc * TOK + nb_i * 512:hc * TOK + (nb_i + 1) * 512]
                    ctx.inc(v.tensor_mul(dst, ctx.PS[4][0:64, :], binv[0:64, :]), "dve")
                ctx.bank_release(ctx.PS[4], "dve", ctx.cnt["dve"])
        ctx.bank_clear()


def emit_elu_bf(ctx, blk, src_f32, tmp_f32, dst_bf, n):
    @blk.vector
    def _(v):
        m = ctx.inc(v.tensor_scalar_min(tmp_f32[:, 0:n], src_f32[:, 0:n], 0.0), "dve")
        ctx.wait(v, "dve", m)
        ctx.inc(v.tensor_scalar_max(src_f32[:, 0:n], src_f32[:, 0:n], 0.0), "dve")
    mx_m = ctx.cnt["dve"]
    @blk.scalar
    def _(a):
        ctx.wait_future(a, "dve", mx_m)
        ctx.inc(a.activation(tmp_f32[:, 0:n], tmp_f32[:, 0:n], AF.Exp), "act")
    @blk.vector
    def _(v):
        ctx.wait(v, "act")
        m = ctx.inc(v.tensor_add(dst_bf[:, 0:n], src_f32[:, 0:n], tmp_f32[:, 0:n]), "dve")
        ctx.wait(v, "dve", m)


def emit_ln_inplace(ctx, z_fn, nch, eps, wb_cols, wcol, sqb, bA, bB, rwt, zbf_slice=None):
    nc = ctx.nc
    with nc.Block() as blk:
        emit_ln_rows(ctx, blk, z_fn, nch, TOK, eps, 1.0 / (nch * 128.0), [sqb, bA], 0)
    ctx.bank_clear()
    emit_rows_bcast_pair_n(ctx, rwt, [TOK, 2 * TOK], [bA, bB], TOK)
    with nc.Block() as blk:
        @blk.vector
        def _(v):
            for c in range(nch):
                zc = z_fn(c)
                m = ctx.inc(v.tensor_mul(sqb[:, :], zc, bA[:, :]), "dve")
                ctx.wait(v, "dve", m)
                m = ctx.inc(v.tensor_add(sqb[:, :], sqb[:, :], bB[:, :]), "dve")
                ctx.wait(v, "dve", m)
                m = ctx.inc(v.tensor_scalar(zc, sqb[:, :], wb_cols[:, wcol + c:wcol + c + 1],
                                            wb_cols[:, wcol + nch + c:wcol + nch + c + 1],
                                            ALU.mult, ALU.add), "dve")
                ctx.wait(v, "dve", m)
                if zbf_slice is not None:
                    m = ctx.inc(v.tensor_copy(zbf_slice(c), zc), "dve")
                    ctx.wait(v, "dve", m)
    ctx.bank_clear()


def emit_ln_inplace_alloc(ctx, z_fn, nch, eps, wbcols, wcol, zbf_slice=None):
    nc = ctx.nc
    with nc.sbuf_tensor(ctx.uname("lni_sq"), [128, TOK], F32) as sqb, \
         nc.sbuf_tensor(ctx.uname("lni_bA"), [128, TOK], F32) as bA, \
         nc.sbuf_tensor(ctx.uname("lni_bB"), [128, TOK], F32) as bB, \
         nc.sbuf_tensor(ctx.uname("lni_rw"), [1, 3 * TOK], F32) as rwt:
        ctx.rw = rwt
        emit_ln_inplace(ctx, z_fn, nch, eps, wbcols, wcol, sqb, bA, bB, rwt, zbf_slice=zbf_slice)


def emit_layer2(ctx):
    ctx.phase = "emit_layer2"
    nc = ctx.nc
    with nc.sbuf_tensor(ctx.uname("l2_brow"), [1, 1024], F32) as brow:
        with nc.sbuf_tensor(ctx.uname("l2_rows"), [1, 2 * TOK], F32) as rows16:
            with nc.sbuf_tensor(ctx.uname("l2_logt"), [16, TOK], F32) as logt:
                ctx.l_rows = rows16
                emit_outer_ln(ctx, rows_out=rows16)
                with nc.Block() as blk:
                    @blk.gpsimd
                    def _(g):
                        ctx.inc(g.memset(brow[0:1, 0:16], 0.0), "pool")
                        g.wait_ge(ctx.sems["pool"], ctx.cnt["pool"])
                emit_gating_fm(ctx, ctx.DI["l2_gw"], 8, brow, logt, dram_rows=ctx.l2_gsc)
            emit_spill_x(ctx)
            with nc.sbuf_tensor(ctx.uname("l2_bA0"), [128, TOK], F32) as bA0, \
                 nc.sbuf_tensor(ctx.uname("l2_bB0"), [128, TOK], F32) as bB0:
                emit_rows_bcast_pair_n(ctx, rows16, [0, TOK], [bA0, bB0], TOK)
                with nc.Block() as blk:
                    @blk.vector
                    def _(v):
                        for c in range(NCH):
                            zc = ctx.xc(c)
                            m = ctx.inc(v.tensor_mul(zc, zc, bA0[:, :]), "dve")
                            ctx.wait(v, "dve", m)
                            m = ctx.inc(v.tensor_add(zc, zc, bB0[:, :]), "dve")
                            ctx.wait(v, "dve", m)
                ctx.bank_clear()
        with nc.Block() as blk:
            @blk.sync
            def _(s):
                for c in range(NCH):
                    ctx.dma(s, ctx.spill_xn.ap()[c * 128:(c + 1) * 128, :], ctx.xnc(c), sem="dio")
                s.wait_ge(ctx.sems["dio"], ctx.cnt["dio"])
        zbf = ctx.xn_bf
        with nc.Block() as blk:
            @blk.vector
            def _(v):
                for c in range(NCH):
                    m = ctx.inc(v.tensor_copy(zbf[:, c * TOK:(c + 1) * TOK], ctx.xc(c)), "dve")
                    ctx.wait(v, "dve", m)
        ctx.bank_clear()
        for i in range(6):
            emit_l2_sublayer(ctx, i, brow, zbf)
        with nc.sbuf_tensor(ctx.uname("l2_fsq"), [128, TOK], F32) as sqb, \
             nc.sbuf_tensor(ctx.uname("l2_fbA"), [128, TOK], F32) as bA, \
             nc.sbuf_tensor(ctx.uname("l2_fbB"), [128, TOK], F32) as bB, \
             nc.sbuf_tensor(ctx.uname("l2_frw"), [1, 3 * TOK], F32) as rwt, \
             nc.sbuf_tensor(ctx.uname("l2_fwb"), [128, 16], F32) as wbcols:
            ctx.rw = rwt
            with nc.Block() as blk:
                @blk.sync
                def _(s):
                    ctx.dma(s, wbcols[:, 0:8], ctx.DI["l2_lnfw_c"].ap(), sem="dio")
                    ctx.dma(s, wbcols[:, 8:16], ctx.DI["l2_lnfb_c"].ap(), sem="dio")
                    s.wait_ge(ctx.sems["dio"], ctx.cnt["dio"])
            emit_ln_inplace(ctx, ctx.xc, NCH, 1e-5, wbcols, 0, sqb, bA, bB, rwt)
        with nc.Block() as blk:
            @blk.sync
            def _(s):
                for c in range(NCH):
                    ctx.dma(s, ctx.spill_z.ap()[c * 128:(c + 1) * 128, :], ctx.xc(c), sem="dio")
                s.wait_ge(ctx.sems["dio"], ctx.cnt["dio"])
        emit_reload_x(ctx)
        with nc.Block() as blk:
            @blk.sync
            def _(s):
                for c in range(NCH):
                    ctx.dma(s, ctx.xnc(c), ctx.spill_xn.ap()[c * 128:(c + 1) * 128, :], sem="dio")
                s.wait_ge(ctx.sems["dio"], ctx.cnt["dio"])
        emit_l2_ffn(ctx, brow)
        with nc.sbuf_tensor(ctx.uname("l2_zl"), [128, TOK], F32) as zl, nc.Block() as blk:
            ctx.sem("dio"); ctx.sem("dve")
            dio_b = ctx.cnt["dio"]
            @blk.sync
            def _(s):
                for c in range(NCH):
                    if c >= 1:
                        ctx.wait_future(s, "dve", ctx.cnt["dve"] + c)
                    ctx.dma(s, zl[:, :], ctx.spill_z.ap()[c * 128:(c + 1) * 128, :], sem="dio")
            @blk.vector
            def _(v):
                for c in range(NCH):
                    ctx.wait_future(v, "dio", dio_b + 16 * (c + 1))
                    dst = ctx.xc(c)
                    ctx.inc(v.tensor_add(dst, dst, zl[:, :]), "dve")
        ctx.bank_clear()


def emit_l2_sublayer(ctx, i, brow, zbf):
    ctx.phase = "emit_l2_sublayer"
    nc = ctx.nc
    def zb(c):
        return zbf[:, c * TOK:(c + 1) * TOK]
    with nc.Block() as blk:
        @blk.sync
        def _(s):
            for c in range(NCH):
                ctx.dma(s, ctx.spill_z.ap()[c * 128:(c + 1) * 128, :], ctx.xc(c), sem="dio")
            s.wait_ge(ctx.sems["dio"], ctx.cnt["dio"])
    xal = ctx.x_sb[:, :].bitcast(BF16)
    def ktile(tt):
        return xal[:, tt * 1024:(tt + 1) * 1024]
    def vtile(tt):
        return xal[:, 16384 + tt * 1024:16384 + (tt + 1) * 1024]
    with nc.sbuf_tensor(ctx.uname("s2_q"), [128, NCH * TOK], BF16) as q_sb, \
         nc.sbuf_tensor(ctx.uname("s2_apre"), [128, NCH * TOK], BF16) as apre:
        with nc.sbuf_tensor(ctx.uname("s2_wb"), [128, NCH * 1024], BF16) as wbuf, \
             nc.sbuf_tensor(ctx.uname("s2_qf"), [128, TOK], F32) as qf, \
             nc.sbuf_tensor(ctx.uname("s2_tmp"), [128, TOK], F32) as tmpf:
            with nc.Block() as blk:
                @blk.gpsimd
                def _(g):
                    load_w_tiles(ctx, g, wbuf, ctx.DI["l2_Wq"].ap(), i * D, 0, NCH, 1024)
                wm = ctx.cnt["dwt"]
                @blk.sync
                def _(s):
                    ctx.dma(s, brow[0:1, 0:D], ctx.DI["l2_bq_r"].ap()[i:i + 1, :], sem="dio")
                    s.wait_ge(ctx.sems["dio"], ctx.cnt["dio"])
            ctx.bank_clear()
            for m in range(NCH):
                with nc.Block() as blk:
                    def lhsQ(k, mm_):
                        return wbuf[:, k * 1024 + m * 128:k * 1024 + (m + 1) * 128]
                    def rhsQ(k, b):
                        return zb(k)[:, b * 512:(b + 1) * 512]
                    def biasQ(mm_, b):
                        return (brow[0:1, m * 128:(m + 1) * 128], ctx.csb["ones_row512"][:, :])
                    def evacQ(mm_, b, a, ps):
                        return a.copy(qf[:, b * 512:(b + 1) * 512], ps[:, :])
                    jobs = fm_jobs(ctx, lhsQ, rhsQ, NCH, 1, NB, biasQ, evacQ,
                                   wm if m == 0 else None, evac_eng="act")
                    run_pipeline(ctx, blk, jobs, banks=[ctx.PS[0], ctx.PS[1], ctx.PS[2], ctx.PS[3]])
                    @blk.vector
                    def _(v):
                        ctx.wait(v, "evac")
                        pass
                    emit_elu_bf(ctx, blk, qf, tmpf, q_sb[:, m * TOK:(m + 1) * TOK], TOK)
                ctx.bank_clear()
            with nc.Block() as blk:
                @blk.gpsimd
                def _(g):
                    load_w_tiles(ctx, g, wbuf, ctx.DI["l2_Wk"].ap(), i * D, 0, NCH, 1024)
                wmk = ctx.cnt["dwt"]
                @blk.sync
                def _(s):
                    ctx.dma(s, brow[0:1, 0:D], ctx.DI["l2_bk_r"].ap()[i:i + 1, :], sem="dio")
                    s.wait_ge(ctx.sems["dio"], ctx.cnt["dio"])
            ctx.bank_clear()
            for tt in range(NT):
                with nc.Block() as blk:
                    def lhsK(k, mm_):
                        return zb(k)[:, tt * 128:(tt + 1) * 128]
                    def rhsK(k, b):
                        return wbuf[:, k * 1024 + b * 512:k * 1024 + (b + 1) * 512]
                    def biasK(mm_, b):
                        return (ctx.csb["ones_row128"][:, :], brow[0:1, b * 512:(b + 1) * 512])
                    def evacK(mm_, b, a, ps):
                        return a.copy(qf[:, b * 512:(b + 1) * 512], ps[:, :])
                    jobs = fm_jobs(ctx, lhsK, rhsK, NCH, 1, 2, biasK, evacK,
                                   wmk if tt == 0 else None, evac_eng="act")
                    run_pipeline(ctx, blk, jobs, banks=[ctx.PS[0], ctx.PS[1]])
                    emit_elu_bf(ctx, blk, qf, tmpf, ktile(tt), 1024)
                ctx.bank_clear()
            with nc.Block() as blk:
                @blk.gpsimd
                def _(g):
                    load_w_tiles(ctx, g, wbuf, ctx.DI["l2_Wv"].ap(), i * D, 0, NCH, 1024)
                wmv = ctx.cnt["dwt"]
                @blk.sync
                def _(s):
                    ctx.dma(s, brow[0:1, 0:D], ctx.DI["l2_bv_r"].ap()[i:i + 1, :], sem="dio")
                    s.wait_ge(ctx.sems["dio"], ctx.cnt["dio"])
            ctx.bank_clear()
            emit_tm_proj(ctx, zb, wbuf, wmv,
                         lambda tt, vb: vtile(tt)[:, vb * 512:(vb + 1) * 512],
                         bias_row_sb=brow, act_fn=None)
        with nc.sbuf_tensor(ctx.uname("s2_kv"), [64, 16 * 65], F32) as kvsb, \
             nc.sbuf_tensor(ctx.uname("s2_kvp"), [64, 16 * 65], F32) as kvp, \
             nc.sbuf_tensor(ctx.uname("s2_kvb"), [128, 8 * 65], BF16) as kvbf, \
             nc.sbuf_tensor(ctx.uname("s2_bi"), [128, 512], F32) as binv, \
             nc.sbuf_tensor(ctx.uname("s2_ar"), [1, 512], F32) as arw, \
             nc.sbuf_tensor(ctx.uname("s2_si"), [128, 2], U32) as shidx, \
             nc.sbuf_tensor(ctx.uname("s2_sf"), [128, 2], F32) as shidx_f:
            emit_linattn_kv(ctx, ktile, vtile, kvsb, kvp, kvbf, shidx, shidx_f)
            emit_linattn_ctx(ctx, q_sb, kvbf, apre, binv, arw, "add")
        with nc.sbuf_tensor(ctx.uname("s2_wb2"), [128, NCH * 1024], BF16) as wbuf:
            with nc.Block() as blk:
                @blk.gpsimd
                def _(g):
                    load_w_tiles(ctx, g, wbuf, ctx.DI["l2_Wo"].ap(), i * D, 0, NCH, 1024)
                wmo = ctx.cnt["dwt"]
                @blk.sync
                def _(s):
                    ctx.dma(s, brow[0:1, 0:D], ctx.DI["l2_bo_r"].ap()[i:i + 1, :], sem="dio")
                    s.wait_ge(ctx.sems["dio"], ctx.cnt["dio"])
            ctx.bank_clear()
            with nc.Block() as blk:
                @blk.sync
                def _(s):
                    for c in range(NCH):
                        ctx.dma(s, ctx.xc(c), ctx.spill_z.ap()[c * 128:(c + 1) * 128, :], sem="dio")
                    s.wait_ge(ctx.sems["dio"], ctx.cnt["dio"])
            with nc.Block() as blk:
                def lhsO(k, m):
                    return wbuf[:, k * 1024 + m * 128:k * 1024 + (m + 1) * 128]
                def rhsO(k, b):
                    return apre[:, k * TOK + b * 512:k * TOK + (b + 1) * 512]
                def biasO(m, b):
                    return (brow[0:1, m * 128:(m + 1) * 128], ctx.csb["ones_row512"][:, :])
                def evacO(m, b, e, ps):
                    dst = ctx.xc(m)[:, b * 512:(b + 1) * 512]
                    return e.tensor_add(dst, ps[:, :], dst)
                jobs = fm_jobs(ctx, lhsO, rhsO, NCH, NCH, NB, biasO, evacO, wmo)
                run_pipeline(ctx, blk, jobs, banks=[ctx.PS[0], ctx.PS[1], ctx.PS[2], ctx.PS[3]])
            ctx.bank_clear()
    with nc.sbuf_tensor(ctx.uname("s2_wc"), [128, 32], F32) as wbcols, \
         nc.sbuf_tensor(ctx.uname("s2_b1c"), [128, 32], F32) as b1c:
        with nc.Block() as blk:
            @blk.sync
            def _(s):
                ctx.dma(s, wbcols[:, 0:8], ctx.DI["l2_ln1w_c"].ap()[:, i * 8:(i + 1) * 8], sem="dio")
                ctx.dma(s, wbcols[:, 8:16], ctx.DI["l2_ln1b_c"].ap()[:, i * 8:(i + 1) * 8], sem="dio")
                ctx.dma(s, wbcols[:, 16:24], ctx.DI["l2_ln2w_c"].ap()[:, i * 8:(i + 1) * 8], sem="dio")
                ctx.dma(s, wbcols[:, 24:32], ctx.DI["l2_ln2b_c"].ap()[:, i * 8:(i + 1) * 8], sem="dio")
                ctx.dma(s, b1c[:, :], ctx.DI["l2_b1_c"].ap()[:, i * 32:(i + 1) * 32], sem="dio")
                s.wait_ge(ctx.sems["dio"], ctx.cnt["dio"])
        emit_ln_inplace_alloc(ctx, ctx.xc, NCH, 1e-5, wbcols, 0,
                              zbf_slice=lambda c: zbf[:, c * TOK:(c + 1) * TOK])
        with nc.sbuf_tensor(ctx.uname("s2_h"), [128, 32 * 1024], BF16) as hh, \
             nc.sbuf_tensor(ctx.uname("s2_b2r"), [1, D], F32) as b2r, \
             nc.sbuf_tensor(ctx.uname("s2_wb3"), [128, NCH * 1024], BF16) as wbuf:
            for half in range(2):
                t0 = half * 1024
                for wg in range(4):
                    with nc.Block() as blk:
                        @blk.gpsimd
                        def _(g):
                            load_w_tiles(ctx, g, wbuf, ctx.DI["l2_W1"].ap(), i * D, wg * 1024, NCH, 1024)
                        wm1 = ctx.cnt["dwt"]
                        def lhsH(k, mi):
                            return wbuf[:, k * 1024 + mi * 128:k * 1024 + (mi + 1) * 128]
                        def rhsH(k, b):
                            return zbf[:, k * TOK + t0 + b * 512:k * TOK + t0 + (b + 1) * 512]
                        def evacH(mi, b, a, ps):
                            mch = wg * 8 + mi
                            return a.activation(hh[:, mch * 1024 + b * 512:mch * 1024 + (b + 1) * 512],
                                                ps[:, :], AF.Relu, bias=b1c[:, mch:mch + 1])
                        jobs = fm_jobs(ctx, lhsH, rhsH, NCH, NCH, 2, None, evacH, wm1,
                                       evac_eng="act")
                        run_pipeline(ctx, blk, jobs, banks=[ctx.PS[0], ctx.PS[1], ctx.PS[2], ctx.PS[3]])
                    ctx.bank_clear()
                for wg in range(4):
                    with nc.Block() as blk:
                        @blk.gpsimd
                        def _(g):
                            load_w_tiles(ctx, g, wbuf, ctx.DI["l2_W2"].ap(), i * 4096 + wg * 1024, 0, NCH, 1024)
                        wm2 = ctx.cnt["dwt"]
                        @blk.sync
                        def _(s):
                            if wg == 0 and half == 0:
                                ctx.dma(s, b2r[0:1, 0:D], ctx.DI["l2_b2_r"].ap()[i:i + 1, :], sem="dio")
                            s.wait_ge(ctx.sems["dio"], ctx.cnt["dio"])
                        def lhsF(k, m):
                            return wbuf[:, k * 1024 + m * 128:k * 1024 + (m + 1) * 128]
                        def rhsF(k, b):
                            kch = wg * 8 + k
                            return hh[:, kch * 1024 + b * 512:kch * 1024 + (b + 1) * 512]
                        def biasF(m, b):
                            return (b2r[0:1, m * 128:(m + 1) * 128], ctx.csb["ones_row512"][:, :])
                        def evacF(m, b, e, ps):
                            dst = ctx.xc(m)[:, t0 + b * 512:t0 + (b + 1) * 512]
                            return e.tensor_add(dst, ps[:, :], dst)
                        bias_arg = biasF if wg == 3 else None
                        jobs = fm_jobs(ctx, lhsF, rhsF, NCH, NCH, 2, bias_arg, evacF, wm2)
                        run_pipeline(ctx, blk, jobs, banks=[ctx.PS[0], ctx.PS[1], ctx.PS[2], ctx.PS[3]])
                    ctx.bank_clear()
        emit_ln_inplace_alloc(ctx, ctx.xc, NCH, 1e-5, wbcols, 16,
                              zbf_slice=lambda c: zbf[:, c * TOK:(c + 1) * TOK])


def emit_l2_ffn(ctx, brow):
    ctx.phase = "emit_l2_ffn"
    nc = ctx.nc
    with nc.sbuf_tensor(ctx.uname("f2_gl"), [128, NT * 8], F32) as gl, \
         nc.sbuf_tensor(ctx.uname("f2_k1"), [128, NT * 8], F32) as k1, \
         nc.sbuf_tensor(ctx.uname("f2_k2"), [128, NT * 8], F32) as k2, \
         nc.sbuf_tensor(ctx.uname("f2_df"), [128, NT], F32) as dif, \
         nc.sbuf_tensor(ctx.uname("f2_sc"), [128, 24], F32) as scw, \
         nc.sbuf_tensor(ctx.uname("f2_wer"), [8, TOK], F32) as wEr, \
         nc.sbuf_tensor(ctx.uname("f2_s8"), [8, 1024], F32) as sel8sb, \
         nc.sbuf_tensor(ctx.uname("f2_b8"), [128, 16], F32) as b8:
        with nc.Block() as blk:
            @blk.sync
            def _(s):
                ctx.dma(s, sel8sb[:, :], ctx.consts["c_sel8"].ap(), sem="dio")
                ctx.dma(s, gl[:, :].rearrange("p (tt g) -> p tt g", g=8),
                        ctx.l2_gsc.ap().rearrange("(tt p) g -> p tt g", p=128)[:, :, 0:8], sem="dio")
                s.wait_ge(ctx.sems["dio"], ctx.cnt["dio"])
            @blk.tensor
            def _(t):
                ctx.wait(t, "dio")
                ctx.bank_acquire(t, ctx.PRA)
                ctx.inc(t.matmul(ctx.PRA[:, 0:8], ctx.csb["ones_row128"][:, :],
                                 ctx.csb["iota8"][0:1, :], start=True, stop=True), "pe")
            @blk.scalar
            def _(a):
                ctx.wait(a, "pe")
                ctx.inc(a.copy(b8[:, 0:8], ctx.PRA[:, 0:8]), "act")
            ctx.bank_release(ctx.PRA, "act", ctx.cnt["act"])
            @blk.vector
            def _(v):
                ctx.wait(v, "act")
                m = ctx.inc(v.tensor_scalar_add(b8[:, 8:16], b8[:, 0:8], -8.0), "dve")
                ctx.wait(v, "dve", m)
                for tt in range(NT):
                    sc = gl[:, tt * 8:(tt + 1) * 8]
                    k1s = k1[:, tt * 8:(tt + 1) * 8]
                    k2s = k2[:, tt * 8:(tt + 1) * 8]
                    row_chain(ctx, v, [
                        lambda e, sc=sc: e.reduce_max(scw[:, 16:17], sc, axis=AX.X),
                        lambda e, sc=sc, k1s=k1s: e.tensor_scalar(k1s, sc, scw[:, 16:17],
                                                                  None, ALU.is_ge),
                        lambda e, k1s=k1s: e.tensor_mul(k1s, k1s, b8[:, 8:16]),
                        lambda e, k1s=k1s: e.tensor_scalar_add(k1s, k1s, 8.0),
                        lambda e, k1s=k1s: e.tensor_reduce(scw[:, 17:18], k1s, axis=AX.X, op=ALU.min),
                        lambda e, k1s=k1s: e.tensor_scalar(k1s, b8[:, 0:8], scw[:, 17:18], None,
                                                           ALU.is_equal),
                        lambda e, k1s=k1s, k2s=k2s: e.tensor_scalar(k2s, k1s, -1e9, None, ALU.mult),
                        lambda e, sc=sc, k2s=k2s: e.tensor_add(k2s, k2s, sc),
                        lambda e, k2s=k2s: e.reduce_max(scw[:, 18:19], k2s, axis=AX.X),
                        lambda e, k2s=k2s: e.tensor_scalar(k2s, k2s, scw[:, 18:19], None, ALU.is_ge),
                        lambda e, k2s=k2s: e.tensor_mul(k2s, k2s, b8[:, 8:16]),
                        lambda e, k2s=k2s: e.tensor_scalar_add(k2s, k2s, 8.0),
                        lambda e, k2s=k2s: e.tensor_reduce(scw[:, 19:20], k2s, axis=AX.X, op=ALU.min),
                        lambda e, k2s=k2s: e.tensor_scalar(k2s, b8[:, 0:8], scw[:, 19:20], None,
                                                           ALU.is_equal),
                        lambda e, tt=tt: e.tensor_sub(dif[:, tt:tt + 1], scw[:, 18:19], scw[:, 16:17]),
                    ])
            sig_pre = ctx.cnt["dve"]
            @blk.scalar
            def _(a):
                ctx.wait_future(a, "dve", sig_pre)
                ctx.inc(a.activation(dif[:, :], dif[:, :], AF.Sigmoid), "act")
            @blk.vector
            def _(v):
                ctx.wait(v, "act")
                for tt in range(NT):
                    k1s = k1[:, tt * 8:(tt + 1) * 8]
                    k2s = k2[:, tt * 8:(tt + 1) * 8]
                    row_chain(ctx, v, [
                        lambda e, tt=tt, k2s=k2s: e.tensor_scalar(k2s, k2s, dif[:, tt:tt + 1],
                                                                  None, ALU.mult),
                        lambda e, tt=tt: e.tensor_scalar(scw[:, 20:21], dif[:, tt:tt + 1], -1.0, 1.0,
                                                         ALU.mult, ALU.add),
                        lambda e, k1s=k1s: e.tensor_scalar(k1s, k1s, scw[:, 20:21], None, ALU.mult),
                        lambda e, k1s=k1s, k2s=k2s: e.tensor_add(k1s, k1s, k2s),
                    ])
            wE_m = ctx.cnt["dve"]
            ctx.sem("act")
            act_b2 = ctx.cnt["act"]
            @blk.tensor
            def _(t):
                ctx.wait_future(t, "dve", wE_m)
                ctx.bank_acquire(t, ctx.PTR)
                for tt in range(NT):
                    if tt >= 1:
                        ctx.wait_future(t, "act", act_b2 + tt)
                    ctx.inc(t.transpose(ctx.PTR[0:8, 0:128], k1[:, tt * 8:(tt + 1) * 8],
                                        ctx.csb["ident_f32"][:, :]), "pe")
            pe_b2 = ctx.cnt["pe"]
            @blk.scalar
            def _(a):
                for tt in range(NT):
                    ctx.wait_future(a, "pe", pe_b2 - NT + tt + 1)
                    ctx.inc(a.copy(wEr[0:8, tt * 128:(tt + 1) * 128], ctx.PTR[0:8, 0:128]), "act")
            ctx.bank_release(ctx.PTR, "act", ctx.cnt["act"])
        ctx.bank_clear()
        with nc.sbuf_tensor(ctx.uname("f2_b3k"), [1, D], F32) as b3k, \
             nc.sbuf_tensor(ctx.uname("f2_b1c"), [128, 16], F32) as b1c, \
             nc.sbuf_tensor(ctx.uname("f2_wb"), [128, NCH * 1024], BF16) as wbuf, \
             nc.sbuf_tensor(ctx.uname("f2_h"), [128, 16 * 512], BF16) as hh, \
             nc.sbuf_tensor(ctx.uname("f2_y"), [128, NCH * 512], F32) as yf, \
             nc.sbuf_tensor(ctx.uname("f2_sq"), [128, 512], F32) as sqb, \
             nc.sbuf_tensor(ctx.uname("f2_bA"), [128, 512], F32) as bA, \
             nc.sbuf_tensor(ctx.uname("f2_bB"), [128, 512], F32) as bB, \
             nc.sbuf_tensor(ctx.uname("f2_bw"), [128, 512], F32) as bwe, \
             nc.sbuf_tensor(ctx.uname("f2_wc"), [128, 16], F32) as wbcols, \
             nc.sbuf_tensor(ctx.uname("f2_rw"), [1, 3 * 512], F32) as rwt:
            ctx.rw = rwt
            for e in range(8):
                with nc.Block() as blk:
                    @blk.sync
                    def _(s):
                        ctx.dma(s, wbcols[:, 0:8], ctx.DI["l2_lng_c"].ap()[:, e * 8:(e + 1) * 8], sem="dio")
                        ctx.dma(s, wbcols[:, 8:16], ctx.DI["l2_lnb_c"].ap()[:, e * 8:(e + 1) * 8], sem="dio")
                        ctx.dma(s, b1c[:, :], ctx.DI["l2_fb1_c"].ap()[:, e * 16:(e + 1) * 16], sem="dio")
                        ctx.dma(s, b3k[0:1, 0:D], ctx.DI["l2_fb2_r"].ap()[e:e + 1, :], sem="dio")
                        s.wait_ge(ctx.sems["dio"], ctx.cnt["dio"])
                ctx.bank_clear()
                for q4 in range(NB):
                    _l2_expert_quarter(ctx, e, q4, b3k, b1c, wbuf, hh, yf, sqb, bA, bB, bwe,
                                       wbcols, rwt, wEr, sel8sb)


def _l2_expert_quarter(ctx, e, q4, b3k, b1c, wbuf, hh, yf, sqb, bA, bB, bwe, wbcols, rwt,
                       wEr, sel8sb):
    nc = ctx.nc
    t0 = q4 * 512
    with nc.Block() as blk:
        @blk.tensor
        def _(t):
            ctx.bank_acquire(t, ctx.PRB)
            ctx.inc(t.matmul(ctx.PRB[:, :], sel8sb[:, e * 128:(e + 1) * 128],
                             wEr[0:8, t0:t0 + 512], start=True, stop=True), "pe")
        @blk.scalar
        def _(a):
            ctx.wait(a, "pe")
            ctx.inc(a.copy(bwe[:, :], ctx.PRB[:, :]), "act")
        ctx.bank_release(ctx.PRB, "act", ctx.cnt["act"])
    ctx.bank_clear()
    for wg2 in range(2):
        with nc.Block() as blk:
            @blk.gpsimd
            def _(g):
                load_w_tiles(ctx, g, wbuf, ctx.DI["l2_w1"].ap(), e * D, wg2 * 1024, NCH, 1024)
            wm1 = ctx.cnt["dwt"]
            def lhsH(k, mi):
                return wbuf[:, k * 1024 + mi * 128:k * 1024 + (mi + 1) * 128]
            def rhsH(k, b):
                return ctx.xnc(k)[:, t0:t0 + 512]
            def evacH(mi, b, a, ps):
                mch = wg2 * 8 + mi
                return a.activation(hh[:, mch * 512:(mch + 1) * 512], ps[:, :], AF.Gelu,
                                    bias=b1c[:, mch:mch + 1])
            jobs = fm_jobs(ctx, lhsH, rhsH, NCH, 8, 1, None, evacH, wm1, evac_eng="act")
            run_pipeline(ctx, blk, jobs, banks=[ctx.PS[0], ctx.PS[1], ctx.PS[2], ctx.PS[3]])
        ctx.bank_clear()
    for wg2 in range(2):
        with nc.Block() as blk:
            @blk.gpsimd
            def _(g):
                load_w_tiles(ctx, g, wbuf, ctx.DI["l2_w2"].ap(), e * 2048 + wg2 * 1024, 0, NCH, 1024)
            wm2 = ctx.cnt["dwt"]
            def lhsF(k, m):
                return wbuf[:, k * 1024 + m * 128:k * 1024 + (m + 1) * 128]
            def rhsF(k, b):
                kch = wg2 * 8 + k
                return hh[:, kch * 512:(kch + 1) * 512]
            def biasF(m, b):
                return (b3k[0:1, m * 128:(m + 1) * 128], ctx.csb["ones_row512"][:, :])
            if wg2 == 0:
                def evacF(m, b, e_, ps):
                    return e_.tensor_add(yf[:, m * 512:(m + 1) * 512], ps[:, :],
                                         ctx.xnc(m)[:, t0:t0 + 512])
                jobs = fm_jobs(ctx, lhsF, rhsF, NCH, NCH, 1, None, evacF, wm2)
            else:
                def evacF(m, b, e_, ps):
                    dst = yf[:, m * 512:(m + 1) * 512]
                    return e_.tensor_add(dst, ps[:, :], dst)
                jobs = fm_jobs(ctx, lhsF, rhsF, NCH, NCH, 1, biasF, evacF, wm2)
            run_pipeline(ctx, blk, jobs, banks=[ctx.PS[0], ctx.PS[1], ctx.PS[2], ctx.PS[3]])
        ctx.bank_clear()
    with nc.Block() as blk:
        def ysrc(c):
            return yf[:, c * 512:(c + 1) * 512]
        emit_ln_rows(ctx, blk, ysrc, NCH, 512, 1e-6, 1.0 / D, [sqb, bA], 0)
    ctx.bank_clear()
    emit_rows_bcast_pair_n(ctx, rwt, [512, 1024], [bA, bB], 512)
    with nc.Block() as blk:
        @blk.vector
        def _(v):
            for c in range(NCH):
                row_chain(ctx, v, [
                    lambda e_, c=c: e_.tensor_mul(sqb[:, :], yf[:, c * 512:(c + 1) * 512], bA[:, :]),
                    lambda e_, c=c: e_.tensor_add(sqb[:, :], sqb[:, :], bB[:, :]),
                    lambda e_, c=c: e_.tensor_scalar(sqb[:, :], sqb[:, :], wbcols[:, c:c + 1],
                                                     wbcols[:, 8 + c:9 + c], ALU.mult, ALU.add),
                    lambda e_, c=c: e_.tensor_mul(sqb[:, :], sqb[:, :], bwe[:, :]),
                    lambda e_, c=c: e_.tensor_add(ctx.xc(c)[:, t0:t0 + 512],
                                                  ctx.xc(c)[:, t0:t0 + 512], sqb[:, :]),
                ])
    ctx.bank_clear()


L3_CAP = 1024


def emit_l3_route(ctx, myrt):
    ctx.phase = "emit_l3_route"
    nc = ctx.nc
    with nc.sbuf_tensor(ctx.uname("r_gl"), [128, 2048], F32) as glog, \
         nc.sbuf_tensor(ctx.uname("r_oh"), [128, 1024], F32) as oh, \
         nc.sbuf_tensor(ctx.uname("r_ob"), [128, 1024], BF16) as ohbf, \
         nc.sbuf_tensor(ctx.uname("r_ic"), [128, 1024], F32) as incl, \
         nc.sbuf_tensor(ctx.uname("r_mx"), [128, 128], F32) as mx, \
         nc.sbuf_tensor(ctx.uname("r_rt"), [128, 128], F32) as rt, \
         nc.sbuf_tensor(ctx.uname("r_kp"), [128, 128], F32) as keep, \
         nc.sbuf_tensor(ctx.uname("r_sl"), [128, 128], F32) as slotc, \
         nc.sbuf_tensor(ctx.uname("r_b8"), [128, 16], F32) as b8, \
         nc.sbuf_tensor(ctx.uname("r_tc"), [128, 8], F32) as tcb, \
         nc.sbuf_tensor(ctx.uname("r_tb"), [128, 8], BF16) as tcbf, \
         nc.sbuf_tensor(ctx.uname("r_ex"), [128, 8], F32) as exf, \
         nc.sbuf_tensor(ctx.uname("r_ib"), [128, 256], F32) as rtb, \
         nc.sbuf_tensor(ctx.uname("r_iu"), [128, NT], U32) as idxu, \
         nc.sbuf_tensor(ctx.uname("r_rw"), [1, 2048], F32) as rw2:
        g33 = glog[:, :].rearrange("p (b e) -> p b e", e=16)[:, :, 0:8]
        oh3 = oh[:, :].rearrange("p (b e) -> p b e", e=8)
        ic3 = incl[:, :].rearrange("p (b e) -> p b e", e=8)
        mxb = mx[:, :].rearrange("p (b e) -> p b e", e=1).broadcast_to((128, 128, 8))
        rtbc = rt[:, :].rearrange("p (b e) -> p b e", e=1).broadcast_to((128, 128, 8))
        b8b = b8[:, 0:8].rearrange("p (b e) -> p b e", b=1).broadcast_to((128, 128, 8))
        b8mb = b8[:, 8:16].rearrange("p (b e) -> p b e", b=1).broadcast_to((128, 128, 8))
        with nc.Block() as blk:
            @blk.gpsimd
            def _(g):
                for h in range(2):
                    ctx.dma(g, glog[:, h * 1024:(h + 1) * 1024].rearrange(
                                "p (b e) -> p b e", e=16),
                            ctx.l3_logit_all.ap()[h * 8192:(h + 1) * 8192, :].rearrange(
                                "(b p) g -> p b g", p=128), sem="dio")
                ctx.dma(g, b8[:, 0:8],
                        ctx.consts["c_iota8"].ap()[0:1, :].broadcast_to((128, 8)), sem="dio")
                g.wait_ge(ctx.sems["dio"], ctx.cnt["dio"])

            @blk.vector
            def _(v):
                ctx.wait(v, "dio")
                row_chain(ctx, v, [
                    lambda e: e.tensor_scalar_add(b8[:, 8:16], b8[:, 0:8], -8.0),
                    lambda e: e.tensor_reduce(mx[:, :], g33, axis=AX.X, op=ALU.max),
                    lambda e: e.tensor_tensor(oh3, g33, mxb, ALU.is_ge),
                    lambda e: e.tensor_tensor(oh3, oh3, b8mb, ALU.mult),
                    lambda e: e.tensor_scalar_add(oh[:, :], oh[:, :], 8.0),
                    lambda e: e.tensor_reduce(rt[:, :], oh3, axis=AX.X, op=ALU.min),
                    lambda e: e.tensor_tensor(oh3, b8b, rtbc, ALU.is_equal),
                    lambda e: e.tensor_copy(ohbf[:, :], oh[:, :]),
                ])
        ctx.bank_clear()
        with nc.Block() as blk:
            @blk.tensor
            def _(t):
                ctx.wait(t, "dve")
                for b in range(2):
                    ctx.bank_acquire(t, ctx.PS[b])
                    ctx.inc(t.matmul(ctx.PS[b][:, :], ctx.csb["tri_incl"][:, :],
                                     ohbf[:, b * 512:(b + 1) * 512], start=True, stop=True), "pe")
                for b in range(2):
                    ctx.bank_acquire(t, ctx.PS[2 + b])
                    ctx.inc(t.matmul(ctx.PS[2 + b][0:1, :], ctx.csb["ones_col_bf"][:, :],
                                     ohbf[:, b * 512:(b + 1) * 512], start=True, stop=True), "pe")
            @blk.scalar
            def _(a):
                ctx.wait(a, "pe")
                for b in range(2):
                    ctx.inc(a.copy(incl[:, b * 512:(b + 1) * 512], ctx.PS[b][:, :]), "act")
                for b in range(2):
                    ctx.inc(a.copy(rw2[0:1, b * 512:(b + 1) * 512], ctx.PS[2 + b][0:1, :]), "act")
            for b in range(4):
                ctx.bank_release(ctx.PS[b], "act", ctx.cnt["act"])
            @blk.sync
            def _(s):
                ctx.wait(s, "act")
                ctx.dma(s, tcb[:, :], rw2[0:1, 0:1024].rearrange("o (b e) -> o b e", e=8),
                        sem="dio")
                s.wait_ge(ctx.sems["dio"], ctx.cnt["dio"])
            @blk.vector
            def _(v):
                ctx.wait(v, "dio")
                m = ctx.inc(v.tensor_copy(tcbf[:, :], tcb[:, :]), "dve")
                ctx.wait(v, "dve", m)
            @blk.tensor
            def _(t):
                ctx.wait(t, "dve")
                ctx.bank_acquire(t, ctx.PS[4])
                ctx.inc(t.matmul(ctx.PS[4][:, 0:8], ctx.csb["tri_excl"][:, :],
                                 tcbf[:, :], start=True, stop=True), "pe")
            @blk.scalar
            def _(a):
                ctx.wait(a, "pe")
                ctx.inc(a.copy(exf[:, :], ctx.PS[4][:, 0:8]), "act")
            ctx.bank_release(ctx.PS[4], "act", ctx.cnt["act"])
            @blk.sync
            def _(s):
                ctx.wait(s, "act")
                ctx.dma(s, rw2[0:1, 1024:2048].rearrange("o (b e) -> o b e", e=8),
                        exf[:, :], sem="dio")
                s.wait_ge(ctx.sems["dio"], ctx.cnt["dio"])
        ctx.bank_clear()
        with nc.Block() as blk:
            @blk.tensor
            def _(t):
                ctx.wait(t, "dio")
                for b in range(2):
                    ctx.bank_acquire(t, ctx.PS[b])
                    ctx.inc(t.matmul(ctx.PS[b][:, :], ctx.csb["ones_row128"][:, :],
                                     rw2[0:1, 1024 + b * 512:1024 + (b + 1) * 512],
                                     start=True, stop=True), "pe")
            @blk.scalar
            def _(a):
                ctx.wait(a, "pe")
                for b in range(2):
                    ctx.inc(a.copy(glog[:, b * 512:(b + 1) * 512], ctx.PS[b][:, :]), "act")
            for b in range(2):
                ctx.bank_release(ctx.PS[b], "act", ctx.cnt["act"])
            @blk.vector
            def _(v):
                ctx.wait(v, "act")
                rtb3 = rtb[:, :].rearrange("p (b g) -> p b g", g=2)
                row_chain(ctx, v, [
                    lambda e: e.tensor_add(incl[:, :], incl[:, :], glog[:, 0:1024]),
                    lambda e: e.tensor_tensor(ic3, ic3, oh3, ALU.mult),
                    lambda e: e.tensor_reduce(mx[:, :], ic3, axis=AX.X, op=ALU.add),
                    lambda e: e.tensor_scalar(keep[:, :], mx[:, :], float(L3_CAP), None, ALU.is_le),
                    lambda e: e.tensor_scalar(slotc[:, :], rt[:, :], float(L3_CAP), None, ALU.mult),
                    lambda e: e.tensor_add(slotc[:, :], slotc[:, :], mx[:, :]),
                    lambda e: e.tensor_scalar_add(slotc[:, :], slotc[:, :],
                                                  -(1.0 + float(NCORES * L3_CAP))),
                    lambda e: e.tensor_mul(slotc[:, :], slotc[:, :], keep[:, :]),
                    lambda e: e.tensor_scalar_add(slotc[:, :], slotc[:, :],
                                                  float(NCORES * L3_CAP)),
                    lambda e: e.tensor_copy(rtb3[:, :, 0:1],
                                            slotc[:, :].rearrange("p (b g) -> p b g", g=1)),
                    lambda e: e.tensor_copy(rtb3[:, :, 1:2],
                                            keep[:, :].rearrange("p (b g) -> p b g", g=1)),
                ])
            @blk.sync
            def _(s):
                ctx.wait(s, "dve")
                ctx.dma(s, ctx.l3_rt.ap().rearrange("(b p) g -> p b g", p=128),
                        rtb[:, :].rearrange("p (b g) -> p b g", g=2), sem="dst2")
                s.wait_ge(ctx.sems["dst2"], ctx.cnt["dst2"])
        ctx.bank_clear()
        with nc.Block() as blk:
            @blk.vector
            def _(v):
                for tt in range(NT):
                    ctx.inc(v.tensor_scalar(oh[:, tt:tt + 1], ctx.csb["iota128"][:, :],
                                            ctx.cinfo[:, 0:1], float(tt * 128),
                                            ALU.add, ALU.add), "dve")
                ctx.wait(v, "dve")
            @blk.gpsimd
            def _(g):
                ctx.wait(g, "dve")
                u32_index(ctx, g, idxu[:, :], oh[:, 0:NT])
                for tt in range(NT):
                    ctx.inc(g.indirect_dma_start(
                        myrt[:, 2 * tt:2 * tt + 2], None, ctx.l3_rt.ap(),
                        IndirectOffsetOnAxis(idxu[:, tt:tt + 1], axis=0)), "dsh", 16)
                g.wait_ge(ctx.sems["dsh"], ctx.cnt["dsh"])
        ctx.bank_clear()


def emit_l3_scatter_xn(ctx, myrt, slotu):
    ctx.phase = "emit_l3_scatter_xn"
    nc = ctx.nc
    with nc.sbuf_tensor(ctx.uname("sx_st"), [128, D], BF16) as stage, \
         nc.sbuf_tensor(ctx.uname("sx_sf"), [128, NT], F32) as sf, nc.Block() as blk:
        @blk.vector
        def _(v):
            m = ctx.inc(v.tensor_copy(
                sf[:, :], myrt[:, :].rearrange("p (tt g) -> p tt g", g=2)[:, :, 0:1]
                .rearrange("p tt g -> p (tt g)")), "dve")
            ctx.wait(v, "dve", m)
        @blk.gpsimd
        def _(g):
            ctx.inc(g.memset(stage[:, :], 0.0), "pool")
            g.wait_ge(ctx.sems["pool"], ctx.cnt["pool"])
            for t in range(NCORES * L3_CAP // 128):
                ctx.inc(g.dma_start(out=ctx.l3_xb_my.ap()[t * 128:(t + 1) * 128, :],
                                    in_=stage[:, :]), "dst2", 16)
            ctx.inc(g.dma_start(out=ctx.l3_xb_my.ap()[NCORES * L3_CAP:NCORES * L3_CAP + 1, :],
                                in_=stage[0:1, :]), "dst2", 16)
            ctx.wait(g, "dve")
            u32_index(ctx, g, slotu[:, :], sf[:, :])
            g.wait_ge(ctx.sems["dst2"], ctx.cnt["dst2"])
            for tt in range(NT):
                ctx.dma(g, stage[:, :], ctx.l3_xn_my.ap()[tt * 128:(tt + 1) * 128, :], sem="dio")
                g.wait_ge(ctx.sems["dio"], ctx.cnt["dio"])
                ctx.inc(g.indirect_dma_start(
                    ctx.l3_xb_my.ap(), IndirectOffsetOnAxis(slotu[:, tt:tt + 1], axis=0),
                    stage[:, :], None), "dsh", 16)
                g.wait_ge(ctx.sems["dsh"], ctx.cnt["dsh"])
            ctx.inc(g.collective_compute(
                "AllReduce", ALU.add, replica_groups=[[0, 1, 2, 3, 4, 5, 6, 7]],
                ins=[ctx.l3_xb_my.ap()], outs=[ctx.l3_xb_all.ap()]), "cc", 1)
    ctx.bank_clear()


def emit_l3_expert(ctx):
    ctx.phase = "emit_l3_expert"
    nc = ctx.nc
    hT = ctx.x_sb[:, :].bitcast(BF16)
    xbT = ctx.xn_bf
    yf = ctx.xn_bf[:, :].bitcast(F32)
    with nc.sbuf_tensor(ctx.uname("e_br"), [1, D], F32) as brow, \
         nc.sbuf_tensor(ctx.uname("e_id"), [128, 8], F32) as idc, \
         nc.sbuf_tensor(ctx.uname("e_iu"), [128, 8], U32) as idu, \
         nc.sbuf_tensor(ctx.uname("e_st"), [128, 2 * D], BF16) as stage, \
         nc.sbuf_tensor(ctx.uname("e_b1"), [128, 32], F32) as eb1c:
        with nc.Block() as blk:
            @blk.vector
            def _(v):
                for j in range(NCORES):
                    ctx.inc(v.tensor_scalar(idc[:, j:j + 1], ctx.csb["iota128"][:, :],
                                            ctx.cinfo[:, 4:5], float(j * 128),
                                            ALU.add, ALU.add), "dve")
                ctx.wait(v, "dve")
            @blk.sync
            def _(s):
                ctx.dma(s, eb1c[:, :], ctx.DI["l3_eb1_c"].ap(), sem="dio")
                ctx.dma(s, brow[0:1, 0:D], ctx.DI["l3_eb2_r"].ap(), sem="dio")
                s.wait_ge(ctx.sems["dio"], ctx.cnt["dio"])
        ctx.bank_clear()
        with nc.Block() as blk:
            ctx.sem("dsh"); ctx.sem("dtp")
            dsh_b = ctx.cnt["dsh"]
            dtp_b = ctx.cnt["dtp"]
            @blk.gpsimd
            def _(g):
                ctx.wait(g, "dve")
                g.wait_ge(ctx.sems["cc"], ctx.l3_cc_xb)
                u32_index(ctx, g, idu[:, :], idc[:, :])
                for j in range(NCORES):
                    if j >= 2:
                        g.wait_ge(ctx.sems["dtp"], dtp_b + 16 * NCH * (j - 1))
                    ctx.inc(g.indirect_dma_start(
                        stage[:, (j % 2) * D:(j % 2) * D + D], None, ctx.l3_xb_all.ap(),
                        IndirectOffsetOnAxis(idu[:, j:j + 1], axis=0)), "dsh", 16)
            @blk.sync
            def _(s):
                for j in range(NCORES):
                    s.wait_ge(ctx.sems["dsh"], dsh_b + 16 * (j + 1))
                    for kc in range(NCH):
                        s.dma_start(out=xbT[:, kc * 1024 + j * 128:kc * 1024 + (j + 1) * 128],
                                    in_=stage[:, (j % 2) * D + kc * 128:(j % 2) * D + (kc + 1) * 128],
                                    transpose=True).then_inc(ctx.sems["dtp"], 16)
                    ctx.cnt["dtp"] += 16 * NCH
                s.wait_ge(ctx.sems["dtp"], ctx.cnt["dtp"])
        ctx.bank_clear()
        emit_l3_expert_mm(ctx, brow, eb1c, hT, yf)
    ctx.bank_clear()


def emit_l3_expert_mm(ctx, brow, eb1c, hT, yf):
    nc = ctx.nc
    nonlocal_box = [None, None]
    with nc.sbuf_tensor(ctx.uname("e_wb0"), [128, NCH * 1024], BF16) as wbuf0, \
         nc.sbuf_tensor(ctx.uname("e_wb1"), [128, NCH * 1024], BF16) as wbuf1:
        wbs = [wbuf0, wbuf1]
        def load_phase(g, i):
            if i < 4:
                load_w_tiles(ctx, g, wbs[i % 2], ctx.DI["l3_ew1"].ap(), 0, i * 1024, NCH, 1024)
            else:
                load_w_tiles(ctx, g, wbs[i % 2], ctx.DI["l3_ew2"].ap(), (i - 4) * 1024, 0,
                             NCH, 1024)
            return ctx.cnt["dwt"]
        wm_next = None
        for q in range(4):
            with nc.Block() as blk:
                wbuf = wbs[q % 2]
                wm_here = wm_next
                @blk.gpsimd
                def _(g, q=q):
                    nonlocal_box[0] = load_phase(g, q) if q == 0 else nonlocal_box[0]
                    if q == 0:
                        nonlocal_box[1] = load_phase(g, 1)
                    elif q < 7:
                        nonlocal_box[1] = load_phase(g, q + 1)
                if q == 0:
                    wm_here = nonlocal_box[0]
                wm_next = nonlocal_box[1]
                def lhs(k, m):
                    return wbuf[:, k * 1024 + m * 128:k * 1024 + (m + 1) * 128]
                def rhs(k, b):
                    return ctx.xn_bf[:, k * 1024 + b * 512:k * 1024 + (b + 1) * 512]
                def ev(m, b, e, ps):
                    mg = q * 8 + m
                    return e.activation(hT[:, mg * 1024 + b * 512:mg * 1024 + (b + 1) * 512],
                                        ps[:, :], AF.Relu, bias=eb1c[:, mg:mg + 1])
                jobs = fm_jobs(ctx, lhs, rhs, NCH, NCH, 2, None, ev, wm_here, evac_eng="act")
                run_pipeline(ctx, blk, jobs, banks=[ctx.PS[0], ctx.PS[1], ctx.PS[2], ctx.PS[3]])
            ctx.bank_clear()
        for gr in range(4):
            with nc.Block() as blk:
                i = 4 + gr
                wbuf = wbs[i % 2]
                wm_here = wm_next
                @blk.gpsimd
                def _(g, i=i):
                    if i < 7:
                        nonlocal_box[1] = load_phase(g, i + 1)
                wm_next = nonlocal_box[1]
                def lhs(k, m):
                    return wbuf[:, k * 1024 + m * 128:k * 1024 + (m + 1) * 128]
                def rhs(k, b):
                    kg = gr * 8 + k
                    return hT[:, kg * 1024 + b * 512:kg * 1024 + (b + 1) * 512]
                def bias(m, b):
                    return (brow[0:1, m * 128:(m + 1) * 128], ctx.csb["ones_row512"][:, :])
                def ev0(m, b, e, ps):
                    return e.copy(yf[:, m * 1024 + b * 512:m * 1024 + (b + 1) * 512], ps[:, :])
                def evn(m, b, e, ps):
                    dst = yf[:, m * 1024 + b * 512:m * 1024 + (b + 1) * 512]
                    return e.tensor_add(dst, dst, ps[:, :])
                jobs = fm_jobs(ctx, lhs, rhs, NCH, NCH, 2, bias if gr == 3 else None,
                               ev0 if gr == 0 else evn, wm_here,
                               evac_eng="act" if gr == 0 else "dve")
                run_pipeline(ctx, blk, jobs, banks=[ctx.PS[0], ctx.PS[1], ctx.PS[2], ctx.PS[3]])
            ctx.bank_clear()
    with nc.sbuf_tensor(ctx.uname("e_ys"), [128, 2 * D], BF16) as ystage, nc.Block() as blk:
        ctx.sem("dout")
        dout_base = ctx.cnt["dout"]
        jobs = []
        for tt in range(8):
            for c in range(NCH):
                def mk(tt, c):
                    def mm(t, ps):
                        return t.transpose(ps[:, 0:128],
                                           yf[:, c * 1024 + tt * 128:c * 1024 + (tt + 1) * 128],
                                           ctx.csb["ident_f32"][:, :])
                    def ev(a, ps):
                        if c == 0 and tt >= 2:
                            ctx.wait_future(a, "dout", dout_base + 16 * (tt - 1))
                        return a.copy(ystage[:, (tt % 2) * D + c * 128:(tt % 2) * D + (c + 1) * 128],
                                      ps[:, 0:128])
                    return {"mm": mm, "evac": ev, "evac_eng": "act"}
                jobs.append(mk(tt, c))
        ev_base = ctx.cnt["evac"]
        run_pipeline(ctx, blk, jobs, banks=[ctx.PS[0], ctx.PS[1], ctx.PS[2], ctx.PS[3]])
        @blk.sync
        def _(s):
            for tt in range(8):
                ctx.wait_future(s, "evac", ev_base + (tt + 1) * NCH)
                ctx.dma(s, ctx.l3_y.ap()[tt * 128:(tt + 1) * 128, :],
                        ystage[:, (tt % 2) * D:(tt % 2) * D + D], sem="dout")
            s.wait_ge(ctx.sems["dout"], ctx.cnt["dout"])
        @blk.gpsimd
        def _(g):
            g.wait_ge(ctx.sems["dout"], ctx.cnt["dout"])
            ctx.inc(g.collective_compute(
                "AllGather", ALU.bypass, replica_groups=[[0, 1, 2, 3, 4, 5, 6, 7]],
                ins=[ctx.l3_y.ap()], outs=[ctx.l3_y_all.ap()[0:NCORES * L3_CAP, :]]), "cc", 1)
    ctx.bank_clear()


def emit_l3_combine(ctx, myrt, slotu):
    ctx.phase = "emit_l3_combine"
    nc = ctx.nc
    with nc.sbuf_tensor(ctx.uname("c_yb"), [128, D], BF16) as ycbf, \
         nc.sbuf_tensor(ctx.uname("c_yc"), [128, D], F32) as ycomb, \
         nc.sbuf_tensor(ctx.uname("c_xn"), [128, D], BF16) as xnr, \
         nc.sbuf_tensor(ctx.uname("c_xf"), [128, D], F32) as xnf, \
         nc.sbuf_tensor(ctx.uname("c_kc"), [128, 2 * NT], F32) as kc2:
        with nc.Block() as blk:
            @blk.vector
            def _(v):
                row_chain(ctx, v, [
                    lambda e: e.tensor_copy(
                        kc2[:, 0:NT], myrt[:, :].rearrange("p (tt g) -> p tt g", g=2)[:, :, 1:2]
                        .rearrange("p tt g -> p (tt g)")),
                    lambda e: e.tensor_scalar(kc2[:, NT:2 * NT], kc2[:, 0:NT], -1.0, 1.0,
                                              ALU.mult, ALU.add),
                ])
        ctx.bank_clear()
        with nc.Block() as blk:
            ctx.sem("dve"); ctx.sem("dsh"); ctx.sem("dio")
            for tt in range(NT):
                @blk.gpsimd
                def _(g, tt=tt):
                    if tt == 0:
                        g.wait_ge(ctx.sems["cc"], ctx.l3_cc_y)
                    else:
                        ctx.wait(g, "dve")
                    ctx.inc(g.indirect_dma_start(
                        ycbf[:, :], None, ctx.l3_y_all.ap(),
                        IndirectOffsetOnAxis(slotu[:, tt:tt + 1], axis=0)), "dsh", 16)
                    ctx.dma(g, xnr[:, :], ctx.l3_xn_my.ap()[tt * 128:(tt + 1) * 128, :], sem="dio")
                    g.wait_ge(ctx.sems["dsh"], ctx.cnt["dsh"])
                    g.wait_ge(ctx.sems["dio"], ctx.cnt["dio"])
                @blk.vector
                def _(v, tt=tt):
                    ctx.wait(v, "dsh")
                    ctx.wait(v, "dio")
                    row_chain(ctx, v, [
                        lambda e: e.tensor_scalar(ycomb[:, :], ycbf[:, :],
                                                  kc2[:, tt:tt + 1], None, ALU.mult),
                        lambda e: e.tensor_copy(xnf[:, :], xnr[:, :]),
                        lambda e: e.tensor_scalar(xnf[:, :], xnf[:, :],
                                                  kc2[:, NT + tt:NT + tt + 1], None, ALU.mult),
                        lambda e: e.tensor_add(ycomb[:, :], ycomb[:, :], xnf[:, :]),
                    ])
                dve_done = ctx.cnt["dve"]
                for c in range(NCH):
                    @blk.tensor
                    def _(t, tt=tt, c=c, dve_done=dve_done):
                        ctx.wait_future(t, "dve", dve_done)
                        ctx.bank_acquire(t, ctx.PTR)
                        ctx.inc(t.transpose(ctx.PTR[:, 0:128], ycomb[:, c * 128:(c + 1) * 128],
                                            ctx.csb["ident_f32"][:, :]), "pe")
                    @blk.vector
                    def _(v, tt=tt, c=c):
                        ctx.wait(v, "pe")
                        dst = ctx.xc(c)[:, tt * 128:(tt + 1) * 128]
                        ctx.inc(v.tensor_add(dst, dst, ctx.PTR[:, 0:128]), "dve")
                    ctx.bank_release(ctx.PTR, "dve", ctx.cnt["dve"])
        ctx.bank_clear()


def emit_layer3(ctx):
    ctx.phase = "emit_layer3"
    nc = ctx.nc
    with nc.sbuf_tensor(ctx.uname("l3_myrt"), [128, 2 * NT], F32) as myrt, \
         nc.sbuf_tensor(ctx.uname("l3_slu"), [128, NT], U32) as slotu:
        with nc.sbuf_tensor(ctx.uname("l3_rows"), [1, 2 * TOK], F32) as rows16:
            with nc.sbuf_tensor(ctx.uname("l3_logt"), [16, TOK], F32) as logt, \
                 nc.sbuf_tensor(ctx.uname("l3_gb"), [1, 16], F32) as brow:
                ctx.l_rows = rows16
                emit_outer_ln(ctx, rows_out=rows16)
                emit_xn_tm(ctx, ctx.l3_xn_my)
                with nc.Block() as blk:
                    @blk.sync
                    def _(s):
                        ctx.dma(s, brow[0:1, 0:16], ctx.DI["l3_bsw_r"].ap(), sem="dio")
                        s.wait_ge(ctx.sems["dio"], ctx.cnt["dio"])
                emit_gating_fm(ctx, ctx.DI["l3_wsw"], 8, brow, logt, dram_rows=ctx.l3_logit)
            with nc.Block() as blk:
                @blk.gpsimd
                def _(g):
                    m = ctx.inc(g.collective_compute(
                        "AllGather", ALU.bypass, replica_groups=[[0, 1, 2, 3, 4, 5, 6, 7]],
                        ins=[ctx.l3_logit.ap()], outs=[ctx.l3_logit_all.ap()]), "cc", 1)
                    g.wait_ge(ctx.sems["cc"], m)
            ctx.bank_clear()
            emit_l3_route(ctx, myrt)
            emit_l3_scatter_xn(ctx, myrt, slotu)
            ctx.l3_cc_xb = ctx.cnt["cc"]
            emit_spill_x(ctx)
            with nc.sbuf_tensor(ctx.uname("l3_wb"), [128, 16], F32) as wbcols:
                with nc.Block() as blk:
                    @blk.sync
                    def _(s):
                        ctx.dma(s, wbcols[:, 0:8], ctx.DI["l3_lng_c"].ap(), sem="dio")
                        ctx.dma(s, wbcols[:, 8:16], ctx.DI["l3_lnb_c"].ap(), sem="dio")
                        s.wait_ge(ctx.sems["dio"], ctx.cnt["dio"])
                    @blk.vector
                    def _(v):
                        for c in range(NCH):
                            m = ctx.inc(v.tensor_copy(ctx.xc(c), ctx.xnc(c)), "dve")
                            ctx.wait(v, "dve", m)
                ctx.bank_clear()
                emit_ln_inplace_alloc(ctx, ctx.xc, NCH, 1e-5, wbcols, 0, zbf_slice=ctx.xnc)
        emit_share_acts(ctx, ctx.xnc)
        emit_l0_attn(ctx, L3_ATTN_CFG)
        emit_spill_x(ctx)
        emit_l3_expert(ctx)
        ctx.l3_cc_y = ctx.cnt["cc"]
        emit_reload_x(ctx)
        emit_l3_combine(ctx, myrt, slotu)



def kernel(**inputs):
    in_maps = prepare_inputs(inputs)
    nc, ctx = build(n_layers=N_LAYERS)
    res = run_bass_kernel_spmd(nc, in_maps, core_ids=list(range(NCORES)))
    outs = [res.results[c]["x_out"] for c in range(NCORES)]
    return np.concatenate(outs, axis=0).reshape(4, SEQ, D).astype(np.float32)
